# version 18
# baseline (speedup 1.0000x reference)
import numpy as np
import re
from contextlib import ExitStack

import concourse.bass as bass
import concourse.mybir as mybir

F32 = mybir.dt.float32
BF16 = mybir.dt.bfloat16
FP16 = mybir.dt.float16
AF = mybir.ActivationFunctionType
ALU = mybir.AluOpType
AX = mybir.AxisListType

D = 1024
H = 4
DV = 256
NCHUNK = 8


def ts(i, n):
    return slice(i * n, (i + 1) * n)


class MK:
    def __init__(self, nseq=2, L=4096, dbg=False, dbg_out=(), dbg_in=()):
        self.dbg_out = set(dbg_out)
        self.dbg_in = set(dbg_in)
        self.nseq = nseq
        self.L = L
        self.T = nseq * L
        self.NT = self.T // 128
        self.NTS = self.L // 128
        self.dbg = dbg
        assert self.L % 512 == 0
        self.NSTR = self.L // 512

        nc = bass.Bass("TRN2", target_bir_lowering=False, detect_race_conditions=False)
        self.nc = nc
        dt = nc.dram_tensor
        self.x_in = dt("x", [self.T, D], F32, kind="ExternalInput").ap()
        for nm, shp in [("Wq", [D, D]), ("Wk", [D, D]), ("Wv", [D, D]), ("Wb", [D, H]),
                        ("wq_conv", [D, 4]), ("wk_conv", [D, 4]), ("wv_conv", [D, 4]),
                        ("fir_long", [D, 64]), ("fir_short", [D, 5]),
                        ("Wg1", [D + 16, D]), ("bg1", [D]), ("Wg2", [D, H]), ("bg2", [H]),
                        ("logit_temp", [1]), ("conv_res_logit", [H]),
                        ("Wres", [D, H]), ("bres", [H]),
                        ("Wog1", [D + 8, D // 2]), ("bog1", [D // 2]),
                        ("Wog2", [D // 2, 1]), ("bog2", [1]),
                        ("norm_w", [DV]), ("Wo", [D, D]),
                        ("w_gate", [4 * D, D]), ("w_up", [4 * D, D]), ("w_down", [D, 4 * D])]:
            setattr(self, nm, dt(nm, shp, F32, kind="ExternalInput").ap())
        self.y_out = dt("y", [self.T, D], F32, kind="ExternalOutput").ap()

        NT = self.NT
        self.XLNT = dt("XLNT", [NT // 4, 128, 4096], F32, kind=("ExternalInput" if "XLNT" in self.dbg_in else ("ExternalOutput" if "XLNT" in self.dbg_out else "Internal"))).ap()
        self.QTM = dt("QTM", [self.T, D], F32, kind=("ExternalInput" if "QTM" in self.dbg_in else ("ExternalOutput" if "QTM" in self.dbg_out else "Internal"))).ap()
        self.KTM = dt("KTM", [self.T, D], F32, kind=("ExternalInput" if "KTM" in self.dbg_in else ("ExternalOutput" if "KTM" in self.dbg_out else "Internal"))).ap()
        self.VTM = dt("VTM", [self.T, D], F32, kind=("ExternalInput" if "VTM" in self.dbg_in else ("ExternalOutput" if "VTM" in self.dbg_out else "Internal"))).ap()
        self.LSTM = dt("LSTM", [self.T, D], F32, kind=("ExternalInput" if "LSTM" in self.dbg_in else ("ExternalOutput" if "LSTM" in self.dbg_out else "Internal"))).ap()
        self.LLTM = dt("LLTM", [self.T, D], F32, kind=("ExternalInput" if "LLTM" in self.dbg_in else ("ExternalOutput" if "LLTM" in self.dbg_out else "Internal"))).ap()
        self.VCM = dt("VCM", [self.NT // 4, 128, 4608], F32, kind="Internal").ap()
        self.LLCM = dt("LLCM", [self.NT // 4, 128, 4096], F32, kind=("ExternalOutput" if "LLCM" in self.dbg_out else "Internal")).ap()
        self.DOTM = dt("DOTM", [self.T, D], F32, kind=("ExternalInput" if "DOTM" in self.dbg_in else ("ExternalOutput" if "DOTM" in self.dbg_out else "Internal"))).ap()
        self.Y1 = dt("Y1", [self.T, D], F32, kind=("ExternalInput" if "Y1" in self.dbg_in else ("ExternalOutput" if "Y1" in self.dbg_out else "Internal"))).ap()
        self.XQT = dt("XQT", [NT // 4, 128, 4096], BF16, kind=("ExternalInput" if "XQT" in self.dbg_in else ("ExternalOutput" if "XQT" in self.dbg_out else "Internal"))).ap()
        self.GQT = dt("GQT", [NT, 128, 4096], BF16, kind=("ExternalInput" if "GQT" in self.dbg_in else ("ExternalOutput" if "GQT" in self.dbg_out else "Internal"))).ap()
        self.WGT = dt("WGT", [D, 4 * D], BF16, kind=("ExternalInput" if "WGT" in self.dbg_in else ("ExternalOutput" if "WGT" in self.dbg_out else "Internal"))).ap()
        self.WUP = dt("WUP", [D, 4 * D], BF16, kind=("ExternalInput" if "WUP" in self.dbg_in else ("ExternalOutput" if "WUP" in self.dbg_out else "Internal"))).ap()
        self.WDN = dt("WDN", [4 * D, D], BF16, kind=("ExternalInput" if "WDN" in self.dbg_in else ("ExternalOutput" if "WDN" in self.dbg_out else "Internal"))).ap()

        self.stack = ExitStack()
        self._sems = {}
        self._qcounts = {}
        self._persist()

    def sb(self, name, shape, dtype=F32):
        return self.stack.enter_context(self.nc.sbuf_tensor(name, shape, dtype))

    def ps(self, name, shape, dtype=F32):
        return self.stack.enter_context(self.nc.psum_tensor(name, shape, dtype))

    def sem(self, name):
        return self.stack.enter_context(self.nc.semaphore(name))

    def _persist(self):
        s = self.sb
        self.ident = s("ident", [128, 128], F32)
        self.maskMn = s("maskMn", [128, 128], F32)
        self.maskUI = s("maskUI", [128, 128], F32)
        self.ones_col = s("ones_col", [1, 128], F32)
        self.small = s("small", [128, 64], F32)
        self.wbres = s("wbres", [128, 8 * NCHUNK], F32)
        self.bres_row = s("bres_row", [1, 8], F32)
        self.convw = s("convw", [128, 12 * NCHUNK], F32)
        self.firl = s("firl", [128, 64 * NCHUNK], F32)
        self.firs = s("firs", [128, 5 * NCHUNK], F32)
        self.wg2 = s("wg2", [128, 4 * NCHUNK], F32)
        self.bg2f = s("bg2f", [4, 1], F32)
        self.wog2 = s("wog2", [128, 1 * 4], F32)
        self.bog2v = s("bog2v", [1, 1], F32)
        self.statq = s("statq", [128, 4], F32)
        self.nwch = s("nwch", [128, NCHUNK], F32)
        self.bg1c = s("bg1c", [128, NCHUNK], F32)
        self.bog1c = s("bog1c", [128, 4], F32)
        self.beta_all = s("beta_all", [128, 8 * self.NT], F32)
        self.dq_x = s("dq_x", [128, self.NT], F32)
        self.dq_g = s("dq_g", [128, self.NT], F32)
        self.wmean = s("wmean", [128, 4], F32)
        self.wscl = s("wscl", [128, 4], F32)
        self.ones128 = s("ones128", [128, 1], F32)
        self.zero64 = s("zero64", [128, 64], F32)
        self.constc = s("constc", [128, 8], F32)
        self.identb = s("identb", [128, 128], BF16)

    def phase_prep(self):
        nc, s = self.nc, self
        with ExitStack() as ctx:
            e = ctx.enter_context
            ld = e(nc.semaphore("p_ld"))
            gp = e(nc.semaphore("p_gp"))
            ve = e(nc.semaphore("p_ve"))
            mm = e(nc.semaphore("p_mm"))
            sc = e(nc.semaphore("p_sc"))
            tmp = e(nc.sbuf_tensor("p_tmp", [128, 8], F32))
            tmp1 = e(nc.sbuf_tensor("p_tmp1", [1, 32], F32))
            pp = e(nc.psum_tensor("p_pp", [128, 16], F32))
            blk = e(nc.Block("prep"))

            NLD = 10 * NCHUNK + 2 * 4 + 5

            @blk.sync
            def _(sync):
                for c in range(NCHUNK):
                    sync.dma_start(out=s.wbres[:, c * 8:c * 8 + 4], in_=s.Wb[ts(c, 128), :]).then_inc(ld, 16)
                    sync.dma_start(out=s.wbres[:, c * 8 + 4:c * 8 + 8], in_=s.Wres[ts(c, 128), :]).then_inc(ld, 16)
                    sync.dma_start(out=s.firl[:, ts(c, 64)], in_=s.fir_long[ts(c, 128), :]).then_inc(ld, 16)
                    sync.dma_start(out=s.firs[:, ts(c, 5)], in_=s.fir_short[ts(c, 128), :]).then_inc(ld, 16)
                    sync.dma_start(out=s.wg2[:, ts(c, 4)], in_=s.Wg2[ts(c, 128), :]).then_inc(ld, 16)
                    sync.dma_start(out=s.convw[:, c * 12:c * 12 + 4], in_=s.wq_conv[ts(c, 128), :]).then_inc(ld, 16)
                    sync.dma_start(out=s.convw[:, c * 12 + 4:c * 12 + 8], in_=s.wk_conv[ts(c, 128), :]).then_inc(ld, 16)
                    sync.dma_start(out=s.convw[:, c * 12 + 8:c * 12 + 12], in_=s.wv_conv[ts(c, 128), :]).then_inc(ld, 16)
                    sync.dma_start(out=s.bg1c[:, c:c + 1], in_=s.bg1[ts(c, 128)].unsqueeze(1)).then_inc(ld, 16)
                    off = (c * 128) % 256
                    sync.dma_start(out=s.nwch[:, c:c + 1], in_=s.norm_w[off:off + 128].unsqueeze(1)).then_inc(ld, 16)
                for c in range(4):
                    sync.dma_start(out=s.wog2[:, c:c + 1], in_=s.Wog2[ts(c, 128), :]).then_inc(ld, 16)
                    sync.dma_start(out=s.bog1c[:, c:c + 1], in_=s.bog1[ts(c, 128)].unsqueeze(1)).then_inc(ld, 16)
                sync.dma_start(out=s.bres_row[0:1, 4:8], in_=s.bres.unsqueeze(0)).then_inc(ld, 16)
                sync.dma_start(out=s.bg2f[0:4, :], in_=s.bg2.unsqueeze(1)).then_inc(ld, 16)
                sync.dma_start(out=s.bog2v[0:1, :], in_=s.bog2.unsqueeze(0)).then_inc(ld, 16)
                sync.dma_start(out=tmp1[0:1, 0:1], in_=s.logit_temp.unsqueeze(0)).then_inc(ld, 16)
                sync.dma_start(out=tmp1[0:1, 8:12], in_=s.conv_res_logit.unsqueeze(0)).then_inc(ld, 16)

            @blk.gpsimd
            def _(gpsimd):
                gpsimd.memset(s.ident[:], 0.0).then_inc(gp, 1)
                gpsimd.memset(s.maskMn[:], 0.0).then_inc(gp, 1)
                gpsimd.memset(s.maskUI[:], 0.0).then_inc(gp, 1)
                gpsimd.memset(s.ones_col[:], 1.0).then_inc(gp, 1)
                gpsimd.memset(s.zero64[:], 0.0).then_inc(gp, 1)
                gpsimd.memset(s.bres_row[0:1, 0:4], 0.0).then_inc(gp, 1)
                gpsimd.memset(s.constc[:, 0:1], 1e-5).then_inc(gp, 1)
                gpsimd.memset(s.constc[:, 1:2], 1e-6).then_inc(gp, 1)
                gpsimd.memset(s.constc[:, 2:3], 1536.0).then_inc(gp, 1)
                gpsimd.memset(s.constc[:, 3:4], -1.0).then_inc(gp, 1)
                gpsimd.memset(s.constc[:, 4:5], 1.0).then_inc(gp, 1)
                gpsimd.memset(s.ones128[:], 1.0).then_inc(gp, 1)
                gpsimd.wait_ge(gp, 12)
                gpsimd.affine_select(out=s.ident[:], in_=s.ident[:], compare_op=ALU.not_equal,
                                     fill=1.0, base=0, pattern=[[-1, 128]],
                                     channel_multiplier=1).then_inc(gp, 1)
                gpsimd.affine_select(out=s.maskMn[:], in_=s.maskMn[:], compare_op=ALU.is_ge,
                                     fill=-1.0, base=0, pattern=[[-1, 128]],
                                     channel_multiplier=1).then_inc(gp, 1)
                gpsimd.affine_select(out=s.maskUI[:], in_=s.maskUI[:], compare_op=ALU.is_ge,
                                     fill=1.0, base=-1, pattern=[[-1, 128]],
                                     channel_multiplier=1).then_inc(gp, 1)

            @blk.scalar
            def _(scalar):
                scalar.wait_ge(ld, 16 * NLD)
                scalar.wait_ge(gp, 15)
                scalar.activation(tmp1[0:1, 1:2], tmp1[0:1, 0:1], AF.Exp)
                scalar.activation(tmp1[0:1, 16:20], tmp1[0:1, 8:12], AF.Sigmoid).then_inc(sc, 1)
                for it in range(7):
                    scalar.wait_ge(ve, it + 1)
                    scalar.activation(tmp1[0:1, 22:23], tmp1[0:1, 2:3], AF.Exp,
                                      scale=s.constc[0:1, 3:4]).then_inc(sc, 1)
                scalar.wait_ge(mm, 1)
                scalar.activation(tmp[:, 0:1], pp[:, 0:1], AF.Identity)
                scalar.activation(s.statq[:, 0:4], pp[:, 4:8], AF.Identity).then_inc(sc, 100)

            @blk.vector
            def _(vector):
                vector.wait_ge(gp, 15)
                vector.tensor_copy(s.identb[:], s.ident[:])
                vector.wait_ge(sc, 1)
                vector.tensor_scalar(tmp1[0:1, 20:21], tmp1[0:1, 1:2], 1.0, None, ALU.add)
                vector.tensor_scalar(tmp1[0:1, 2:3], tmp1[0:1, 0:1], 0.0, 0.35,
                                     ALU.max, ALU.add).then_inc(ve, 1)
                for it in range(7):
                    vector.wait_ge(sc, it + 2)
                    vector.tensor_scalar(tmp1[0:1, 23:24], tmp1[0:1, 22:23],
                                         tmp1[0:1, 20:21], -1.0, ALU.mult, ALU.add)
                    vector.tensor_add(tmp1[0:1, 2:3], tmp1[0:1, 2:3],
                                      tmp1[0:1, 23:24]).then_inc(ve, 1)
                vector.reciprocal(tmp1[0:1, 3:4], tmp1[0:1, 2:3]).then_inc(ve, 100)
                vector.wait_ge(sc, 108)
                vector.tensor_scalar(s.bg2f[0:4, :], s.bg2f[0:4, :], tmp[0:4, 0:1], None, ALU.mult)
                for c in range(NCHUNK):
                    vector.tensor_scalar(s.wg2[:, ts(c, 4)], s.wg2[:, ts(c, 4)],
                                         tmp[:, 0:1], None, ALU.mult)
                vector.nop().then_inc(ve, 2)

            @blk.tensor
            def _(tensor):
                tensor.wait_ge(gp, 15)
                tensor.wait_ge(ve, 108)
                tensor.matmul(pp[:, 0:1], s.ones_col[0:1, :], tmp1[0:1, 3:4],
                              start=True, stop=True)
                tensor.matmul(pp[:, 4:8], s.ones_col[0:1, :], tmp1[0:1, 16:20],
                              start=True, stop=True).then_inc(mm, 1)

    def barrier(self):
        self.nc.all_engine_barrier()

class Step:
    __slots__ = ("eng", "fn", "rd", "wr", "dma", "waits", "inc", "qkey")
    def __init__(self, eng, fn, rd, wr, dma):
        self.eng, self.fn, self.rd, self.wr, self.dma = eng, fn, rd, wr, dma


class Auto:
    ENGS = ("tensor", "vector", "scalar", "gpsimd", "sync")

    def __init__(self, m, name, coarse=False):
        self.m = m
        self.nc = m.nc
        self.name = name
        self.steps = []
        self.coarse = coarse

    @staticmethod
    def _regions(aps):
        out = []
        for a in aps:
            if a is None:
                continue
            if isinstance(a, tuple):
                if len(a) == 3:
                    out.append((a[0], 0, 1 << 30, a[1], a[2]))
                else:
                    out.append(a)
                continue
            t = a.tensor
            nm = t.name
            try:
                off = int(a.offset)
                shp = list(a.shape)
            except Exception:
                out.append((nm, 0, 1 << 30, 0, 1 << 60))
                continue
            sp = str(getattr(t, "space", ""))
            if sp.endswith("SBUF") or sp.endswith("PSUM"):
                rowlen = int(np.prod(t.shape[1:])) if len(t.shape) > 1 else 1
                width = int(np.prod(shp[1:])) if len(shp) > 1 else 1
                p_lo, c_lo = off // rowlen, off % rowlen
                p_hi, c_hi = p_lo + int(shp[0]), off % rowlen + width
                if c_hi > rowlen or p_hi > int(t.shape[0]):
                    out.append((nm, 0, 1 << 30, 0, 1 << 60))
                else:
                    out.append((nm, p_lo, p_hi, c_lo, c_hi))
            else:
                out.append((nm, 0, 1 << 30, off, off + int(np.prod(shp))))
        return out

    _PREP_BUFS = {"wbres", "firl", "firs", "wg2", "convw", "bg1c", "nwch", "wog2",
                  "bog1c", "bres_row", "bg2f", "bog2v", "p_tmp1"}

    @classmethod
    def _qkey_map(cls, name):
        if name in cls._PREP_BUFS:
            return "prep"
        m = re.match(r"b_fac\d+_([qkv])$", name)
        if m:
            return "b_fac_" + m.group(1)
        m = re.match(r"c_[qkv]t(\d)(\d)$", name)
        if m:
            return "c_ld" + m.group(1) + m.group(2)
        m = re.match(r"d_(?:ls|ll|do|vd)(\d)$", name)
        if m:
            return "d_tm" + m.group(1)
        return name

    @classmethod
    def _dma_buf(cls, rd, wr):
        for aps in (wr, rd):
            for a in aps:
                if a is None or isinstance(a, tuple):
                    continue
                if str(getattr(a.tensor, "space", "")).endswith("SBUF"):
                    return cls._qkey_map(a.tensor.name)
        return "misc"

    def step(self, eng, fn, rd=(), wr=(), dma=False):
        key = self._dma_buf(rd, wr) if dma else None
        r, w = self._regions(rd), self._regions(wr)
        co = self.coarse
        if co is True or (isinstance(co, (set, frozenset)) and
                          any(nm in co for (nm, *_x) in r + w)):
            r = [(nm, 0, 1 << 30, 0, 1 << 60) for (nm, *_x) in r]
            w = [(nm, 0, 1 << 30, 0, 1 << 60) for (nm, *_x) in w]
        st = Step(eng, fn, r, w, dma)
        st.qkey = key
        self.steps.append(st)

    def emit(self, limit=None):
        nc = self.nc
        if limit is not None:
            self.steps = self.steps[:limit]
        qname = lambda st: ("q_" + st.eng + "_" + st.qkey) if st.dma else st.eng
        counts = self.m._qcounts
        base_counts = dict(counts)
        writers = {}
        readers = {}
        for s_ in self.steps:
            q = qname(s_)
            n = counts.get(q, 0) + (16 if s_.dma else 1)
            counts[q] = n
            evt = (q, n)
            waits = {}

            def need(e2):
                q2, n2 = e2
                if s_.dma and (q2 == q or q2 == s_.eng):
                    return
                if q2.startswith("q_"):
                    n2 = counts.get(q2, 0)
                if waits.get(q2, 0) < n2:
                    waits[q2] = n2

            for (nm, pl, ph, lo, hi) in s_.rd:
                for (pl2, ph2, lo2, hi2, e2) in writers.get(nm, ()):
                    if lo < hi2 and lo2 < hi and pl < ph2 and pl2 < ph:
                        need(e2)
            for (nm, pl, ph, lo, hi) in s_.wr:
                for (pl2, ph2, lo2, hi2, e2) in writers.get(nm, ()):
                    if lo < hi2 and lo2 < hi and pl < ph2 and pl2 < ph:
                        need(e2)
                for (pl2, ph2, lo2, hi2, e2, q2) in readers.get(nm, ()):
                    if lo < hi2 and lo2 < hi and pl < ph2 and pl2 < ph:
                        need(e2)
            s_.waits = waits
            s_.inc = evt
            for (nm, pl, ph, lo, hi) in s_.wr:
                wl = writers.setdefault(nm, [])
                wl[:] = [w for w in wl
                         if not (pl <= w[0] and w[1] <= ph and lo <= w[2] and w[3] <= hi)]
                wl.append((pl, ph, lo, hi, evt))
                rl = readers.setdefault(nm, [])
                rl[:] = [r for r in rl
                         if not (pl <= r[0] and r[1] <= ph and lo <= r[2] and r[3] <= hi)]
            for (nm, pl, ph, lo, hi) in s_.rd:
                rl = readers.setdefault(nm, [])
                rl[:] = [r for r in rl
                         if not (pl <= r[0] and r[1] <= ph and lo <= r[2] and r[3] <= hi
                                 and r[5] == q)]
                rl.append((pl, ph, lo, hi, evt, q))
        with ExitStack() as ctx:
            e = ctx.enter_context
            sems = self.m._sems
            for q in counts:
                if q not in sems:
                    sems[q] = self.m.stack.enter_context(nc.semaphore(f"au_{q}"))
            blk = e(nc.Block(self.name))
            by_eng = {en: [s_ for s_ in self.steps if s_.eng == en] for en in self.ENGS}
            last_waited = {}

            def body(engine, en):
                for s_ in by_eng[en]:
                    for q2, n2 in sorted(s_.waits.items()):
                        if last_waited.get((en, q2), -1) >= n2:
                            continue
                for s_ in []:
                    pass

            def run_body(engine, en):
                for s_ in by_eng[en]:
                    for q2, n2 in sorted(s_.waits.items()):
                        if last_waited.get((en, q2), -1) >= n2:
                            continue
                        engine.wait_ge(sems[q2], n2)
                        last_waited[(en, q2)] = n2
                    ins = s_.fn(engine)
                    assert ins is not None, f"step fn on {en} returned None in {self.name}"
                    ins.then_inc(sems[s_.inc[0]], 16 if s_.dma else 1)
                pref = "q_" + en + "_"
                for qn in counts:
                    if qn.startswith(pref) and counts[qn] != base_counts.get(qn, 0):
                        engine.wait_ge(sems[qn], counts[qn])

            for en in self.ENGS:
                if by_eng[en]:
                    getattr(blk, en)(lambda engine, _en=en: run_body(engine, _en))

    def finish_wait(self):
        pass


def _phase_d1(self):
    nc, s = self.nc, self
    NT = self.NT
    self.O2T = nc.dram_tensor("O2T", [NT // 4, 128, 4096], F32, kind=("ExternalInput" if "O2T" in self.dbg_in else ("ExternalOutput" if "O2T" in self.dbg_out else "Internal"))).ap()
    au = Auto(self, "phD1", coarse=getattr(self, "d1_coarse", False))
    dbg = "D1DBG" in self.dbg_out
    if dbg:
        self.DBG_statT = nc.dram_tensor("DBG_statT", [16, 512], F32, kind="ExternalOutput").ap()
        self.DBG_lgT = nc.dram_tensor("DBG_lgT", [4, 512], F32, kind="ExternalOutput").ap()
        self.DBG_ltm = nc.dram_tensor("DBG_ltm", [4, 128, 24], F32, kind="ExternalOutput").ap()
        self.DBG_ob = nc.dram_tensor("DBG_ob", [4, 128, 1024], F32, kind="ExternalOutput").ap()
        self.DBG_ogsT = nc.dram_tensor("DBG_ogsT", [1, 512], F32, kind="ExternalOutput").ap()
        self.DBG_ogT = nc.dram_tensor("DBG_ogT", [8, 512], F32, kind="ExternalOutput").ap()
    _ctx = ExitStack()
    st_ = _ctx.enter_context
    wg1sb = st_(nc.sbuf_tensor("d_wg1", [128, 9 * 1024], F32))
    wog1sb = st_(nc.sbuf_tensor("d_wog1", [128, 9 * 512], F32))
    xlsb = st_(nc.sbuf_tensor("d_xl", [128, 4096], F32))
    TN = ("ls", "ll", "do", "vd")
    SRC = {"ls": s.LSTM, "ll": s.LLTM, "do": s.DOTM, "vd": s.VTM}
    tmb = {nm: [st_(nc.sbuf_tensor(f"d_{nm}{q}", [128, D], F32)) for q in range(4)]
           for nm in TN}
    stat4 = [st_(nc.sbuf_tensor(f"d_stat{q}", [128, 80], F32)) for q in range(4)]
    trash2 = [st_(nc.sbuf_tensor(f"d_tr{p}", [128, 1024], F32)) for p in range(2)]
    vst = st_(nc.sbuf_tensor("d_vst", [128, 16], F32))
    statT = st_(nc.sbuf_tensor("d_statT", [16, 512], F32))
    ogT = st_(nc.sbuf_tensor("d_ogT", [8, 512], F32))
    g1sb = [st_(nc.sbuf_tensor(f"d_g1{p}", [128, 512], F32)) for p in range(2)]
    ltm = [st_(nc.sbuf_tensor(f"d_ltm{p}", [128, 24], F32)) for p in range(4)]
    lgT = st_(nc.sbuf_tensor("d_lgT", [4, 512], F32))
    ogsT = st_(nc.sbuf_tensor("d_ogsT", [1, 512], F32))
    ob = [st_(nc.sbuf_tensor(f"d_ob{p}", [128, D], F32)) for p in range(4)]
    o2stg = st_(nc.sbuf_tensor("d_o2stg", [128, 4096], F32))
    pst = [st_(nc.psum_tensor(f"d_pst{p}", [128, 512], F32)) for p in range(2)]
    pg = [st_(nc.psum_tensor(f"d_pg{p}", [128, 512], F32)) for p in range(2)]
    plg = st_(nc.psum_tensor("d_plg", [128, 512], F32))
    ptr = [st_(nc.psum_tensor(f"d_ptr{p}", [128, 512], F32)) for p in range(2)]

    for c in range(NCHUNK):
        au.step("sync", lambda E, c=c: E.dma_start(out=wg1sb[:, ts(c, 1024)],
                in_=s.Wg1[ts(c, 128), :]), wr=[wg1sb[:, ts(c, 1024)]], dma=True)
    au.step("sync", lambda E: E.dma_start(out=wg1sb[0:16, 8 * 1024:9 * 1024],
            in_=s.Wg1[1024:1040, :]), wr=[wg1sb[0:16, 8 * 1024:9 * 1024]], dma=True)
    for c in range(NCHUNK):
        au.step("sync", lambda E, c=c: E.dma_start(out=wog1sb[:, ts(c, 512)],
                in_=s.Wog1[ts(c, 128), :]), wr=[wog1sb[:, ts(c, 512)]], dma=True)
    au.step("sync", lambda E: E.dma_start(out=wog1sb[0:8, 8 * 512:9 * 512],
            in_=s.Wog1[1024:1032, :]), wr=[wog1sb[0:8, 8 * 512:9 * 512]], dma=True)

    def stats_into(st, src_ap, W, cols, trash):
        base = bass.AP(src_ap.tensor, src_ap.offset, [[D, 128], [1, D]])
        for h in range(4):
            st("scalar", lambda E, h=h: E.activation(trash[:, ts(h, 256)],
                    base[:, ts(h, 256)], AF.Square, accum_out=W[:, 28 + h:29 + h]),
                    rd=[src_ap], wr=[trash[:, ts(h, 256)], W[:, 28 + h:29 + h]])
            st("vector", lambda E, h=h: E.reduce_sum(W[:, 24 + h:25 + h],
                    base[:, ts(h, 256)], AX.X), rd=[src_ap], wr=[W[:, 24 + h:25 + h]])
        st("vector", lambda E: E.reduce_sum(W[:, 32:33], src_ap, AX.X,
                apply_absolute_value=True), rd=[src_ap], wr=[W[:, 32:33]])
        st("vector", lambda E: E.tensor_scalar(W[:, 33:37], W[:, 24:28], 1.0 / 256, None,
                ALU.mult), rd=[W[:, 24:28]], wr=[W[:, 33:37]])
        st("vector", lambda E: E.reduce_sum(W[:, 37:38], W[:, 33:37], AX.X),
                rd=[W[:, 33:37]], wr=[W[:, 37:38]])
        st("vector", lambda E: E.tensor_scalar(W[:, cols:cols + 1], W[:, 37:38], 0.25,
                None, ALU.mult), rd=[W[:, 37:38]], wr=[W[:, cols:cols + 1]])
        st("vector", lambda E: E.tensor_mul(W[:, 38:42], W[:, 33:37], W[:, 33:37]),
                rd=[W[:, 33:37]], wr=[W[:, 38:42]])
        st("vector", lambda E: E.tensor_scalar(W[:, 42:46], W[:, 28:32], 1.0 / 256, None,
                ALU.mult), rd=[W[:, 28:32]], wr=[W[:, 42:46]])
        st("vector", lambda E: E.tensor_sub(W[:, 42:46], W[:, 42:46], W[:, 38:42]),
                rd=[W[:, 38:42], W[:, 42:46]], wr=[W[:, 42:46]])
        st("vector", lambda E: E.reduce_sum(W[:, 46:47], W[:, 42:46], AX.X),
                rd=[W[:, 42:46]], wr=[W[:, 46:47]])
        st("vector", lambda E: E.tensor_scalar(W[:, cols + 1:cols + 2], W[:, 46:47], 0.25,
                None, ALU.mult), rd=[W[:, 46:47]], wr=[W[:, cols + 1:cols + 2]])
        st("vector", lambda E: E.tensor_scalar(W[:, cols + 2:cols + 3], W[:, 32:33],
                1.0 / 1024, None, ALU.mult), rd=[W[:, 32:33]], wr=[W[:, cols + 2:cols + 3]])
        st("scalar", lambda E: E.activation(W[:, 48:52], W[:, 28:32], AF.Sqrt),
                rd=[W[:, 28:32]], wr=[W[:, 48:52]])
        st("vector", lambda E: E.reduce_sum(W[:, 52:53], W[:, 48:52], AX.X),
                rd=[W[:, 48:52]], wr=[W[:, 52:53]])
        st("vector", lambda E: E.tensor_scalar(W[:, cols + 3:cols + 4], W[:, 52:53], 0.25,
                None, ALU.mult), rd=[W[:, 52:53]], wr=[W[:, cols + 3:cols + 4]])

    def rr_merge(gens):
        pos = [0] * len(gens)
        rem = sum(len(g_) for g_ in gens)
        while rem:
            for k in range(len(gens)):
                if pos[k] < len(gens[k]):
                    for (eng, fn, rd, wr, dma) in gens[k][pos[k]]:
                        au.step(eng, fn, rd=rd, wr=wr, dma=dma)
                    pos[k] += 1
                    rem -= 1

    def mk_st():
        groups = []
        def _st(eng, fn, rd=(), wr=(), dma=False, glue=False):
            if glue and groups:
                groups[-1].append((eng, fn, rd, wr, dma))
            else:
                groups.append([(eng, fn, rd, wr, dma)])
        return groups, _st

    def stats_gens(stp):
        gens = []
        for q in range(4):
            tt = stp * 4 + q
            steps, st = mk_st()
            for nm in TN:
                st("sync", lambda E, nm=nm, tt=tt, q=q: E.dma_start(
                    out=tmb[nm][q][:], in_=SRC[nm][ts(tt, 128), :]),
                    wr=[tmb[nm][q][:]], dma=True)
            for j, nm in enumerate(TN):
                stats_into(st, tmb[nm][q][:], stat4[q], j * 4, trash2[q % 2])
            st("vector", lambda E, q=q: E.tensor_copy(vst[:, ts(q, 4)],
                    stat4[q][:, 12:16]), rd=[stat4[q][:, 12:16]], wr=[vst[:, ts(q, 4)]])
            st("tensor", lambda E, q=q: E.transpose(pst[q % 2][0:16, 0:128],
                    stat4[q][:, 0:16], s.ident[:]), rd=[stat4[q][:, 0:16]],
                    wr=[pst[q % 2][:]])
            st("vector", lambda E, q=q: E.tensor_copy(statT[:, ts(q, 128)],
                    pst[q % 2][0:16, 0:128]), rd=[pst[q % 2][:]],
                    wr=[statT[:, ts(q, 128)]], glue=True)
            gens.append(steps)
        return gens

    for stp in range(NT // 4):
        au.step("sync", lambda E, stp=stp: E.dma_start(out=xlsb[:], in_=s.XLNT[stp]),
                wr=[xlsb[:]], dma=True)
        rr_merge(stats_gens(stp))
        for gsl in range(NCHUNK):
            def mmg(E, gsl=gsl):
                for c in range(NCHUNK):
                    E.matmul(pg[gsl % 2][:], wg1sb[:, c * 1024 + gsl * 128: c * 1024 + (gsl + 1) * 128],
                             xlsb[:, ts(c, 512)], start=(c == 0), stop=False)
                return E.matmul(pg[gsl % 2][:], wg1sb[0:16, 8 * 1024 + gsl * 128: 8 * 1024 + (gsl + 1) * 128],
                                statT[:], start=False, stop=True)
            au.step("tensor", mmg, rd=[xlsb[:], statT[:], (wg1sb.ap().tensor.name, 0, 1 << 40)],
                    wr=[pg[gsl % 2][:]])
            au.step("scalar", lambda E, gsl=gsl: E.activation(g1sb[gsl % 2][:], pg[gsl % 2][:],
                    AF.Gelu, bias=s.bg1c[:, gsl:gsl + 1]),
                    rd=[pg[gsl % 2][:]], wr=[g1sb[gsl % 2][:]])
            au.step("tensor", lambda E, gsl=gsl: E.matmul(plg[0:4, :], s.wg2[:, ts(gsl, 4)],
                    g1sb[gsl % 2][:], start=(gsl == 0), stop=(gsl == 7)),
                    rd=[g1sb[gsl % 2][:], s.wg2[:]], wr=[plg[:]])
        au.step("scalar", lambda E: E.activation(lgT[:], plg[0:4, :], AF.Identity,
                bias=s.bg2f[:, 0:1]), rd=[plg[:]], wr=[lgT[:]])
        if dbg and stp == 0:
            au.step("gpsimd", lambda E: E.dma_start(out=self.DBG_statT, in_=statT[:]),
                    rd=[statT[:]], dma=True)
            au.step("gpsimd", lambda E: E.dma_start(out=self.DBG_lgT, in_=lgT[:]),
                    rd=[lgT[:]], dma=True)
        gens = []
        for q in range(4):
            tt = stp * 4 + q
            lt = ltm[q]
            steps, st = mk_st()
            st("tensor", lambda E, q=q: E.transpose(pst[q % 2][0:128, 0:4],
                    lgT[:, ts(q, 128)], s.ident[0:4, 0:4]),
                    rd=[lgT[:, ts(q, 128)]], wr=[pst[q % 2][:]])
            st("vector", lambda E, lt=lt, q=q: E.tensor_copy(lt[:, 0:4],
                    pst[q % 2][0:128, 0:4]), rd=[pst[q % 2][:]], wr=[lt[:, 0:4]], glue=True)
            st("vector", lambda E, lt=lt: E.reduce_max(lt[:, 12:13], lt[:, 0:4], AX.X,
                    negate=True), rd=[lt[:, 0:4]], wr=[lt[:, 12:13]])
            st("scalar", lambda E, lt=lt: E.activation(lt[:, 4:8], lt[:, 0:4], AF.Exp,
                    bias=lt[:, 12:13]), rd=[lt[:, 0:4], lt[:, 12:13]], wr=[lt[:, 4:8]])
            st("vector", lambda E, lt=lt: E.reduce_sum(lt[:, 13:14], lt[:, 4:8], AX.X),
                    rd=[lt[:, 4:8]], wr=[lt[:, 13:14]])
            st("vector", lambda E, lt=lt: E.reciprocal(lt[:, 14:15], lt[:, 13:14]),
                    rd=[lt[:, 13:14]], wr=[lt[:, 14:15]])
            st("vector", lambda E, lt=lt: E.tensor_scalar(lt[:, 4:8], lt[:, 4:8],
                    lt[:, 14:15], 0.02, ALU.mult, ALU.max),
                    rd=[lt[:, 4:8], lt[:, 14:15]], wr=[lt[:, 4:8]])
            st("vector", lambda E, lt=lt: E.reduce_sum(lt[:, 15:16], lt[:, 4:8], AX.X),
                    rd=[lt[:, 4:8]], wr=[lt[:, 15:16]])
            st("vector", lambda E, lt=lt: E.reciprocal(lt[:, 16:17], lt[:, 15:16]),
                    rd=[lt[:, 15:16]], wr=[lt[:, 16:17]])
            st("vector", lambda E, lt=lt: E.tensor_scalar(lt[:, 4:8], lt[:, 4:8],
                    lt[:, 16:17], None, ALU.mult),
                    rd=[lt[:, 4:8], lt[:, 16:17]], wr=[lt[:, 4:8]])
            st("vector", lambda E, lt=lt, tt=tt: E.tensor_mul(lt[:, 8:12], s.statq[:, 0:4],
                    s.beta_all[:, tt * 8 + 4: tt * 8 + 8]),
                    rd=[s.statq[:], s.beta_all[:, tt * 8 + 4: tt * 8 + 8]], wr=[lt[:, 8:12]])
            st("vector", lambda E, lt=lt: E.tensor_scalar(lt[:, 8:12], lt[:, 8:12],
                    lt[:, 4:5], None, ALU.add), rd=[lt[:, 4:5], lt[:, 8:12]], wr=[lt[:, 8:12]])
            for h in range(4):
                st("vector", lambda E, h=h, q=q, lt=lt: E.tensor_scalar(
                    ob[q][:, ts(h, 256)], tmb["ls"][q][:, ts(h, 256)], lt[:, 8 + h: 9 + h],
                    None, ALU.mult), rd=[tmb["ls"][q][:, ts(h, 256)], lt[:, 8 + h: 9 + h]],
                    wr=[ob[q][:, ts(h, 256)]])
            for nm, pc in (("ll", 5), ("do", 6), ("vd", 7)):
                st("vector", lambda E, nm=nm, pc=pc, q=q, lt=lt: E.scalar_tensor_tensor(
                    ob[q][:], tmb[nm][q][:], lt[:, pc:pc + 1], ob[q][:], ALU.mult, ALU.add),
                    rd=[tmb[nm][q][:], lt[:, pc:pc + 1], ob[q][:]], wr=[ob[q][:]])
            if dbg and stp == 0:
                st("gpsimd", lambda E, q=q: E.dma_start(out=self.DBG_ob[q],
                        in_=ob[q][:]), rd=[ob[q][:]], dma=True)
                st("gpsimd", lambda E, q=q: E.dma_start(out=self.DBG_ltm[q],
                        in_=ltm[q][:]), rd=[ltm[q][:]], dma=True)
            stats_into(st, ob[q][:], stat4[q], 16, o2stg[:, ts(q, 1024)])
            st("vector", lambda E, q=q: E.tensor_copy(stat4[q][:, 20:24],
                    vst[:, ts(q, 4)]), rd=[vst[:, ts(q, 4)]], wr=[stat4[q][:, 20:24]])
            st("tensor", lambda E, q=q: E.transpose(pst[q % 2][0:8, 0:128],
                    stat4[q][:, 16:24], s.ident[:]), rd=[stat4[q][:, 16:24]],
                    wr=[pst[q % 2][:]])
            st("vector", lambda E, q=q: E.tensor_copy(ogT[:, ts(q, 128)],
                    pst[q % 2][0:8, 0:128]), rd=[pst[q % 2][:]], wr=[ogT[:, ts(q, 128)]],
                    glue=True)
            gens.append(steps)
        rr_merge(gens)
        for osl in range(4):
            def mmo(E, osl=osl):
                for c in range(NCHUNK):
                    E.matmul(pg[osl % 2][:], wog1sb[:, c * 512 + osl * 128: c * 512 + (osl + 1) * 128],
                             xlsb[:, ts(c, 512)], start=(c == 0), stop=False)
                return E.matmul(pg[osl % 2][:], wog1sb[0:8, 8 * 512 + osl * 128: 8 * 512 + (osl + 1) * 128],
                                ogT[:], start=False, stop=True)
            au.step("tensor", mmo, rd=[xlsb[:], ogT[:], (wog1sb.ap().tensor.name, 0, 1 << 40)],
                    wr=[pg[osl % 2][:]])
            au.step("scalar", lambda E, osl=osl: E.activation(g1sb[osl % 2][:], pg[osl % 2][:],
                    AF.Gelu, bias=s.bog1c[:, osl:osl + 1]),
                    rd=[pg[osl % 2][:]], wr=[g1sb[osl % 2][:]])
            au.step("tensor", lambda E, osl=osl: E.matmul(plg[32:33, :], s.wog2[:, osl:osl + 1],
                    g1sb[osl % 2][:], start=(osl == 0), stop=(osl == 3)),
                    rd=[g1sb[osl % 2][:], s.wog2[:]], wr=[plg[:]])
        au.step("scalar", lambda E: E.activation(ogsT[:], plg[32:33, :], AF.Sigmoid,
                bias=s.bog2v[:, 0:1]), rd=[plg[:]], wr=[ogsT[:]])
        if dbg and stp == 0:
            au.step("gpsimd", lambda E: E.dma_start(out=self.DBG_ogsT, in_=ogsT[:]),
                    rd=[ogsT[:]], dma=True)
            au.step("gpsimd", lambda E: E.dma_start(out=self.DBG_ogT, in_=ogT[:]),
                    rd=[ogT[:]], dma=True)
        gens = []
        for q in range(4):
            lt = ltm[q]
            W = stat4[q]
            steps, st = mk_st()
            st("tensor", lambda E, q=q: E.transpose(pst[q % 2][0:128, 4:5],
                    ogsT[:, ts(q, 128)], s.ident[0:1, 0:1]),
                    rd=[ogsT[:, ts(q, 128)]], wr=[pst[q % 2][:]])
            st("vector", lambda E, lt=lt, q=q: E.tensor_copy(lt[:, 17:18],
                    pst[q % 2][0:128, 4:5]), rd=[pst[q % 2][:]], wr=[lt[:, 17:18]], glue=True)
            st("vector", lambda E, q=q, lt=lt: E.tensor_scalar(ob[q][:], ob[q][:],
                    lt[:, 17:18], None, ALU.mult), rd=[ob[q][:], lt[:, 17:18]], wr=[ob[q][:]])
            for h in range(4):
                st("scalar", lambda E, q=q, h=h, W=W: E.activation(
                    o2stg[:, q * 1024 + h * 256: q * 1024 + (h + 1) * 256],
                    ob[q][:, ts(h, 256)], AF.Square, accum_out=W[:, 64 + h:65 + h]),
                    rd=[ob[q][:, ts(h, 256)]],
                    wr=[o2stg[:, q * 1024 + h * 256: q * 1024 + (h + 1) * 256],
                        W[:, 64 + h:65 + h]])
            st("vector", lambda E, W=W: E.tensor_scalar(W[:, 64:68],
                    W[:, 64:68], 1.0 / 256, None, ALU.mult),
                    rd=[W[:, 64:68]], wr=[W[:, 64:68]])
            st("scalar", lambda E, W=W: E.activation(W[:, 68:72], W[:, 64:68],
                    AF.Sqrt, bias=s.constc[:, 0:1]), rd=[W[:, 64:68]], wr=[W[:, 68:72]])
            st("vector", lambda E, W=W: E.reciprocal(W[:, 72:76], W[:, 68:72]),
                    rd=[W[:, 68:72]], wr=[W[:, 72:76]])
            for h in range(4):
                st("vector", lambda E, h=h, q=q, W=W: E.tensor_scalar(ob[q][:, ts(h, 256)],
                        ob[q][:, ts(h, 256)], W[:, 72 + h: 73 + h], None, ALU.mult),
                        rd=[ob[q][:, ts(h, 256)], W[:, 72 + h: 73 + h]],
                        wr=[ob[q][:, ts(h, 256)]])
            for c in range(NCHUNK):
                b = (q + c) % 2
                st("tensor", lambda E, c=c, q=q, b=b: E.transpose(
                    ptr[b][:, ts(c % 4, 128)], ob[q][:, ts(c, 128)], s.ident[:]),
                    rd=[ob[q][:, ts(c, 128)]], wr=[ptr[b][:]])
                st("vector" if c % 2 == 0 else "scalar",
                        lambda E, c=c, q=q, b=b: (E.tensor_copy if c % 2 == 0 else (lambda o_, i_: E.activation(o_, i_, AF.Identity)))(
                            o2stg[:, c * 512 + q * 128: c * 512 + (q + 1) * 128],
                            ptr[b][:, ts(c % 4, 128)]),
                        rd=[ptr[b][:]], wr=[o2stg[:, c * 512 + q * 128: c * 512 + (q + 1) * 128]],
                        glue=True)
            gens.append(steps)
        rr_merge(gens)
        au.step("gpsimd", lambda E, stp=stp: E.dma_start(out=self.O2T[stp], in_=o2stg[:]),
                rd=[o2stg[:]], dma=True)
    au.emit(limit=getattr(self, "d1_limit", None))
    _ctx.close()


MK.phase_d1 = _phase_d1


def _phase_w(self):
    nc, s = self.nc, self
    au = Auto(self, "phW")
    _ctx = ExitStack()
    st_ = _ctx.enter_context
    wch = [st_(nc.sbuf_tensor(f"w_ch{p}", [128, 4096], F32)) for p in range(2)]
    tern = st_(nc.sbuf_tensor("w_tern", [128, 4096], F32))
    tb = st_(nc.sbuf_tensor("w_tb", [128, 4096], BF16))
    red = st_(nc.sbuf_tensor("w_red", [128, 40], F32))
    sc1 = st_(nc.sbuf_tensor("w_sc1", [1, 8], F32))
    stg = st_(nc.sbuf_tensor("w_stg", [128, 32768], BF16))
    pw = st_(nc.psum_tensor("w_pw", [128, 512], F32))
    pt = [st_(nc.psum_tensor(f"w_pt{p}", [128, 256], BF16)) for p in range(2)]

    WS = [(s.w_gate, s.WGT, 32, 1024, 0), (s.w_up, s.WUP, 32, 1024, 1),
          (s.w_down, s.WDN, 8, 4096, 2)]
    for w_ap, dst, nch, width, widx in WS:
        ncols = width // 1024
        for c in range(nch):
            au.step("sync", lambda E, c=c, w_ap=w_ap, width=width: E.dma_start(
                out=wch[c % 2][:, 0:width], in_=w_ap[ts(c, 128), :]),
                wr=[wch[c % 2][:, 0:width]], dma=True)
            au.step("vector", lambda E, c=c, width=width: E.reduce_sum(
                red[:, c:c + 1], wch[c % 2][:, 0:width], AX.X, apply_absolute_value=True),
                rd=[wch[c % 2][:, 0:width]], wr=[red[:, c:c + 1]])
        au.step("vector", lambda E, nch=nch: E.reduce_sum(red[:, 32:33], red[:, 0:nch], AX.X),
                rd=[red[:, 0:32]], wr=[red[:, 32:33]])
        au.step("tensor", lambda E: E.matmul(pw[0:1, 0:1], red[:, 32:33], s.ones128[:],
                start=True, stop=True), rd=[red[:, 32:33], s.ones128[:]], wr=[pw[:]])
        au.step("vector", lambda E: E.tensor_scalar(sc1[0:1, 0:1], pw[0:1, 0:1],
                1.0 / (4096 * 1024), 1e-5, ALU.mult, ALU.max), rd=[pw[:]], wr=[sc1[0:1, 0:1]])
        au.step("vector", lambda E: E.reciprocal(sc1[0:1, 1:2], sc1[0:1, 0:1]),
                rd=[sc1[0:1, 0:1]], wr=[sc1[0:1, 1:2]])
        au.step("tensor", lambda E: E.matmul(pw[0:128, 1:2], s.ones_col[0:1, :], sc1[0:1, 0:1],
                start=True, stop=True), rd=[sc1[0:1, 0:1], s.ones_col[:]], wr=[pw[:]])
        au.step("tensor", lambda E: E.matmul(pw[0:128, 2:3], s.ones_col[0:1, :], sc1[0:1, 1:2],
                start=True, stop=True), rd=[sc1[0:1, 1:2], s.ones_col[:]], wr=[pw[:]])
        au.step("vector", lambda E, widx=widx: E.tensor_copy(s.wmean[:, widx:widx + 1],
                pw[0:128, 1:2]), rd=[pw[:]], wr=[s.wmean[:, widx:widx + 1]])
        au.step("vector", lambda E, widx=widx: E.tensor_copy(s.wscl[:, widx:widx + 1],
                pw[0:128, 2:3]), rd=[pw[:]], wr=[s.wscl[:, widx:widx + 1]])
        for c in range(nch):
            au.step("sync", lambda E, c=c, w_ap=w_ap, width=width: E.dma_start(
                out=wch[c % 2][:, 0:width], in_=w_ap[ts(c, 128), :]),
                wr=[wch[c % 2][:, 0:width]], dma=True)
            au.step("vector", lambda E, c=c, width=width, widx=widx: E.tensor_scalar(
                tern[:, 0:width], wch[c % 2][:, 0:width], s.wscl[:, widx:widx + 1],
                None, ALU.mult), rd=[wch[c % 2][:, 0:width], s.wscl[:, widx:widx + 1]],
                wr=[tern[:, 0:width]])
            au.step("vector", lambda E, width=width: E.tensor_scalar(
                tb[:, 0:width], tern[:, 0:width], 0.5, None, ALU.is_ge),
                rd=[tern[:, 0:width]], wr=[tb[:, 0:width]])
            au.step("vector", lambda E, width=width: E.tensor_scalar(
                tern[:, 0:width], tern[:, 0:width], -0.5, None, ALU.is_le),
                rd=[tern[:, 0:width]], wr=[tern[:, 0:width]])
            au.step("vector", lambda E, width=width: E.tensor_sub(
                tb[:, 0:width], tern[:, 0:width], tb[:, 0:width]),
                rd=[tern[:, 0:width], tb[:, 0:width]], wr=[tb[:, 0:width]])
            for b in range(width // 128):
                col = b * (nch * 128) + c * 128
                au.step("tensor", lambda E, b=b: E.transpose(pt[b % 2][:, 0:128],
                        tb[:, ts(b, 128)], s.identb[:]), rd=[tb[:, ts(b, 128)]], wr=[pt[b % 2][:]])
                au.step("vector" if b % 2 == 0 else "scalar",
                        lambda E, b=b, col=col: (E.tensor_copy if b % 2 == 0 else (lambda o_, i_: E.activation(o_, i_, AF.Identity)))(
                            stg[:, col:col + 128], pt[b % 2][:, 0:128]),
                        rd=[pt[b % 2][:]], wr=[stg[:, col:col + 128]])
        for b in range(width // 128):
            au.step("gpsimd", lambda E, b=b, dst=dst, nch=nch: E.dma_start(
                out=dst[ts(b, 128), :], in_=stg[:, b * (nch * 128):(b + 1) * (nch * 128)]),
                rd=[stg[:, b * (nch * 128):(b + 1) * (nch * 128)]], dma=True)
    au.emit()
    _ctx.close()


MK.phase_w = _phase_w


def _phase_d2(self):
    nc, s = self.nc, self
    NT = self.NT
    au = Auto(self, "phD2")
    _ctx = ExitStack()
    st_ = _ctx.enter_context
    wo_sb = st_(nc.sbuf_tensor("d2_wo", [128, 8192], F32))
    o2sb = st_(nc.sbuf_tensor("d2_o2", [128, 4096], F32))
    xb = [st_(nc.sbuf_tensor(f"d2_x{p}", [128, D], F32)) for p in range(2)]
    y1b = [st_(nc.sbuf_tensor(f"d2_y1{p}", [128, D], F32)) for p in range(2)]
    xl2 = [st_(nc.sbuf_tensor(f"d2_xl2{p}", [128, D], F32)) for p in range(2)]
    h16 = st_(nc.sbuf_tensor("d2_h16", [128, D], FP16))
    sqt = st_(nc.sbuf_tensor("d2_sqt", [128, D], F32))
    wk = [st_(nc.sbuf_tensor(f"d2_wk{p}", [128, 16], F32)) for p in range(2)]
    xqstg = st_(nc.sbuf_tensor("d2_xqstg", [128, 4096], BF16))
    xq16 = [st_(nc.sbuf_tensor(f"d2_xq16{p}", [128, D], BF16)) for p in range(2)]
    py = [st_(nc.psum_tensor(f"d2_py{p}", [128, 512], F32)) for p in range(2)]
    ptq = [st_(nc.psum_tensor(f"d2_ptq{p}", [128, 128], BF16)) for p in range(2)]

    for c in range(NCHUNK):
        au.step("sync", lambda E, c=c: E.dma_start(out=wo_sb[:, ts(c, 1024)],
                in_=s.Wo[ts(c, 128), :]), wr=[wo_sb[:, ts(c, 1024)]], dma=True)
        au.step("vector", lambda E, c=c: E.tensor_scalar(wo_sb[:, ts(c, 1024)],
                wo_sb[:, ts(c, 1024)], s.nwch[:, c:c + 1], None, ALU.mult),
                rd=[wo_sb[:, ts(c, 1024)], s.nwch[:, c:c + 1]], wr=[wo_sb[:, ts(c, 1024)]])

    for tt in range(NT):
        p = tt % 2
        q = tt % 4
        st = tt // 4
        if q == 0:
            au.step("sync", lambda E, st=st: E.dma_start(out=o2sb[:], in_=s.O2T[st]),
                    wr=[o2sb[:]], dma=True)
        au.step("sync", lambda E, tt=tt, p=p: E.dma_start(out=xb[p][:],
                in_=s.x_in[ts(tt, 128), :]), wr=[xb[p][:]], dma=True)
        for nch in range(2):
            def mmwo(E, nch=nch, q=q):
                for c in range(NCHUNK):
                    ins = E.matmul(py[nch][:], o2sb[:, c * 512 + q * 128: c * 512 + (q + 1) * 128],
                                   wo_sb[:, c * 1024 + nch * 512: c * 1024 + (nch + 1) * 512],
                                   start=(c == 0), stop=(c == 7))
                return ins
            au.step("tensor", mmwo, rd=[o2sb[:], (wo_sb.ap().tensor.name, 0, 1 << 40)],
                    wr=[py[nch][:]])
            au.step("vector", lambda E, nch=nch, p=p: E.scalar_tensor_tensor(
                y1b[p][:, ts(nch, 512)], py[nch][:], 1.0, xb[p][:, ts(nch, 512)],
                ALU.mult, ALU.add), rd=[py[nch][:], xb[p][:, ts(nch, 512)]],
                wr=[y1b[p][:, ts(nch, 512)]])
        au.step("gpsimd", lambda E, tt=tt, p=p: E.dma_start(out=s.Y1[ts(tt, 128), :],
                in_=y1b[p][:]), rd=[y1b[p][:]], dma=True)
        W = wk[p]
        au.step("scalar", lambda E, p=p, W=W: E.activation(sqt[:], y1b[p][:], AF.Square,
                accum_out=W[:, 0:1]), rd=[y1b[p][:]], wr=[sqt[:], W[:, 0:1]])
        au.step("vector", lambda E, p=p, W=W: E.reduce_sum(W[:, 1:2], y1b[p][:], AX.X,
                negate=True), rd=[y1b[p][:]], wr=[W[:, 1:2]])
        au.step("vector", lambda E, W=W: E.tensor_scalar(W[:, 2:3], W[:, 1:2], 1.0 / D,
                None, ALU.mult), rd=[W[:, 1:2]], wr=[W[:, 2:3]])
        au.step("vector", lambda E, W=W: E.tensor_mul(W[:, 3:4], W[:, 2:3], W[:, 2:3]),
                rd=[W[:, 2:3]], wr=[W[:, 3:4]])
        au.step("vector", lambda E, W=W: E.tensor_scalar(W[:, 4:5], W[:, 0:1], 1.0 / D,
                W[:, 3:4], ALU.mult, ALU.subtract), rd=[W[:, 0:1], W[:, 3:4]], wr=[W[:, 4:5]])
        au.step("scalar", lambda E, W=W: E.activation(W[:, 5:6], W[:, 4:5], AF.Sqrt,
                bias=s.constc[:, 0:1]), rd=[W[:, 4:5]], wr=[W[:, 5:6]])
        au.step("vector", lambda E, W=W: E.reciprocal(W[:, 6:7], W[:, 5:6]),
                rd=[W[:, 5:6]], wr=[W[:, 6:7]])
        au.step("vector", lambda E, p=p, W=W: E.tensor_scalar(xl2[p][:], y1b[p][:],
                W[:, 2:3], W[:, 6:7], ALU.add, ALU.mult),
                rd=[y1b[p][:], W[:, 2:3], W[:, 6:7]], wr=[xl2[p][:]])
        au.step("vector", lambda E, p=p, W=W: E.reduce_max(W[:, 7:8], xl2[p][:], AX.X,
                apply_absolute_value=True), rd=[xl2[p][:]], wr=[W[:, 7:8]])
        au.step("vector", lambda E, W=W: E.tensor_scalar(W[:, 8:9], W[:, 7:8], 1e-5,
                None, ALU.max), rd=[W[:, 7:8]], wr=[W[:, 8:9]])
        au.step("vector", lambda E, W=W: E.reciprocal(W[:, 9:10], W[:, 8:9]),
                rd=[W[:, 8:9]], wr=[W[:, 9:10]])
        au.step("vector", lambda E, W=W: E.tensor_scalar(W[:, 10:11], W[:, 9:10], 127.0,
                None, ALU.mult), rd=[W[:, 9:10]], wr=[W[:, 10:11]])
        au.step("vector", lambda E, W=W, tt=tt: E.tensor_scalar(s.dq_x[:, tt:tt + 1],
                W[:, 8:9], 1.0 / 127, None, ALU.mult), rd=[W[:, 8:9]],
                wr=[s.dq_x[:, tt:tt + 1]])
        au.step("scalar", lambda E, p=p, W=W: E.activation(sqt[:], xl2[p][:], AF.Identity,
                scale=W[:, 10:11], bias=s.constc[:, 2:3]),
                rd=[xl2[p][:], W[:, 10:11]], wr=[sqt[:]])
        au.step("vector", lambda E: E.tensor_copy(h16[:], sqt[:]), rd=[sqt[:]], wr=[h16[:]])
        au.step("vector", lambda E, p=p: E.tensor_scalar(xl2[p][:], h16[:], -1.0, 1536.0,
                ALU.mult, ALU.add), rd=[h16[:]], wr=[xl2[p][:]])
        au.step("vector", lambda E, p=p: E.tensor_scalar(xq16[p][:], xl2[p][:], -127.0, 128.0,
                ALU.max, ALU.min), rd=[xl2[p][:]], wr=[xq16[p][:]])
        for c in range(NCHUNK):
            b = c % 2
            au.step("tensor", lambda E, c=c, p=p, b=b: E.transpose(ptq[b][:],
                    xq16[p][:, ts(c, 128)], s.identb[:]), rd=[xq16[p][:, ts(c, 128)]],
                    wr=[ptq[b][:]])
            au.step("vector" if c % 2 == 0 else "scalar",
                    lambda E, c=c, q=q, b=b: (E.tensor_copy if c % 2 == 0 else (lambda o_, i_: E.activation(o_, i_, AF.Identity)))(
                        xqstg[:, c * 512 + q * 128: c * 512 + (q + 1) * 128], ptq[b][:]),
                    rd=[ptq[b][:]], wr=[xqstg[:, c * 512 + q * 128: c * 512 + (q + 1) * 128]])
        if q == 3:
            au.step("gpsimd", lambda E, st=st: E.dma_start(out=s.XQT[st], in_=xqstg[:]),
                    rd=[xqstg[:]], dma=True)
    au.emit()
    _ctx.close()


MK.phase_d2 = _phase_d2


def _phase_e(self):
    nc, s = self.nc, self
    NT = self.NT
    au = Auto(self, "phE")
    _ctx = ExitStack()
    st_ = _ctx.enter_context
    wgt_sb = st_(nc.sbuf_tensor("e_wg", [128, 8 * 4096], BF16))
    wup_sb = st_(nc.sbuf_tensor("e_wu", [128, 8 * 4096], BF16))
    xqsb = [st_(nc.sbuf_tensor(f"e_xq{p}", [128, 4096], BF16)) for p in range(2)]
    gsil = st_(nc.sbuf_tensor("e_gsil", [128, 1024], F32))
    gbuf = st_(nc.sbuf_tensor("e_gbuf", [128, 4096], F32))
    h16 = st_(nc.sbuf_tensor("e_h16", [128, 4096], FP16))
    wk = st_(nc.sbuf_tensor("e_wk", [128, 16], F32))
    gstg = st_(nc.sbuf_tensor("e_gstg", [128, 4096], BF16))
    gb16 = st_(nc.sbuf_tensor("e_gb16", [128, 4096], BF16))
    pga = st_(nc.psum_tensor("e_pga", [128, 1024], F32))
    pup = st_(nc.psum_tensor("e_pup", [128, 1024], F32))
    ptq = [st_(nc.psum_tensor(f"e_ptq{p}", [128, 128], BF16)) for p in range(2)]

    for c in range(NCHUNK):
        au.step("sync", lambda E, c=c: E.dma_start(out=wgt_sb[:, ts(c, 4096)],
                in_=s.WGT[ts(c, 128), :]), wr=[wgt_sb[:, ts(c, 4096)]], dma=True)
        au.step("sync", lambda E, c=c: E.dma_start(out=wup_sb[:, ts(c, 4096)],
                in_=s.WUP[ts(c, 128), :]), wr=[wup_sb[:, ts(c, 4096)]], dma=True)

    for tt in range(NT):
        p = tt % 2
        q = tt % 4
        st = tt // 4
        p2 = st % 2
        if q == 0:
            au.step("sync", lambda E, st=st, p2=p2: E.dma_start(
                out=xqsb[p2][:], in_=s.XQT[st]), wr=[xqsb[p2][:]], dma=True)
        au.step("vector", lambda E, tt=tt: E.tensor_scalar(wk[:, 4:5], s.dq_x[:, tt:tt + 1],
                s.wmean[:, 0:1], None, ALU.mult),
                rd=[s.dq_x[:, tt:tt + 1], s.wmean[:, 0:1]], wr=[wk[:, 4:5]])
        au.step("vector", lambda E, tt=tt: E.tensor_scalar(wk[:, 5:6], s.dq_x[:, tt:tt + 1],
                s.wmean[:, 1:2], None, ALU.mult),
                rd=[s.dq_x[:, tt:tt + 1], s.wmean[:, 1:2]], wr=[wk[:, 5:6]])
        for r in range(4):
            def mmr(E, r=r, q=q, p2=p2, which=0):
                pass
            def mk_mm(which, wsb, pdst):
                def f(E, r=r, q=q, p2=p2, wsb=wsb, pdst=pdst):
                    for c in range(NCHUNK):
                        for n in range(2):
                            ins = E.matmul(pdst[:, ts(n, 512)],
                                           xqsb[p2][:, c * 512 + q * 128: c * 512 + (q + 1) * 128],
                                           wsb[:, c * 4096 + r * 1024 + n * 512:
                                               c * 4096 + r * 1024 + (n + 1) * 512],
                                           start=(c == 0), stop=(c == 7))
                    return ins
                return f
            au.step("tensor", mk_mm(0, wgt_sb, pga),
                    rd=[xqsb[p2][:], (wgt_sb.ap().tensor.name, 0, 1 << 40)], wr=[pga[:]])
            au.step("tensor", mk_mm(1, wup_sb, pup),
                    rd=[xqsb[p2][:], (wup_sb.ap().tensor.name, 0, 1 << 40)], wr=[pup[:]])
            au.step("scalar", lambda E: E.activation(gsil[:], pga[:], AF.Silu,
                    scale=wk[:, 4:5]), rd=[pga[:], wk[:, 4:5]], wr=[gsil[:]])
            au.step("vector", lambda E, r=r: E.scalar_tensor_tensor(
                gbuf[:, ts(r, 1024)], pup[:], wk[:, 5:6], gsil[:],
                ALU.mult, ALU.mult), rd=[pup[:], wk[:, 5:6], gsil[:]],
                wr=[gbuf[:, ts(r, 1024)]])
        au.step("vector", lambda E: E.reduce_max(wk[:, 0:1], gbuf[:], AX.X,
                apply_absolute_value=True), rd=[gbuf[:]], wr=[wk[:, 0:1]])
        au.step("vector", lambda E: E.tensor_scalar(wk[:, 1:2], wk[:, 0:1], 1e-5,
                None, ALU.max), rd=[wk[:, 0:1]], wr=[wk[:, 1:2]])
        au.step("vector", lambda E: E.reciprocal(wk[:, 2:3], wk[:, 1:2]),
                rd=[wk[:, 1:2]], wr=[wk[:, 2:3]])
        au.step("vector", lambda E: E.tensor_scalar(wk[:, 3:4], wk[:, 2:3], 127.0,
                None, ALU.mult), rd=[wk[:, 2:3]], wr=[wk[:, 3:4]])
        au.step("vector", lambda E, tt=tt: E.tensor_scalar(s.dq_g[:, tt:tt + 1], wk[:, 1:2],
                s.wmean[:, 2:3], 1.0 / 127, ALU.mult, ALU.mult),
                rd=[wk[:, 1:2], s.wmean[:, 2:3]], wr=[s.dq_g[:, tt:tt + 1]])
        au.step("scalar", lambda E: E.activation(gbuf[:], gbuf[:], AF.Identity,
                scale=wk[:, 3:4], bias=s.constc[:, 2:3]), rd=[gbuf[:], wk[:, 3:4]],
                wr=[gbuf[:]])
        au.step("vector", lambda E: E.tensor_copy(h16[:], gbuf[:]), rd=[gbuf[:]], wr=[h16[:]])
        au.step("vector", lambda E: E.tensor_scalar(gbuf[:], h16[:], -1.0, 1536.0,
                ALU.mult, ALU.add), rd=[h16[:]], wr=[gbuf[:]])
        au.step("vector", lambda E: E.tensor_scalar(gb16[:], gbuf[:], -127.0, 128.0,
                ALU.max, ALU.min), rd=[gbuf[:]], wr=[gb16[:]])
        for c in range(32):
            b = c % 2
            au.step("tensor", lambda E, c=c, b=b: E.transpose(ptq[b][:],
                    gb16[:, ts(c, 128)], s.identb[:]), rd=[gb16[:, ts(c, 128)]], wr=[ptq[b][:]])
            au.step("vector" if c % 2 == 0 else "scalar",
                    lambda E, c=c, b=b: (E.tensor_copy if c % 2 == 0 else (lambda o_, i_: E.activation(o_, i_, AF.Identity)))(
                        gstg[:, ts(c, 128)], ptq[b][:]),
                    rd=[ptq[b][:]], wr=[gstg[:, ts(c, 128)]])
        au.step("gpsimd", lambda E, tt=tt: E.dma_start(out=s.GQT[tt], in_=gstg[:]),
                rd=[gstg[:]], dma=True)
    au.emit()
    _ctx.close()


MK.phase_e = _phase_e


def _phase_f(self):
    nc, s = self.nc, self
    NT = self.NT
    au = Auto(self, "phF")
    _ctx = ExitStack()
    st_ = _ctx.enter_context
    wdn_sb = st_(nc.sbuf_tensor("f_wd", [128, 32 * 1024], BF16))
    gqb = [st_(nc.sbuf_tensor(f"f_gq{p}", [128, 4096], BF16)) for p in range(2)]
    y1c = [st_(nc.sbuf_tensor(f"f_y1{p}", [128, D], F32)) for p in range(2)]
    yb = [st_(nc.sbuf_tensor(f"f_yb{p}", [128, D], F32)) for p in range(2)]
    dqc = st_(nc.sbuf_tensor("f_dqc", [128, 1], F32))
    py = [st_(nc.psum_tensor(f"f_py{p}", [128, 512], F32)) for p in range(2)]

    for c in range(32):
        au.step("sync", lambda E, c=c: E.dma_start(out=wdn_sb[:, ts(c, 1024)],
                in_=s.WDN[ts(c, 128), :]), wr=[wdn_sb[:, ts(c, 1024)]], dma=True)
    for tt in range(NT):
        p = tt % 2
        au.step("sync", lambda E, tt=tt, p=p: E.dma_start(out=gqb[p][:], in_=s.GQT[tt]),
                wr=[gqb[p][:]], dma=True)
        au.step("sync", lambda E, tt=tt, p=p: E.dma_start(out=y1c[p][:],
                in_=s.Y1[ts(tt, 128), :]), wr=[y1c[p][:]], dma=True)
        for nch in range(2):
            def mmd(E, nch=nch, p=p):
                for c in range(32):
                    ins = E.matmul(py[nch][:], gqb[p][:, ts(c, 128)],
                                   wdn_sb[:, c * 1024 + nch * 512: c * 1024 + (nch + 1) * 512],
                                   start=(c == 0), stop=(c == 31))
                return ins
            au.step("tensor", mmd, rd=[gqb[p][:], (wdn_sb.ap().tensor.name, 0, 1 << 40)],
                    wr=[py[nch][:]])
            au.step("vector", lambda E, nch=nch, p=p, tt=tt: E.scalar_tensor_tensor(
                yb[p][:, ts(nch, 512)], py[nch][:], s.dq_g[:, tt:tt + 1],
                y1c[p][:, ts(nch, 512)], ALU.mult, ALU.add),
                rd=[py[nch][:], s.dq_g[:, tt:tt + 1], y1c[p][:, ts(nch, 512)]],
                wr=[yb[p][:, ts(nch, 512)]])
        au.step("gpsimd", lambda E, tt=tt, p=p: E.dma_start(out=s.y_out[ts(tt, 128), :],
                in_=yb[p][:]), rd=[yb[p][:]], wr=[("YOUT", tt, tt + 1)], dma=True)
    au.step("gpsimd", lambda E: E.memset(dqc[:], 0.0), rd=[("YOUT", 0, 1 << 40)])
    au.emit()
    _ctx.close()


MK.phase_f = _phase_f


def build_full(nseq=2, L=4096, d1_coarse=False):
    m = MK(nseq=nseq, L=L)
    m.d1_coarse = d1_coarse
    m.phase_prep()
    m.barrier()
    m.phase_w()
    m.barrier()
    m.phase_a()
    m.barrier()
    m.phase_b()
    m.barrier()
    m.phase_c()
    m.barrier()
    m.phase_d1()
    m.barrier()
    m.phase_d2()
    m.barrier()
    m.phase_e()
    m.barrier()
    m.phase_f()
    m.stack.close()
    return m


def _phase_a(self):
    nc, s = self.nc, self
    NT = self.NT
    au = Auto(self, "phA")
    _ctx = ExitStack()
    st_ = _ctx.enter_context
    xbuf = [st_(nc.sbuf_tensor(f"a_x{p}", [128, D], F32)) for p in range(2)]
    xlnb = [st_(nc.sbuf_tensor(f"a_ln{p}", [128, D], F32)) for p in range(2)]
    stg = [st_(nc.sbuf_tensor(f"a_st{p}", [128, D], F32)) for p in range(2)]
    spar = [st_(nc.sbuf_tensor(f"a_sp{p}", [128, 8], F32)) for p in range(2)]
    strash = st_(nc.sbuf_tensor("a_trash", [128, D], F32))
    ptr = [st_(nc.psum_tensor(f"a_ptr{p}", [128, 512], F32)) for p in range(4)]
    pbeta = st_(nc.psum_tensor("a_pbeta", [128, 8], F32))

    for i in range(NT):
        p = i % 2
        W = spar[p]
        au.step("sync", lambda E, i=i, p=p: E.dma_start(out=xbuf[p][:],
                in_=s.x_in[ts(i, 128), :]), wr=[xbuf[p][:]], dma=True)
        au.step("scalar", lambda E, p=p, W=W: E.activation(strash[:], xbuf[p][:], AF.Square,
                accum_out=W[:, 0:1]), rd=[xbuf[p][:]], wr=[strash[:], W[:, 0:1]])
        au.step("vector", lambda E, p=p, W=W: E.reduce_sum(W[:, 1:2], xbuf[p][:], AX.X,
                negate=True), rd=[xbuf[p][:]], wr=[W[:, 1:2]])
        au.step("vector", lambda E, W=W: E.tensor_scalar(W[:, 2:3], W[:, 1:2], 1.0 / D,
                None, ALU.mult), rd=[W[:, 1:2]], wr=[W[:, 2:3]])
        au.step("vector", lambda E, W=W: E.tensor_mul(W[:, 3:4], W[:, 2:3], W[:, 2:3]),
                rd=[W[:, 2:3]], wr=[W[:, 3:4]])
        au.step("vector", lambda E, W=W: E.tensor_scalar(W[:, 4:5], W[:, 0:1], 1.0 / D,
                W[:, 3:4], ALU.mult, ALU.subtract), rd=[W[:, 0:1], W[:, 3:4]], wr=[W[:, 4:5]])
        au.step("scalar", lambda E, W=W: E.activation(W[:, 5:6], W[:, 4:5], AF.Sqrt,
                bias=s.constc[:, 0:1]), rd=[W[:, 4:5]], wr=[W[:, 5:6]])
        au.step("vector", lambda E, W=W: E.reciprocal(W[:, 6:7], W[:, 5:6]),
                rd=[W[:, 5:6]], wr=[W[:, 6:7]])
        au.step("vector", lambda E, p=p, W=W: E.tensor_scalar(xlnb[p][:], xbuf[p][:],
                W[:, 2:3], W[:, 6:7], ALU.add, ALU.mult),
                rd=[xbuf[p][:], W[:, 2:3], W[:, 6:7]], wr=[xlnb[p][:]])
        bp = p * 2
        def tr4(E, p=p, bp=bp, half=0):
            for c in range(4):
                ins = E.transpose(ptr[bp + half][:, ts(c, 128)],
                                  xlnb[p][:, ts(half * 4 + c, 128)], s.ident[:])
            return ins
        au.step("tensor", lambda E, p=p, bp=bp: tr4(E, p, bp, 0),
                rd=[xlnb[p][:]], wr=[ptr[bp][:]])
        au.step("tensor", lambda E, p=p, bp=bp: tr4(E, p, bp, 1),
                rd=[xlnb[p][:]], wr=[ptr[bp + 1][:]])
        au.step("vector", lambda E, p=p, bp=bp: E.tensor_copy(stg[p][:, 0:512], ptr[bp][:]),
                rd=[ptr[bp][:]], wr=[stg[p][:, 0:512]])
        au.step("scalar", lambda E, p=p, bp=bp: E.activation(stg[p][:, 512:1024],
                ptr[bp + 1][:], AF.Identity), rd=[ptr[bp + 1][:]], wr=[stg[p][:, 512:1024]])
        def betamm(E, p=p):
            for c in range(NCHUNK):
                E.matmul(pbeta[:], stg[p][:, ts(c, 128)], s.wbres[:, ts(c, 8)],
                         start=(c == 0), stop=False)
            return E.matmul(pbeta[:], s.ones_col[0:1, :], s.bres_row[0:1, :],
                            start=False, stop=True)
        au.step("tensor", betamm, rd=[stg[p][:], s.wbres[:], s.ones_col[:], s.bres_row[:]],
                wr=[pbeta[:]])
        au.step("scalar", lambda E, i=i: E.activation(s.beta_all[:, i * 8:(i + 1) * 8],
                pbeta[:], AF.Sigmoid), rd=[pbeta[:]],
                wr=[s.beta_all[:, i * 8:(i + 1) * 8]])
        for c in range(NCHUNK):
            au.step("gpsimd", lambda E, c=c, i=i, p=p: E.dma_start(
                out=s.XLNT[i // 4][:, c * 512 + (i % 4) * 128: c * 512 + (i % 4 + 1) * 128],
                in_=stg[p][:, ts(c, 128)]), rd=[stg[p][:, ts(c, 128)]], dma=True)
    au.emit()
    _ctx.close()


MK.phase_a = _phase_a


def _pass_proj(self, wname, w_ap, conv_off, K, dst_tm, with_fir,
               fir64_own=(), fir64_jobs=()):
    nc, s = self.nc, self
    NSTR_T = self.nseq * self.NSTR
    SEQ_STRIPES = self.NSTR
    au = Auto(self, f"phB{wname}")
    _ctx = ExitStack()
    st_ = _ctx.enter_context
    wbuf = st_(nc.sbuf_tensor(f"b_w_{wname}", [128, 8192], F32))
    xsb = [st_(nc.sbuf_tensor(f"b_x{p}_{wname}", [128, 4096], F32)) for p in range(2)]
    prebufC = st_(nc.sbuf_tensor(f"b_pre_{wname}", [128, 8 * 516], F32))
    acc8 = [st_(nc.sbuf_tensor(f"b_acc{c}_{wname}", [128, 512], F32)) for c in range(8)]
    tailb = st_(nc.sbuf_tensor(f"b_tail_{wname}", [128, 32], F32))
    stage = st_(nc.sbuf_tensor(f"b_stage_{wname}", [128, 4096], F32))
    ppj = [st_(nc.psum_tensor(f"b_ppj{p}_{wname}", [128, 512], F32)) for p in range(2)]
    ptrb = [st_(nc.psum_tensor(f"b_ptr{p}_{wname}", [128, 512], F32)) for p in range(2)]
    if with_fir:
        vact = st_(nc.sbuf_tensor(f"b_vact_{wname}", [128, 4608], F32))
        facc8 = [st_(nc.sbuf_tensor(f"b_fac{c}_{wname}", [128, 512], F32)) for c in range(8)]
        fstage = st_(nc.sbuf_tensor(f"b_fstage_{wname}", [128, 4096], F32))
        pfir = [st_(nc.psum_tensor(f"b_pf{p}_{wname}", [128, 512], F32)) for p in range(2)]
    else:
        acts = st_(nc.sbuf_tensor(f"b_acts_{wname}", [128, 4096], F32))
        if fir64_jobs:
            vbuf2 = [st_(nc.sbuf_tensor(f"b_vb{p}_{wname}", [128, 4608], F32)) for p in range(2)]
            facc8 = [st_(nc.sbuf_tensor(f"b_fac{c}_{wname}", [128, 512], F32)) for c in range(8)]
            fstage = st_(nc.sbuf_tensor(f"b_fstage_{wname}", [128, 4096], F32))
            pfir = [st_(nc.psum_tensor(f"b_pf{p}_{wname}", [128, 512], F32)) for p in range(2)]

    def emit_fir64(fs, vsrc):
        for j in range(64):
            for c in range(NCHUNK):
                wcol = s.firl[:, c * 64 + j: c * 64 + j + 1]
                base = c * 576 + 1
                rdr = [vsrc[:, base + j: base + j + 512], wcol]
                if j == 0:
                    au.step("vector", lambda E, c=c, wcol=wcol, base=base, vsrc=vsrc:
                            E.tensor_scalar(facc8[c][:], vsrc[:, base: base + 512],
                                            wcol, None, ALU.mult),
                            rd=rdr, wr=[facc8[c][:]])
                else:
                    au.step("vector", lambda E, c=c, wcol=wcol, base=base, j=j, vsrc=vsrc:
                            E.scalar_tensor_tensor(facc8[c][:],
                                                   vsrc[:, base + j: base + j + 512],
                                                   wcol, facc8[c][:], ALU.mult, ALU.add),
                            rd=rdr + [facc8[c][:]], wr=[facc8[c][:]])
        frow0 = (fs // SEQ_STRIPES) * self.L + (fs % SEQ_STRIPES) * 512
        emit_tr_block(lambda c, tt: facc8[c][:, ts(tt, 128)],
                      lambda c: facc8[c][:], fstage, pfir, s.LLTM, frow0)

    def emit_tr_block(srcf, srdf, stg, pbank, dst, row0):
        for c in range(NCHUNK):
            def trp(E, c=c, srcf=srcf):
                for tt in range(4):
                    ins = E.transpose(pbank[c % 2][:, ts(tt, 128)], srcf(c, tt), s.ident[:])
                return ins
            au.step("tensor", trp, rd=[srdf(c)], wr=[pbank[c % 2][:]])
            def evq(E, c=c, stg=stg):
                for tt in range(4):
                    ins = E.activation(stg[:, tt * 1024 + c * 128: tt * 1024 + (c + 1) * 128],
                                       pbank[c % 2][:, ts(tt, 128)], AF.Identity)
                return ins
            au.step("scalar", evq, rd=[pbank[c % 2][:]],
                    wr=[stg[:, tt * 1024 + c * 128: tt * 1024 + (c + 1) * 128]
                        for tt in range(4)])
        for tt in range(4):
            au.step("gpsimd", lambda E, tt=tt, row0=row0, dst=dst, stg=stg: E.dma_start(
                out=dst[row0 + tt * 128: row0 + (tt + 1) * 128, :],
                in_=stg[:, ts(tt, 1024)]), rd=[stg[:]], dma=True)

    jobs = list(fir64_jobs)

    for c in range(NCHUNK):
        au.step("sync", lambda E, c=c: E.dma_start(out=wbuf[:, ts(c, 1024)],
                in_=w_ap[ts(c, 128), :]), wr=[wbuf[:, ts(c, 1024)]], dma=True)

    for st in range(NSTR_T):
        seq_start = (st % SEQ_STRIPES) == 0
        au.step("sync", lambda E, st=st: E.dma_start(out=xsb[st % 2][:], in_=s.XLNT[st]),
                wr=[xsb[st % 2][:]], dma=True)
        if not with_fir and jobs:
            while jobs and len(jobs) > ((NSTR_T - 1 - st) * len(fir64_jobs)) // NSTR_T:
                fs = jobs.pop(0)
                jp = (len(fir64_jobs) - len(jobs)) % 2
                au.step("sync", lambda E, fs=fs, jp=jp: E.dma_start(out=vbuf2[jp][:],
                        in_=s.VCM[fs]), wr=[vbuf2[jp][:]], dma=True)
                emit_fir64(fs, vbuf2[jp])
        for c in range(NCHUNK):
            def mmp(E, c=c, st=st):
                for dch in range(NCHUNK):
                    ins = E.matmul(ppj[c % 2][:],
                                   wbuf[:, dch * 1024 + c * 128: dch * 1024 + (c + 1) * 128],
                                   xsb[st % 2][:, ts(dch, 512)],
                                   start=(dch == 0), stop=(dch == 7))
                return ins
            au.step("tensor", mmp, rd=[xsb[st % 2][:], (wbuf.ap().tensor.name, 0, 1 << 40)],
                    wr=[ppj[c % 2][:]])
            au.step("scalar", lambda E, c=c: E.activation(
                prebufC[:, c * 516 + 4: c * 516 + 516], ppj[c % 2][:], AF.Identity),
                rd=[ppj[c % 2][:]], wr=[prebufC[:, c * 516 + 4: c * 516 + 516]])
        for c in range(NCHUNK):
            au.step("scalar", lambda E, c=c, seq_start=seq_start: E.activation(
                prebufC[:, c * 516: c * 516 + 4],
                s.zero64[:, 0:4] if seq_start else tailb[:, ts(c, 4)], AF.Identity),
                rd=[] if seq_start else [tailb[:, ts(c, 4)]],
                wr=[prebufC[:, c * 516: c * 516 + 4]])
            au.step("scalar", lambda E, c=c: E.activation(tailb[:, ts(c, 4)],
                    prebufC[:, c * 516 + 512: c * 516 + 516], AF.Identity),
                    rd=[prebufC[:, c * 516 + 512: c * 516 + 516]], wr=[tailb[:, ts(c, 4)]])
        for j in range(K):
            for c in range(NCHUNK):
                wcol = s.convw[:, c * 12 + conv_off + j: c * 12 + conv_off + j + 1]
                lo = c * 516 + 1 + j
                if j == 0:
                    au.step("vector", lambda E, c=c, wcol=wcol, lo=lo: E.tensor_scalar(
                        acc8[c][:], prebufC[:, lo:lo + 512], wcol, None, ALU.mult),
                        rd=[prebufC[:, lo:lo + 512], wcol], wr=[acc8[c][:]])
                else:
                    au.step("vector", lambda E, c=c, wcol=wcol, lo=lo: E.scalar_tensor_tensor(
                        acc8[c][:], prebufC[:, lo:lo + 512], wcol, acc8[c][:],
                        ALU.mult, ALU.add), rd=[prebufC[:, lo:lo + 512], wcol, acc8[c][:]],
                        wr=[acc8[c][:]])
        for c in range(NCHUNK):
            if with_fir:
                au.step("scalar", lambda E, c=c, seq_start=seq_start: E.activation(
                    vact[:, c * 576: c * 576 + 64],
                    s.zero64[:, 0:64] if seq_start else vact[:, c * 576 + 512: c * 576 + 576],
                    AF.Identity),
                    rd=[] if seq_start else [vact[:, c * 576 + 512: c * 576 + 576]],
                    wr=[vact[:, c * 576: c * 576 + 64]])
                au.step("scalar", lambda E, c=c: E.activation(
                    vact[:, c * 576 + 64: (c + 1) * 576], acc8[c][:], AF.Silu),
                    rd=[acc8[c][:]], wr=[vact[:, c * 576 + 64: (c + 1) * 576]])
            else:
                au.step("scalar", lambda E, c=c: E.activation(acts[:, ts(c, 512)],
                        acc8[c][:], AF.Silu), rd=[acc8[c][:]], wr=[acts[:, ts(c, 512)]])
        seq = st // SEQ_STRIPES
        row0 = seq * self.L + (st % SEQ_STRIPES) * 512
        src_of = (lambda c, tt: vact[:, c * 576 + 64 + tt * 128: c * 576 + 64 + (tt + 1) * 128]) \
            if with_fir else (lambda c, tt: acts[:, c * 512 + tt * 128: c * 512 + (tt + 1) * 128])
        srd_of = (lambda c: vact[:, c * 576 + 64: (c + 1) * 576]) if with_fir \
            else (lambda c: acts[:, ts(c, 512)])
        emit_tr_block(src_of, srd_of, stage, ptrb, dst_tm, row0)
        if with_fir:
            au.step("gpsimd", lambda E, st=st: E.dma_start(out=s.VCM[st], in_=vact[:]),
                    rd=[vact[:]], dma=True)
            taps, wtile, wid, dst = 5, s.firs, 5, s.LSTM
            for j in range(taps):
                for c in range(NCHUNK):
                    wcol = wtile[:, c * wid + j: c * wid + j + 1]
                    base = c * 576 + 64 - (taps - 1)
                    rdr = [vact[:, base + j: base + j + 512], wcol]
                    if j == 0:
                        au.step("vector", lambda E, c=c, wcol=wcol, base=base: E.tensor_scalar(
                            facc8[c][:], vact[:, base: base + 512], wcol, None, ALU.mult),
                            rd=rdr, wr=[facc8[c][:]])
                    else:
                        au.step("vector", lambda E, c=c, wcol=wcol, base=base, j=j:
                                E.scalar_tensor_tensor(facc8[c][:],
                                                       vact[:, base + j: base + j + 512],
                                                       wcol, facc8[c][:], ALU.mult, ALU.add),
                                rd=rdr + [facc8[c][:]], wr=[facc8[c][:]])
            emit_tr_block(lambda c, tt: facc8[c][:, ts(tt, 128)],
                          lambda c: facc8[c][:], fstage, pfir, dst, row0)
            if st in fir64_own:
                emit_fir64(st, vact)
    au.emit()
    _ctx.close()


def _phase_b(self):
    n = self.nseq * self.NSTR
    qn = (n * 3) // 8
    kn = (n * 3) // 8
    stripes = list(range(n))
    qjobs = stripes[:qn]
    kjobs = stripes[qn:qn + kn]
    own = stripes[qn + kn:]
    self._pass_proj("v", self.Wv, 8, 4, self.VTM, True, fir64_own=own)
    self.barrier()
    self._pass_proj("q", self.Wq, 0, 4, self.QTM, False, fir64_jobs=qjobs)
    self.barrier()
    self._pass_proj("k", self.Wk, 4, 4, self.KTM, False, fir64_jobs=kjobs)
    self.barrier()


MK._pass_proj = _pass_proj
MK.phase_b = _phase_b




class _AuProxy:
    def __init__(self, st):
        self.step = st

def _phase_c(self):
    nc, s = self.nc, self
    NSTRM = self.nseq * 2
    NCHK = self.NTS
    au = Auto(self, "phC")
    _ctx = ExitStack()
    st_ = _ctx.enter_context
    ST = []
    for k in range(NSTRM):
        d = {}
        d["S"] = [st_(nc.sbuf_tensor(f"c_S{k}_{h}", [128, 256], F32)) for h in range(2)]
        d["qt"] = [st_(nc.sbuf_tensor(f"c_qt{k}{p}", [128, 256], F32)) for p in range(2)]
        d["kt"] = [st_(nc.sbuf_tensor(f"c_kt{k}{p}", [128, 256], F32)) for p in range(2)]
        d["vt"] = [st_(nc.sbuf_tensor(f"c_vt{k}{p}", [128, 256], F32)) for p in range(2)]
        for nm in ("qn", "kn", "vb", "kb", "qnf", "knf", "kbpf", "kbnf", "z", "ost"):
            d[nm] = st_(nc.sbuf_tensor(f"c_{nm}{k}", [128, 256], F32))
        for nm in ("M", "Xa", "Xb", "Y", "at"):
            d[nm] = st_(nc.sbuf_tensor(f"c_{nm}{k}", [128, 128], F32))
        d["sq"] = st_(nc.sbuf_tensor(f"c_sq{k}", [128, 264], F32))
        d["pa"] = st_(nc.psum_tensor(f"c_pa{k}", [128, 512], F32))
        d["pb"] = st_(nc.psum_tensor(f"c_pb{k}", [128, 512], F32))
        ST.append(d)

    NG = 2

    def rows(seq, i):
        r0 = seq * self.L + i * 128
        return slice(r0, r0 + 128)

    def Xj(d, j):
        if j == 0:
            return d["M"]
        return d["Xa"] if j % 2 == 1 else d["Xb"]

    def chunk_steps(g, i, k):
        steps = []
        def _st(eng, fn, rd=(), wr=(), dma=False):
            steps.append((eng, fn, rd, wr, dma))
        au = _AuProxy(_st)
        d = ST[k]
        seq, hh = k // 2, k % 2
        head = g * 2 + hh
        p = i % 2
        cs = slice(head * 256, (head + 1) * 256)
        tt = seq * self.NTS + i
        bcol = s.beta_all[:, tt * 8 + head: tt * 8 + head + 1]
        sq = d["sq"]
        pa, pb = d["pa"], d["pb"]
        au.step("sync", lambda E, d=d, p=p, seq=seq, i=i, cs=cs: E.dma_start(
            out=d["qt"][p][:], in_=s.QTM[rows(seq, i), cs]),
            wr=[d["qt"][p][:]], dma=True)
        au.step("sync", lambda E, d=d, p=p, seq=seq, i=i, cs=cs: E.dma_start(
            out=d["kt"][p][:], in_=s.KTM[rows(seq, i), cs]),
            wr=[d["kt"][p][:]], dma=True)
        au.step("sync", lambda E, d=d, p=p, seq=seq, i=i, cs=cs: E.dma_start(
            out=d["vt"][p][:], in_=s.VTM[rows(seq, i), cs]),
            wr=[d["vt"][p][:]], dma=True)
        au.step("scalar", lambda E, d=d, p=p, sq=sq: E.activation(sq[:, 0:256],
                d["qt"][p][:], AF.Square, accum_out=sq[:, 256:257]),
                rd=[d["qt"][p][:]], wr=[sq[:, 0:256], sq[:, 256:257]])
        au.step("scalar", lambda E, d=d, p=p, sq=sq: E.activation(sq[:, 0:256],
                d["kt"][p][:], AF.Square, accum_out=sq[:, 257:258]),
                rd=[d["kt"][p][:]], wr=[sq[:, 0:256], sq[:, 257:258]])
        au.step("scalar", lambda E, sq=sq: E.activation(sq[:, 258:259],
                sq[:, 256:257], AF.Sqrt, bias=s.constc[:, 1:2]),
                rd=[sq[:, 256:257]], wr=[sq[:, 258:259]])
        au.step("scalar", lambda E, sq=sq: E.activation(sq[:, 259:260],
                sq[:, 257:258], AF.Sqrt, bias=s.constc[:, 1:2]),
                rd=[sq[:, 257:258]], wr=[sq[:, 259:260]])
        au.step("vector", lambda E, sq=sq: E.reciprocal(sq[:, 260:261],
                sq[:, 258:259]), rd=[sq[:, 258:259]], wr=[sq[:, 260:261]])
        au.step("vector", lambda E, sq=sq: E.reciprocal(sq[:, 261:262],
                sq[:, 259:260]), rd=[sq[:, 259:260]], wr=[sq[:, 261:262]])
        au.step("vector", lambda E, d=d, p=p, sq=sq: E.tensor_scalar(d["qn"][:],
                d["qt"][p][:], sq[:, 260:261], None, ALU.mult),
                rd=[d["qt"][p][:], sq[:, 260:261]], wr=[d["qn"][:]])
        au.step("vector", lambda E, d=d, p=p, sq=sq: E.tensor_scalar(d["kn"][:],
                d["kt"][p][:], sq[:, 261:262], None, ALU.mult),
                rd=[d["kt"][p][:], sq[:, 261:262]], wr=[d["kn"][:]])
        au.step("vector", lambda E, d=d, bcol=bcol: E.tensor_scalar(d["kb"][:],
                d["kn"][:], bcol, None, ALU.mult),
                rd=[d["kn"][:], bcol], wr=[d["kb"][:]])
        au.step("vector", lambda E, d=d, p=p, bcol=bcol: E.tensor_scalar(d["vb"][:],
                d["vt"][p][:], bcol, None, ALU.mult),
                rd=[d["vt"][p][:], bcol], wr=[d["vb"][:]])
        def tr_qk(E, d=d):
            for cch in range(2):
                E.transpose(d["pa"][:, cch * 128:(cch + 1) * 128],
                            d["qn"][:, ts(cch, 128)], s.ident[:])
            for cch in range(2):
                ins = E.transpose(d["pa"][:, 256 + cch * 128: 384 + cch * 128],
                                  d["kn"][:, ts(cch, 128)], s.ident[:])
            return ins
        au.step("tensor", tr_qk, rd=[d["qn"][:], d["kn"][:]], wr=[pa[:]])
        def tr_kb(E, d=d):
            E.transpose(d["pb"][:, 0:128], d["kb"][:, 0:128], s.ident[:])
            return E.transpose(d["pb"][:, 128:256], d["kb"][:, 128:256], s.ident[:])
        au.step("tensor", tr_kb, rd=[d["kb"][:]], wr=[pb[:]])
        au.step("vector", lambda E, d=d: E.tensor_copy(d["qnf"][:], d["pa"][:, 0:256]),
                rd=[pa[:]], wr=[d["qnf"][:]])
        au.step("vector", lambda E, d=d: E.tensor_copy(d["knf"][:], d["pa"][:, 256:512]),
                rd=[pa[:]], wr=[d["knf"][:]])
        au.step("scalar", lambda E, d=d: E.activation(d["kbpf"][:], d["pb"][:, 0:256],
                AF.Identity), rd=[pb[:]], wr=[d["kbpf"][:]])
        au.step("scalar", lambda E, d=d: E.activation(d["kbnf"][:], d["pb"][:, 0:256],
                AF.Identity, scale=s.constc[:, 3:4]), rd=[pb[:]], wr=[d["kbnf"][:]])
        def mm_m(E, d=d):
            for cch in range(2):
                ins = E.matmul(d["pa"][:, 0:128], d["knf"][:, ts(cch, 128)],
                               d["kbpf"][:, ts(cch, 128)],
                               start=(cch == 0), stop=(cch == 1))
            return ins
        au.step("tensor", mm_m, rd=[d["knf"][:], d["kbpf"][:]], wr=[pa[:]])
        au.step("vector", lambda E, d=d: E.tensor_mul(d["M"][:], d["pa"][:, 0:128],
                s.maskMn[:]), rd=[pa[:], s.maskMn[:]], wr=[d["M"][:]])
        for kstep in range(8):
            if kstep == 0:
                if i == 0:
                    au.step("tensor", lambda E, d=d: E.matmul(d["pb"][:, 0:256],
                            s.ident[:], d["vb"][:], start=True, stop=True),
                            rd=[s.ident[:], d["vb"][:]], wr=[pb[:]])
                else:
                    def mm_z0(E, d=d):
                        E.matmul(d["pb"][:, 0:256], s.ident[:], d["vb"][:],
                                 start=True, stop=False)
                        E.matmul(d["pb"][:, 0:256], d["kbnf"][:, 0:128],
                                 d["S"][0][:], start=False, stop=False)
                        return E.matmul(d["pb"][:, 0:256], d["kbnf"][:, 128:256],
                                        d["S"][1][:], start=False, stop=True)
                    au.step("tensor", mm_z0,
                            rd=[s.ident[:], d["vb"][:], d["kbnf"][:],
                                d["S"][0][:], d["S"][1][:]], wr=[pb[:]])
                au.step("vector", lambda E, d=d: E.tensor_copy(d["z"][:],
                        d["pb"][:, 0:256]), rd=[pb[:]], wr=[d["z"][:]])
            else:
                X = Xj(d, kstep - 1)
                au.step("tensor", lambda E, d=d, X=X: E.matmul(d["pb"][:, 0:256],
                        X[:], d["z"][:], start=True, stop=True),
                        rd=[X[:], d["z"][:]], wr=[pb[:]])
                au.step("vector", lambda E, d=d: E.tensor_add(d["z"][:], d["z"][:],
                        d["pb"][:, 0:256]), rd=[pb[:], d["z"][:]], wr=[d["z"][:]])
            if kstep < 6:
                j2 = kstep + 1
                Xp, Xn = Xj(d, j2 - 1), Xj(d, j2)
                au.step("tensor", lambda E, d=d, Xp=Xp: E.transpose(
                        d["pa"][:, 128:256], Xp[:], s.ident[:]),
                        rd=[Xp[:]], wr=[pa[:]])
                au.step("vector", lambda E, d=d: E.tensor_copy(d["Y"][:],
                        d["pa"][:, 128:256]), rd=[pa[:]], wr=[d["Y"][:]])
                au.step("tensor", lambda E, d=d, Xp=Xp: E.matmul(d["pa"][:, 0:128],
                        d["Y"][:], Xp[:], start=True, stop=True),
                        rd=[d["Y"][:], Xp[:]], wr=[pa[:]])
                au.step("vector", lambda E, d=d, Xn=Xn: E.tensor_copy(Xn[:],
                        d["pa"][:, 0:128]), rd=[pa[:]], wr=[Xn[:]])
        def mm_a(E, d=d):
            for cch in range(2):
                ins = E.matmul(d["pa"][:, 0:128], d["knf"][:, ts(cch, 128)],
                               d["qnf"][:, ts(cch, 128)],
                               start=(cch == 0), stop=(cch == 1))
            return ins
        au.step("tensor", mm_a, rd=[d["knf"][:], d["qnf"][:]], wr=[pa[:]])
        au.step("vector", lambda E, d=d: E.tensor_mul(d["at"][:], d["pa"][:, 0:128],
                s.maskUI[:]), rd=[pa[:], s.maskUI[:]], wr=[d["at"][:]])
        if i == 0:
            au.step("tensor", lambda E, d=d: E.matmul(d["pb"][:, 256:512],
                    d["at"][:], d["z"][:], start=True, stop=True),
                    rd=[d["at"][:], d["z"][:]], wr=[pb[:]])
        else:
            def mm_o(E, d=d):
                E.matmul(d["pb"][:, 256:512], d["qnf"][:, 0:128], d["S"][0][:],
                         start=True, stop=False)
                E.matmul(d["pb"][:, 256:512], d["qnf"][:, 128:256], d["S"][1][:],
                         start=False, stop=False)
                return E.matmul(d["pb"][:, 256:512], d["at"][:], d["z"][:],
                                start=False, stop=True)
            au.step("tensor", mm_o, rd=[d["qnf"][:], d["S"][0][:], d["S"][1][:],
                    d["at"][:], d["z"][:]], wr=[pb[:]])
        au.step("scalar", lambda E, d=d: E.activation(d["ost"][:], d["pb"][:, 256:512],
                AF.Identity), rd=[pb[:]], wr=[d["ost"][:]])
        def mm_su(E, d=d):
            E.matmul(d["pa"][:, 0:256], d["kn"][:, 0:128], d["z"][:],
                     start=True, stop=True)
            return E.matmul(d["pa"][:, 256:512], d["kn"][:, 128:256], d["z"][:],
                            start=True, stop=True)
        au.step("tensor", mm_su, rd=[d["kn"][:], d["z"][:]], wr=[pa[:]])
        if i == 0:
            au.step("vector", lambda E, d=d: E.tensor_copy(d["S"][0][:],
                    d["pa"][:, 0:256]), rd=[pa[:]], wr=[d["S"][0][:]])
            au.step("vector", lambda E, d=d: E.tensor_copy(d["S"][1][:],
                    d["pa"][:, 256:512]), rd=[pa[:]], wr=[d["S"][1][:]])
        else:
            au.step("vector", lambda E, d=d: E.tensor_add(d["S"][0][:], d["S"][0][:],
                    d["pa"][:, 0:256]), rd=[pa[:], d["S"][0][:]], wr=[d["S"][0][:]])
            au.step("vector", lambda E, d=d: E.tensor_add(d["S"][1][:], d["S"][1][:],
                    d["pa"][:, 256:512]), rd=[pa[:], d["S"][1][:]], wr=[d["S"][1][:]])
        au.step("gpsimd", lambda E, d=d, seq=seq, i=i, head=head: E.dma_start(
                out=s.DOTM[rows(seq, i), head * 256:(head + 1) * 256],
                in_=d["ost"][:]), rd=[d["ost"][:]], dma=True)
        return steps

    for g in range(NG):
        for i in range(NCHK):
            gens = [chunk_steps(g, i, k) for k in range(NSTRM)]
            pos = [0] * NSTRM
            rem = sum(len(g_) for g_ in gens)
            while rem:
                for k in range(NSTRM):
                    if pos[k] < len(gens[k]):
                        eng, fn, rd, wr, dma = gens[k][pos[k]]
                        au.step(eng, fn, rd=rd, wr=wr, dma=dma)
                        pos[k] += 1
                        rem -= 1
    au.emit()
    _ctx.close()


MK.phase_c = _phase_c


def _phase_prep(self):
    nc, s = self.nc, self
    au = Auto(self, "prep")
    _ctx = ExitStack()
    st_ = _ctx.enter_context
    tmp = st_(nc.sbuf_tensor("p_tmp", [128, 8], F32))
    tmp1 = st_(nc.sbuf_tensor("p_tmp1", [1, 32], F32))
    pp = st_(nc.psum_tensor("p_pp", [128, 16], F32))

    def ld(fn, wr_ap):
        au.step("sync", fn, wr=[wr_ap], dma=True)
    for c in range(NCHUNK):
        ld(lambda E, c=c: E.dma_start(out=s.wbres[:, c * 8:c * 8 + 4], in_=s.Wb[ts(c, 128), :]), s.wbres[:, c * 8:c * 8 + 4])
        ld(lambda E, c=c: E.dma_start(out=s.wbres[:, c * 8 + 4:c * 8 + 8], in_=s.Wres[ts(c, 128), :]), s.wbres[:, c * 8 + 4:c * 8 + 8])
        ld(lambda E, c=c: E.dma_start(out=s.firl[:, ts(c, 64)], in_=s.fir_long[ts(c, 128), :]), s.firl[:, ts(c, 64)])
        ld(lambda E, c=c: E.dma_start(out=s.firs[:, ts(c, 5)], in_=s.fir_short[ts(c, 128), :]), s.firs[:, ts(c, 5)])
        ld(lambda E, c=c: E.dma_start(out=s.wg2[:, ts(c, 4)], in_=s.Wg2[ts(c, 128), :]), s.wg2[:, ts(c, 4)])
        ld(lambda E, c=c: E.dma_start(out=s.convw[:, c * 12:c * 12 + 4], in_=s.wq_conv[ts(c, 128), :]), s.convw[:, c * 12:c * 12 + 4])
        ld(lambda E, c=c: E.dma_start(out=s.convw[:, c * 12 + 4:c * 12 + 8], in_=s.wk_conv[ts(c, 128), :]), s.convw[:, c * 12 + 4:c * 12 + 8])
        ld(lambda E, c=c: E.dma_start(out=s.convw[:, c * 12 + 8:c * 12 + 12], in_=s.wv_conv[ts(c, 128), :]), s.convw[:, c * 12 + 8:c * 12 + 12])
        ld(lambda E, c=c: E.dma_start(out=s.bg1c[:, c:c + 1], in_=s.bg1[ts(c, 128)].unsqueeze(1)), s.bg1c[:, c:c + 1])
        off = (c * 128) % 256
        ld(lambda E, c=c, off=off: E.dma_start(out=s.nwch[:, c:c + 1], in_=s.norm_w[off:off + 128].unsqueeze(1)), s.nwch[:, c:c + 1])
    for c in range(4):
        ld(lambda E, c=c: E.dma_start(out=s.wog2[:, c:c + 1], in_=s.Wog2[ts(c, 128), :]), s.wog2[:, c:c + 1])
        ld(lambda E, c=c: E.dma_start(out=s.bog1c[:, c:c + 1], in_=s.bog1[ts(c, 128)].unsqueeze(1)), s.bog1c[:, c:c + 1])
    ld(lambda E: E.dma_start(out=s.bres_row[0:1, 4:8], in_=s.bres.unsqueeze(0)), s.bres_row[0:1, 4:8])
    ld(lambda E: E.dma_start(out=s.bg2f[0:4, :], in_=s.bg2.unsqueeze(1)), s.bg2f[:])
    ld(lambda E: E.dma_start(out=s.bog2v[0:1, :], in_=s.bog2.unsqueeze(0)), s.bog2v[:])
    ld(lambda E: E.dma_start(out=tmp1[0:1, 0:1], in_=s.logit_temp.unsqueeze(0)), tmp1[0:1, 0:1])
    ld(lambda E: E.dma_start(out=tmp1[0:1, 8:12], in_=s.conv_res_logit.unsqueeze(0)), tmp1[0:1, 8:12])

    G = lambda fn, rd_, wr_: au.step("gpsimd", fn, rd=rd_, wr=wr_)
    G(lambda E: E.memset(s.ident[:], 0.0), [], [s.ident[:]])
    G(lambda E: E.memset(s.maskMn[:], 0.0), [], [s.maskMn[:]])
    G(lambda E: E.memset(s.maskUI[:], 0.0), [], [s.maskUI[:]])
    G(lambda E: E.memset(s.ones_col[:], 1.0), [], [s.ones_col[:]])
    G(lambda E: E.memset(s.zero64[:], 0.0), [], [s.zero64[:]])
    G(lambda E: E.memset(s.bres_row[0:1, 0:4], 0.0), [], [s.bres_row[0:1, 0:4]])
    G(lambda E: E.memset(s.constc[:, 0:1], 1e-5), [], [s.constc[:, 0:1]])
    G(lambda E: E.memset(s.constc[:, 1:2], 1e-6), [], [s.constc[:, 1:2]])
    G(lambda E: E.memset(s.constc[:, 2:3], 1536.0), [], [s.constc[:, 2:3]])
    G(lambda E: E.memset(s.constc[:, 3:4], -1.0), [], [s.constc[:, 3:4]])
    G(lambda E: E.memset(s.constc[:, 4:5], 1.0), [], [s.constc[:, 4:5]])
    G(lambda E: E.memset(s.ones128[:], 1.0), [], [s.ones128[:]])
    G(lambda E: E.affine_select(out=s.ident[:], in_=s.ident[:], compare_op=ALU.not_equal,
                                fill=1.0, base=0, pattern=[[-1, 128]], channel_multiplier=1),
      [s.ident[:]], [s.ident[:]])
    G(lambda E: E.affine_select(out=s.maskMn[:], in_=s.maskMn[:], compare_op=ALU.is_ge,
                                fill=-1.0, base=0, pattern=[[-1, 128]], channel_multiplier=1),
      [s.maskMn[:]], [s.maskMn[:]])
    G(lambda E: E.affine_select(out=s.maskUI[:], in_=s.maskUI[:], compare_op=ALU.is_ge,
                                fill=1.0, base=-1, pattern=[[-1, 128]], channel_multiplier=1),
      [s.maskUI[:]], [s.maskUI[:]])
    au.step("vector", lambda E: E.tensor_copy(s.identb[:], s.ident[:]),
            rd=[s.ident[:]], wr=[s.identb[:]])

    au.step("scalar", lambda E: E.activation(tmp1[0:1, 1:2], tmp1[0:1, 0:1], AF.Exp),
            rd=[tmp1[0:1, 0:1]], wr=[tmp1[0:1, 1:2]])
    au.step("scalar", lambda E: E.activation(tmp1[0:1, 16:20], tmp1[0:1, 8:12], AF.Sigmoid),
            rd=[tmp1[0:1, 8:12]], wr=[tmp1[0:1, 16:20]])
    au.step("vector", lambda E: E.tensor_scalar(tmp1[0:1, 20:21], tmp1[0:1, 1:2], 1.0, None,
            ALU.add), rd=[tmp1[0:1, 1:2]], wr=[tmp1[0:1, 20:21]])
    au.step("vector", lambda E: E.tensor_scalar(tmp1[0:1, 2:3], tmp1[0:1, 0:1], 0.0, 0.35,
            ALU.max, ALU.add), rd=[tmp1[0:1, 0:1]], wr=[tmp1[0:1, 2:3]])
    for it in range(7):
        au.step("scalar", lambda E: E.activation(tmp1[0:1, 22:23], tmp1[0:1, 2:3], AF.Exp,
                scale=s.constc[0:1, 3:4]), rd=[tmp1[0:1, 2:3]], wr=[tmp1[0:1, 22:23]])
        au.step("vector", lambda E: E.tensor_scalar(tmp1[0:1, 23:24], tmp1[0:1, 22:23],
                tmp1[0:1, 20:21], -1.0, ALU.mult, ALU.add),
                rd=[tmp1[0:1, 22:23], tmp1[0:1, 20:21]], wr=[tmp1[0:1, 23:24]])
        au.step("vector", lambda E: E.tensor_add(tmp1[0:1, 2:3], tmp1[0:1, 2:3],
                tmp1[0:1, 23:24]), rd=[tmp1[0:1, 2:3], tmp1[0:1, 23:24]], wr=[tmp1[0:1, 2:3]])
    au.step("vector", lambda E: E.reciprocal(tmp1[0:1, 3:4], tmp1[0:1, 2:3]),
            rd=[tmp1[0:1, 2:3]], wr=[tmp1[0:1, 3:4]])
    au.step("tensor", lambda E: E.matmul(pp[:, 0:1], s.ones_col[0:1, :], tmp1[0:1, 3:4],
            start=True, stop=True), rd=[s.ones_col[:], tmp1[0:1, 3:4]], wr=[pp[:]])
    au.step("scalar", lambda E: E.activation(tmp[:, 0:1], pp[:, 0:1], AF.Identity),
            rd=[pp[:]], wr=[tmp[:, 0:1]])
    au.step("tensor", lambda E: E.matmul(pp[:, 4:8], s.ones_col[0:1, :], tmp1[0:1, 16:20],
            start=True, stop=True), rd=[s.ones_col[:], tmp1[0:1, 16:20]], wr=[pp[:]])
    au.step("scalar", lambda E: E.activation(s.statq[:, 0:4], pp[:, 4:8], AF.Identity),
            rd=[pp[:]], wr=[s.statq[:]])
    au.step("vector", lambda E: E.tensor_scalar(s.bg2f[0:4, :], s.bg2f[0:4, :], tmp[0:4, 0:1],
            None, ALU.mult), rd=[s.bg2f[:], tmp[:, 0:1]], wr=[s.bg2f[:]])
    for c in range(NCHUNK):
        au.step("vector", lambda E, c=c: E.tensor_scalar(s.wg2[:, ts(c, 4)], s.wg2[:, ts(c, 4)],
                tmp[:, 0:1], None, ALU.mult), rd=[s.wg2[:, ts(c, 4)], tmp[:, 0:1]],
                wr=[s.wg2[:, ts(c, 4)]])
    au.emit()
    _ctx.close()


MK.phase_prep = _phase_prep
from concourse.bass_utils import run_bass_kernel_spmd

_NC_CACHE = {}


def _get_m():
    if "m" not in _NC_CACHE:
        _NC_CACHE["m"] = build_full(nseq=2, L=4096)
    return _NC_CACHE["m"]


def kernel(**inputs):
    m = _get_m()
    B, L = 16, 4096
    ncore = 8
    x = np.ascontiguousarray(inputs["L0_attn_hidden_states"], dtype=np.float32)
    ren = {
        "Wq": "L0_attn_Wq", "Wk": "L0_attn_Wk", "Wv": "L0_attn_Wv", "Wb": "L0_attn_Wb",
        "wq_conv": "L0_attn_wq_conv", "wk_conv": "L0_attn_wk_conv", "wv_conv": "L0_attn_wv_conv",
        "Wg1": "L0_attn_Wg1", "bg1": "L0_attn_bg1", "Wg2": "L0_attn_Wg2", "bg2": "L0_attn_bg2",
        "logit_temp": "L0_attn_logit_temp", "conv_res_logit": "L0_attn_conv_res_logit",
        "Wres": "L0_attn_Wres", "bres": "L0_attn_bres",
        "Wog1": "L0_attn_Wog1", "bog1": "L0_attn_bog1",
        "Wog2": "L0_attn_Wog2", "bog2": "L0_attn_bog2",
        "norm_w": "L0_attn_norm_w", "Wo": "L0_attn_Wo",
        "w_gate": "L0_ffn_w_gate", "w_up": "L0_ffn_w_up", "w_down": "L0_ffn_w_down",
    }
    shared = {k: np.ascontiguousarray(inputs[v], dtype=np.float32) for k, v in ren.items()}
    shared["fir_long"] = np.ascontiguousarray(
        inputs["L0_attn_fir_long"], dtype=np.float32).reshape(1024, 64)
    shared["fir_short"] = np.ascontiguousarray(
        inputs["L0_attn_fir_short"], dtype=np.float32).reshape(1024, 5)
    in_maps = []
    for c in range(ncore):
        im = dict(shared)
        im["x"] = np.ascontiguousarray(x[2 * c: 2 * c + 2].reshape(2 * L, 1024))
        in_maps.append(im)
    res = run_bass_kernel_spmd(m.nc, in_maps, core_ids=list(range(ncore)))
    out = np.empty((B, L, 1024), np.float32)
    for c in range(ncore):
        out[2 * c: 2 * c + 2] = np.asarray(res.results[c]["y"]).reshape(2, L, 1024)
    return out


# version 19
# speedup vs baseline: 1.0203x; 1.0203x over previous
import numpy as np
import re
from contextlib import ExitStack

import concourse.bass as bass
import concourse.mybir as mybir

F32 = mybir.dt.float32
BF16 = mybir.dt.bfloat16
FP16 = mybir.dt.float16
AF = mybir.ActivationFunctionType
ALU = mybir.AluOpType
AX = mybir.AxisListType

D = 1024
H = 4
DV = 256
NCHUNK = 8


def ts(i, n):
    return slice(i * n, (i + 1) * n)


class MK:
    def __init__(self, nseq=2, L=4096, dbg=False, dbg_out=(), dbg_in=()):
        self.dbg_out = set(dbg_out)
        self.dbg_in = set(dbg_in)
        self.nseq = nseq
        self.L = L
        self.T = nseq * L
        self.NT = self.T // 128
        self.NTS = self.L // 128
        self.dbg = dbg
        assert self.L % 512 == 0
        self.NSTR = self.L // 512

        nc = bass.Bass("TRN2", target_bir_lowering=False, detect_race_conditions=False)
        self.nc = nc
        dt = nc.dram_tensor
        self.x_in = dt("x", [self.T, D], F32, kind="ExternalInput").ap()
        for nm, shp in [("Wq", [D, D]), ("Wk", [D, D]), ("Wv", [D, D]), ("Wb", [D, H]),
                        ("wq_conv", [D, 4]), ("wk_conv", [D, 4]), ("wv_conv", [D, 4]),
                        ("fir_long", [D, 64]), ("fir_short", [D, 5]),
                        ("Wg1", [D + 16, D]), ("bg1", [D]), ("Wg2", [D, H]), ("bg2", [H]),
                        ("logit_temp", [1]), ("conv_res_logit", [H]),
                        ("Wres", [D, H]), ("bres", [H]),
                        ("Wog1", [D + 8, D // 2]), ("bog1", [D // 2]),
                        ("Wog2", [D // 2, 1]), ("bog2", [1]),
                        ("norm_w", [DV]), ("Wo", [D, D]),
                        ("w_gate", [4 * D, D]), ("w_up", [4 * D, D]), ("w_down", [D, 4 * D])]:
            setattr(self, nm, dt(nm, shp, F32, kind="ExternalInput").ap())
        self.y_out = dt("y", [self.T, D], F32, kind="ExternalOutput").ap()

        NT = self.NT
        self.XLNT = dt("XLNT", [NT // 4, 128, 4096], F32, kind=("ExternalInput" if "XLNT" in self.dbg_in else ("ExternalOutput" if "XLNT" in self.dbg_out else "Internal"))).ap()
        self.QTM = dt("QTM", [self.T, D], F32, kind=("ExternalInput" if "QTM" in self.dbg_in else ("ExternalOutput" if "QTM" in self.dbg_out else "Internal"))).ap()
        self.KTM = dt("KTM", [self.T, D], F32, kind=("ExternalInput" if "KTM" in self.dbg_in else ("ExternalOutput" if "KTM" in self.dbg_out else "Internal"))).ap()
        self.VTM = dt("VTM", [self.T, D], F32, kind=("ExternalInput" if "VTM" in self.dbg_in else ("ExternalOutput" if "VTM" in self.dbg_out else "Internal"))).ap()
        self.LSTM = dt("LSTM", [self.T, D], F32, kind=("ExternalInput" if "LSTM" in self.dbg_in else ("ExternalOutput" if "LSTM" in self.dbg_out else "Internal"))).ap()
        self.LLTM = dt("LLTM", [self.T, D], F32, kind=("ExternalInput" if "LLTM" in self.dbg_in else ("ExternalOutput" if "LLTM" in self.dbg_out else "Internal"))).ap()
        self.VCM = dt("VCM", [self.NT // 4, 128, 4608], F32, kind="Internal").ap()
        self.LLCM = dt("LLCM", [self.NT // 4, 128, 4096], F32, kind=("ExternalOutput" if "LLCM" in self.dbg_out else "Internal")).ap()
        self.DOTM = dt("DOTM", [self.T, D], F32, kind=("ExternalInput" if "DOTM" in self.dbg_in else ("ExternalOutput" if "DOTM" in self.dbg_out else "Internal"))).ap()
        self.Y1 = dt("Y1", [self.T, D], F32, kind=("ExternalInput" if "Y1" in self.dbg_in else ("ExternalOutput" if "Y1" in self.dbg_out else "Internal"))).ap()
        self.XQT = dt("XQT", [NT // 4, 128, 4096], BF16, kind=("ExternalInput" if "XQT" in self.dbg_in else ("ExternalOutput" if "XQT" in self.dbg_out else "Internal"))).ap()
        self.GQT = dt("GQT", [NT, 128, 4096], BF16, kind=("ExternalInput" if "GQT" in self.dbg_in else ("ExternalOutput" if "GQT" in self.dbg_out else "Internal"))).ap()
        self.WGT = dt("WGT", [D, 4 * D], BF16, kind=("ExternalInput" if "WGT" in self.dbg_in else ("ExternalOutput" if "WGT" in self.dbg_out else "Internal"))).ap()
        self.WUP = dt("WUP", [D, 4 * D], BF16, kind=("ExternalInput" if "WUP" in self.dbg_in else ("ExternalOutput" if "WUP" in self.dbg_out else "Internal"))).ap()
        self.WDN = dt("WDN", [4 * D, D], BF16, kind=("ExternalInput" if "WDN" in self.dbg_in else ("ExternalOutput" if "WDN" in self.dbg_out else "Internal"))).ap()

        self.stack = ExitStack()
        self._sems = {}
        self._qcounts = {}
        self._persist()

    def sb(self, name, shape, dtype=F32):
        return self.stack.enter_context(self.nc.sbuf_tensor(name, shape, dtype))

    def ps(self, name, shape, dtype=F32):
        return self.stack.enter_context(self.nc.psum_tensor(name, shape, dtype))

    def sem(self, name):
        return self.stack.enter_context(self.nc.semaphore(name))

    def _persist(self):
        s = self.sb
        self.ident = s("ident", [128, 128], F32)
        self.maskMn = s("maskMn", [128, 128], F32)
        self.maskUI = s("maskUI", [128, 128], F32)
        self.ones_col = s("ones_col", [1, 128], F32)
        self.small = s("small", [128, 64], F32)
        self.wbres = s("wbres", [128, 8 * NCHUNK], F32)
        self.bres_row = s("bres_row", [1, 8], F32)
        self.convw = s("convw", [128, 12 * NCHUNK], F32)
        self.firl = s("firl", [128, 64 * NCHUNK], F32)
        self.firs = s("firs", [128, 5 * NCHUNK], F32)
        self.wg2 = s("wg2", [128, 4 * NCHUNK], F32)
        self.bg2f = s("bg2f", [4, 1], F32)
        self.wog2 = s("wog2", [128, 1 * 4], F32)
        self.bog2v = s("bog2v", [1, 1], F32)
        self.statq = s("statq", [128, 4], F32)
        self.nwch = s("nwch", [128, NCHUNK], F32)
        self.bg1c = s("bg1c", [128, NCHUNK], F32)
        self.bog1c = s("bog1c", [128, 4], F32)
        self.beta_all = s("beta_all", [128, 8 * self.NT], F32)
        self.dq_x = s("dq_x", [128, self.NT], F32)
        self.dq_g = s("dq_g", [128, self.NT], F32)
        self.wmean = s("wmean", [128, 4], F32)
        self.wscl = s("wscl", [128, 4], F32)
        self.ones128 = s("ones128", [128, 1], F32)
        self.zero64 = s("zero64", [128, 64], F32)
        self.constc = s("constc", [128, 8], F32)
        self.identb = s("identb", [128, 128], BF16)

    def phase_prep(self):
        nc, s = self.nc, self
        with ExitStack() as ctx:
            e = ctx.enter_context
            ld = e(nc.semaphore("p_ld"))
            gp = e(nc.semaphore("p_gp"))
            ve = e(nc.semaphore("p_ve"))
            mm = e(nc.semaphore("p_mm"))
            sc = e(nc.semaphore("p_sc"))
            tmp = e(nc.sbuf_tensor("p_tmp", [128, 8], F32))
            tmp1 = e(nc.sbuf_tensor("p_tmp1", [1, 32], F32))
            pp = e(nc.psum_tensor("p_pp", [128, 16], F32))
            blk = e(nc.Block("prep"))

            NLD = 10 * NCHUNK + 2 * 4 + 5

            @blk.sync
            def _(sync):
                for c in range(NCHUNK):
                    sync.dma_start(out=s.wbres[:, c * 8:c * 8 + 4], in_=s.Wb[ts(c, 128), :]).then_inc(ld, 16)
                    sync.dma_start(out=s.wbres[:, c * 8 + 4:c * 8 + 8], in_=s.Wres[ts(c, 128), :]).then_inc(ld, 16)
                    sync.dma_start(out=s.firl[:, ts(c, 64)], in_=s.fir_long[ts(c, 128), :]).then_inc(ld, 16)
                    sync.dma_start(out=s.firs[:, ts(c, 5)], in_=s.fir_short[ts(c, 128), :]).then_inc(ld, 16)
                    sync.dma_start(out=s.wg2[:, ts(c, 4)], in_=s.Wg2[ts(c, 128), :]).then_inc(ld, 16)
                    sync.dma_start(out=s.convw[:, c * 12:c * 12 + 4], in_=s.wq_conv[ts(c, 128), :]).then_inc(ld, 16)
                    sync.dma_start(out=s.convw[:, c * 12 + 4:c * 12 + 8], in_=s.wk_conv[ts(c, 128), :]).then_inc(ld, 16)
                    sync.dma_start(out=s.convw[:, c * 12 + 8:c * 12 + 12], in_=s.wv_conv[ts(c, 128), :]).then_inc(ld, 16)
                    sync.dma_start(out=s.bg1c[:, c:c + 1], in_=s.bg1[ts(c, 128)].unsqueeze(1)).then_inc(ld, 16)
                    off = (c * 128) % 256
                    sync.dma_start(out=s.nwch[:, c:c + 1], in_=s.norm_w[off:off + 128].unsqueeze(1)).then_inc(ld, 16)
                for c in range(4):
                    sync.dma_start(out=s.wog2[:, c:c + 1], in_=s.Wog2[ts(c, 128), :]).then_inc(ld, 16)
                    sync.dma_start(out=s.bog1c[:, c:c + 1], in_=s.bog1[ts(c, 128)].unsqueeze(1)).then_inc(ld, 16)
                sync.dma_start(out=s.bres_row[0:1, 4:8], in_=s.bres.unsqueeze(0)).then_inc(ld, 16)
                sync.dma_start(out=s.bg2f[0:4, :], in_=s.bg2.unsqueeze(1)).then_inc(ld, 16)
                sync.dma_start(out=s.bog2v[0:1, :], in_=s.bog2.unsqueeze(0)).then_inc(ld, 16)
                sync.dma_start(out=tmp1[0:1, 0:1], in_=s.logit_temp.unsqueeze(0)).then_inc(ld, 16)
                sync.dma_start(out=tmp1[0:1, 8:12], in_=s.conv_res_logit.unsqueeze(0)).then_inc(ld, 16)

            @blk.gpsimd
            def _(gpsimd):
                gpsimd.memset(s.ident[:], 0.0).then_inc(gp, 1)
                gpsimd.memset(s.maskMn[:], 0.0).then_inc(gp, 1)
                gpsimd.memset(s.maskUI[:], 0.0).then_inc(gp, 1)
                gpsimd.memset(s.ones_col[:], 1.0).then_inc(gp, 1)
                gpsimd.memset(s.zero64[:], 0.0).then_inc(gp, 1)
                gpsimd.memset(s.bres_row[0:1, 0:4], 0.0).then_inc(gp, 1)
                gpsimd.memset(s.constc[:, 0:1], 1e-5).then_inc(gp, 1)
                gpsimd.memset(s.constc[:, 1:2], 1e-6).then_inc(gp, 1)
                gpsimd.memset(s.constc[:, 2:3], 1536.0).then_inc(gp, 1)
                gpsimd.memset(s.constc[:, 3:4], -1.0).then_inc(gp, 1)
                gpsimd.memset(s.constc[:, 4:5], 1.0).then_inc(gp, 1)
                gpsimd.memset(s.ones128[:], 1.0).then_inc(gp, 1)
                gpsimd.wait_ge(gp, 12)
                gpsimd.affine_select(out=s.ident[:], in_=s.ident[:], compare_op=ALU.not_equal,
                                     fill=1.0, base=0, pattern=[[-1, 128]],
                                     channel_multiplier=1).then_inc(gp, 1)
                gpsimd.affine_select(out=s.maskMn[:], in_=s.maskMn[:], compare_op=ALU.is_ge,
                                     fill=-1.0, base=0, pattern=[[-1, 128]],
                                     channel_multiplier=1).then_inc(gp, 1)
                gpsimd.affine_select(out=s.maskUI[:], in_=s.maskUI[:], compare_op=ALU.is_ge,
                                     fill=1.0, base=-1, pattern=[[-1, 128]],
                                     channel_multiplier=1).then_inc(gp, 1)

            @blk.scalar
            def _(scalar):
                scalar.wait_ge(ld, 16 * NLD)
                scalar.wait_ge(gp, 15)
                scalar.activation(tmp1[0:1, 1:2], tmp1[0:1, 0:1], AF.Exp)
                scalar.activation(tmp1[0:1, 16:20], tmp1[0:1, 8:12], AF.Sigmoid).then_inc(sc, 1)
                for it in range(7):
                    scalar.wait_ge(ve, it + 1)
                    scalar.activation(tmp1[0:1, 22:23], tmp1[0:1, 2:3], AF.Exp,
                                      scale=s.constc[0:1, 3:4]).then_inc(sc, 1)
                scalar.wait_ge(mm, 1)
                scalar.activation(tmp[:, 0:1], pp[:, 0:1], AF.Identity)
                scalar.activation(s.statq[:, 0:4], pp[:, 4:8], AF.Identity).then_inc(sc, 100)

            @blk.vector
            def _(vector):
                vector.wait_ge(gp, 15)
                vector.tensor_copy(s.identb[:], s.ident[:])
                vector.wait_ge(sc, 1)
                vector.tensor_scalar(tmp1[0:1, 20:21], tmp1[0:1, 1:2], 1.0, None, ALU.add)
                vector.tensor_scalar(tmp1[0:1, 2:3], tmp1[0:1, 0:1], 0.0, 0.35,
                                     ALU.max, ALU.add).then_inc(ve, 1)
                for it in range(7):
                    vector.wait_ge(sc, it + 2)
                    vector.tensor_scalar(tmp1[0:1, 23:24], tmp1[0:1, 22:23],
                                         tmp1[0:1, 20:21], -1.0, ALU.mult, ALU.add)
                    vector.tensor_add(tmp1[0:1, 2:3], tmp1[0:1, 2:3],
                                      tmp1[0:1, 23:24]).then_inc(ve, 1)
                vector.reciprocal(tmp1[0:1, 3:4], tmp1[0:1, 2:3]).then_inc(ve, 100)
                vector.wait_ge(sc, 108)
                vector.tensor_scalar(s.bg2f[0:4, :], s.bg2f[0:4, :], tmp[0:4, 0:1], None, ALU.mult)
                for c in range(NCHUNK):
                    vector.tensor_scalar(s.wg2[:, ts(c, 4)], s.wg2[:, ts(c, 4)],
                                         tmp[:, 0:1], None, ALU.mult)
                vector.nop().then_inc(ve, 2)

            @blk.tensor
            def _(tensor):
                tensor.wait_ge(gp, 15)
                tensor.wait_ge(ve, 108)
                tensor.matmul(pp[:, 0:1], s.ones_col[0:1, :], tmp1[0:1, 3:4],
                              start=True, stop=True)
                tensor.matmul(pp[:, 4:8], s.ones_col[0:1, :], tmp1[0:1, 16:20],
                              start=True, stop=True).then_inc(mm, 1)

    def barrier(self):
        self.nc.all_engine_barrier()

class Step:
    __slots__ = ("eng", "fn", "rd", "wr", "dma", "waits", "inc", "qkey")
    def __init__(self, eng, fn, rd, wr, dma):
        self.eng, self.fn, self.rd, self.wr, self.dma = eng, fn, rd, wr, dma


class Auto:
    ENGS = ("tensor", "vector", "scalar", "gpsimd", "sync")

    def __init__(self, m, name, coarse=False):
        self.m = m
        self.nc = m.nc
        self.name = name
        self.steps = []
        self.coarse = coarse

    @staticmethod
    def _regions(aps):
        out = []
        for a in aps:
            if a is None:
                continue
            if isinstance(a, tuple):
                if len(a) == 3:
                    out.append((a[0], 0, 1 << 30, a[1], a[2]))
                else:
                    out.append(a)
                continue
            t = a.tensor
            nm = t.name
            try:
                off = int(a.offset)
                shp = list(a.shape)
            except Exception:
                out.append((nm, 0, 1 << 30, 0, 1 << 60))
                continue
            sp = str(getattr(t, "space", ""))
            if sp.endswith("SBUF") or sp.endswith("PSUM"):
                rowlen = int(np.prod(t.shape[1:])) if len(t.shape) > 1 else 1
                width = int(np.prod(shp[1:])) if len(shp) > 1 else 1
                p_lo, c_lo = off // rowlen, off % rowlen
                p_hi, c_hi = p_lo + int(shp[0]), off % rowlen + width
                if c_hi > rowlen or p_hi > int(t.shape[0]):
                    out.append((nm, 0, 1 << 30, 0, 1 << 60))
                else:
                    out.append((nm, p_lo, p_hi, c_lo, c_hi))
            else:
                out.append((nm, 0, 1 << 30, off, off + int(np.prod(shp))))
        return out

    _PREP_BUFS = {"wbres", "firl", "firs", "wg2", "convw", "bg1c", "nwch", "wog2",
                  "bog1c", "bres_row", "bg2f", "bog2v", "p_tmp1"}

    @classmethod
    def _qkey_map(cls, name):
        if name in cls._PREP_BUFS:
            return "prep"
        m = re.match(r"b_fac\d+_([qkv])$", name)
        if m:
            return "b_fac_" + m.group(1)
        m = re.match(r"c_[qkv]t(\d)(\d)$", name)
        if m:
            return "c_ld" + m.group(1) + m.group(2)
        m = re.match(r"d_(?:ls|ll|do|vd)(\d)$", name)
        if m:
            return "d_tm" + m.group(1)
        return name

    @classmethod
    def _dma_buf(cls, rd, wr):
        for aps in (wr, rd):
            for a in aps:
                if a is None or isinstance(a, tuple):
                    continue
                if str(getattr(a.tensor, "space", "")).endswith("SBUF"):
                    return cls._qkey_map(a.tensor.name)
        return "misc"

    def step(self, eng, fn, rd=(), wr=(), dma=False):
        key = self._dma_buf(rd, wr) if dma else None
        r, w = self._regions(rd), self._regions(wr)
        co = self.coarse
        if co is True or (isinstance(co, (set, frozenset)) and
                          any(nm in co for (nm, *_x) in r + w)):
            r = [(nm, 0, 1 << 30, 0, 1 << 60) for (nm, *_x) in r]
            w = [(nm, 0, 1 << 30, 0, 1 << 60) for (nm, *_x) in w]
        st = Step(eng, fn, r, w, dma)
        st.qkey = key
        self.steps.append(st)

    def emit(self, limit=None):
        nc = self.nc
        if limit is not None:
            self.steps = self.steps[:limit]
        qname = lambda st: ("q_" + st.eng + "_" + st.qkey) if st.dma else st.eng
        counts = self.m._qcounts
        base_counts = dict(counts)
        writers = {}
        readers = {}
        for s_ in self.steps:
            q = qname(s_)
            n = counts.get(q, 0) + (16 if s_.dma else 1)
            counts[q] = n
            evt = (q, n)
            waits = {}

            def need(e2):
                q2, n2 = e2
                if s_.dma and (q2 == q or q2 == s_.eng):
                    return
                if q2.startswith("q_"):
                    n2 = counts.get(q2, 0)
                if waits.get(q2, 0) < n2:
                    waits[q2] = n2

            for (nm, pl, ph, lo, hi) in s_.rd:
                for (pl2, ph2, lo2, hi2, e2) in writers.get(nm, ()):
                    if lo < hi2 and lo2 < hi and pl < ph2 and pl2 < ph:
                        need(e2)
            for (nm, pl, ph, lo, hi) in s_.wr:
                for (pl2, ph2, lo2, hi2, e2) in writers.get(nm, ()):
                    if lo < hi2 and lo2 < hi and pl < ph2 and pl2 < ph:
                        need(e2)
                for (pl2, ph2, lo2, hi2, e2, q2) in readers.get(nm, ()):
                    if lo < hi2 and lo2 < hi and pl < ph2 and pl2 < ph:
                        need(e2)
            s_.waits = waits
            s_.inc = evt
            for (nm, pl, ph, lo, hi) in s_.wr:
                wl = writers.setdefault(nm, [])
                wl[:] = [w for w in wl
                         if not (pl <= w[0] and w[1] <= ph and lo <= w[2] and w[3] <= hi)]
                wl.append((pl, ph, lo, hi, evt))
                rl = readers.setdefault(nm, [])
                rl[:] = [r for r in rl
                         if not (pl <= r[0] and r[1] <= ph and lo <= r[2] and r[3] <= hi)]
            for (nm, pl, ph, lo, hi) in s_.rd:
                rl = readers.setdefault(nm, [])
                rl[:] = [r for r in rl
                         if not (pl <= r[0] and r[1] <= ph and lo <= r[2] and r[3] <= hi
                                 and r[5] == q)]
                rl.append((pl, ph, lo, hi, evt, q))
        with ExitStack() as ctx:
            e = ctx.enter_context
            sems = self.m._sems
            for q in counts:
                if q not in sems:
                    sems[q] = self.m.stack.enter_context(nc.semaphore(f"au_{q}"))
            blk = e(nc.Block(self.name))
            by_eng = {en: [s_ for s_ in self.steps if s_.eng == en] for en in self.ENGS}
            last_waited = {}

            def body(engine, en):
                for s_ in by_eng[en]:
                    for q2, n2 in sorted(s_.waits.items()):
                        if last_waited.get((en, q2), -1) >= n2:
                            continue
                for s_ in []:
                    pass

            def run_body(engine, en):
                for s_ in by_eng[en]:
                    for q2, n2 in sorted(s_.waits.items()):
                        if last_waited.get((en, q2), -1) >= n2:
                            continue
                        engine.wait_ge(sems[q2], n2)
                        last_waited[(en, q2)] = n2
                    ins = s_.fn(engine)
                    assert ins is not None, f"step fn on {en} returned None in {self.name}"
                    ins.then_inc(sems[s_.inc[0]], 16 if s_.dma else 1)
                pref = "q_" + en + "_"
                for qn in counts:
                    if qn.startswith(pref) and counts[qn] != base_counts.get(qn, 0):
                        engine.wait_ge(sems[qn], counts[qn])

            for en in self.ENGS:
                if by_eng[en]:
                    getattr(blk, en)(lambda engine, _en=en: run_body(engine, _en))

    def finish_wait(self):
        pass


def _phase_d1(self):
    nc, s = self.nc, self
    NT = self.NT
    self.O2T = nc.dram_tensor("O2T", [NT // 4, 128, 4096], F32, kind=("ExternalInput" if "O2T" in self.dbg_in else ("ExternalOutput" if "O2T" in self.dbg_out else "Internal"))).ap()
    au = Auto(self, "phD1", coarse=getattr(self, "d1_coarse", False))
    dbg = "D1DBG" in self.dbg_out
    if dbg:
        self.DBG_statT = nc.dram_tensor("DBG_statT", [16, 512], F32, kind="ExternalOutput").ap()
        self.DBG_lgT = nc.dram_tensor("DBG_lgT", [4, 512], F32, kind="ExternalOutput").ap()
        self.DBG_ltm = nc.dram_tensor("DBG_ltm", [4, 128, 24], F32, kind="ExternalOutput").ap()
        self.DBG_ob = nc.dram_tensor("DBG_ob", [4, 128, 1024], F32, kind="ExternalOutput").ap()
        self.DBG_ogsT = nc.dram_tensor("DBG_ogsT", [1, 512], F32, kind="ExternalOutput").ap()
        self.DBG_ogT = nc.dram_tensor("DBG_ogT", [8, 512], F32, kind="ExternalOutput").ap()
    _ctx = ExitStack()
    st_ = _ctx.enter_context
    wg1sb = st_(nc.sbuf_tensor("d_wg1", [128, 9 * 1024], F32))
    wog1sb = st_(nc.sbuf_tensor("d_wog1", [128, 9 * 512], F32))
    xlsb = st_(nc.sbuf_tensor("d_xl", [128, 4096], F32))
    TN = ("ls", "ll", "do", "vd")
    SRC = {"ls": s.LSTM, "ll": s.LLTM, "do": s.DOTM, "vd": s.VTM}
    tmb = {nm: [st_(nc.sbuf_tensor(f"d_{nm}{q}", [128, D], F32)) for q in range(4)]
           for nm in TN}
    stat4 = [st_(nc.sbuf_tensor(f"d_stat{q}", [128, 80], F32)) for q in range(4)]
    trash2 = [st_(nc.sbuf_tensor(f"d_tr{p}", [128, 1024], F32)) for p in range(2)]
    vst = st_(nc.sbuf_tensor("d_vst", [128, 16], F32))
    statT = st_(nc.sbuf_tensor("d_statT", [16, 512], F32))
    ogT = st_(nc.sbuf_tensor("d_ogT", [8, 512], F32))
    g1sb = [st_(nc.sbuf_tensor(f"d_g1{p}", [128, 512], F32)) for p in range(2)]
    ltm = [st_(nc.sbuf_tensor(f"d_ltm{p}", [128, 24], F32)) for p in range(4)]
    lgT = st_(nc.sbuf_tensor("d_lgT", [4, 512], F32))
    ogsT = st_(nc.sbuf_tensor("d_ogsT", [1, 512], F32))
    ob = [st_(nc.sbuf_tensor(f"d_ob{p}", [128, D], F32)) for p in range(4)]
    o2stg = st_(nc.sbuf_tensor("d_o2stg", [128, 4096], F32))
    pst = [st_(nc.psum_tensor(f"d_pst{p}", [128, 512], F32)) for p in range(2)]
    pg = [st_(nc.psum_tensor(f"d_pg{p}", [128, 512], F32)) for p in range(2)]
    plg = st_(nc.psum_tensor("d_plg", [128, 512], F32))
    ptr = [st_(nc.psum_tensor(f"d_ptr{p}", [128, 512], F32)) for p in range(2)]

    for c in range(NCHUNK):
        au.step("sync", lambda E, c=c: E.dma_start(out=wg1sb[:, ts(c, 1024)],
                in_=s.Wg1[ts(c, 128), :]), wr=[wg1sb[:, ts(c, 1024)]], dma=True)
    au.step("sync", lambda E: E.dma_start(out=wg1sb[0:16, 8 * 1024:9 * 1024],
            in_=s.Wg1[1024:1040, :]), wr=[wg1sb[0:16, 8 * 1024:9 * 1024]], dma=True)
    for c in range(NCHUNK):
        au.step("sync", lambda E, c=c: E.dma_start(out=wog1sb[:, ts(c, 512)],
                in_=s.Wog1[ts(c, 128), :]), wr=[wog1sb[:, ts(c, 512)]], dma=True)
    au.step("sync", lambda E: E.dma_start(out=wog1sb[0:8, 8 * 512:9 * 512],
            in_=s.Wog1[1024:1032, :]), wr=[wog1sb[0:8, 8 * 512:9 * 512]], dma=True)

    def stats_into(st, src_ap, W, cols, trash):
        base = bass.AP(src_ap.tensor, src_ap.offset, [[D, 128], [1, D]])
        for h in range(4):
            st("scalar", lambda E, h=h: E.activation(trash[:, ts(h, 256)],
                    base[:, ts(h, 256)], AF.Square, accum_out=W[:, 28 + h:29 + h]),
                    rd=[src_ap], wr=[trash[:, ts(h, 256)], W[:, 28 + h:29 + h]])
            st("vector", lambda E, h=h: E.reduce_sum(W[:, 24 + h:25 + h],
                    base[:, ts(h, 256)], AX.X), rd=[src_ap], wr=[W[:, 24 + h:25 + h]])
        st("vector", lambda E: E.reduce_sum(W[:, 32:33], src_ap, AX.X,
                apply_absolute_value=True), rd=[src_ap], wr=[W[:, 32:33]])
        st("vector", lambda E: E.tensor_scalar(W[:, 33:37], W[:, 24:28], 1.0 / 256, None,
                ALU.mult), rd=[W[:, 24:28]], wr=[W[:, 33:37]])
        st("vector", lambda E: E.reduce_sum(W[:, 37:38], W[:, 33:37], AX.X),
                rd=[W[:, 33:37]], wr=[W[:, 37:38]])
        st("vector", lambda E: E.tensor_scalar(W[:, cols:cols + 1], W[:, 37:38], 0.25,
                None, ALU.mult), rd=[W[:, 37:38]], wr=[W[:, cols:cols + 1]])
        st("vector", lambda E: E.tensor_mul(W[:, 38:42], W[:, 33:37], W[:, 33:37]),
                rd=[W[:, 33:37]], wr=[W[:, 38:42]])
        st("vector", lambda E: E.tensor_scalar(W[:, 42:46], W[:, 28:32], 1.0 / 256, None,
                ALU.mult), rd=[W[:, 28:32]], wr=[W[:, 42:46]])
        st("vector", lambda E: E.tensor_sub(W[:, 42:46], W[:, 42:46], W[:, 38:42]),
                rd=[W[:, 38:42], W[:, 42:46]], wr=[W[:, 42:46]])
        st("vector", lambda E: E.reduce_sum(W[:, 46:47], W[:, 42:46], AX.X),
                rd=[W[:, 42:46]], wr=[W[:, 46:47]])
        st("vector", lambda E: E.tensor_scalar(W[:, cols + 1:cols + 2], W[:, 46:47], 0.25,
                None, ALU.mult), rd=[W[:, 46:47]], wr=[W[:, cols + 1:cols + 2]])
        st("vector", lambda E: E.tensor_scalar(W[:, cols + 2:cols + 3], W[:, 32:33],
                1.0 / 1024, None, ALU.mult), rd=[W[:, 32:33]], wr=[W[:, cols + 2:cols + 3]])
        st("scalar", lambda E: E.activation(W[:, 48:52], W[:, 28:32], AF.Sqrt),
                rd=[W[:, 28:32]], wr=[W[:, 48:52]])
        st("vector", lambda E: E.reduce_sum(W[:, 52:53], W[:, 48:52], AX.X),
                rd=[W[:, 48:52]], wr=[W[:, 52:53]])
        st("vector", lambda E: E.tensor_scalar(W[:, cols + 3:cols + 4], W[:, 52:53], 0.25,
                None, ALU.mult), rd=[W[:, 52:53]], wr=[W[:, cols + 3:cols + 4]])

    def rr_merge(gens):
        pos = [0] * len(gens)
        rem = sum(len(g_) for g_ in gens)
        while rem:
            for k in range(len(gens)):
                if pos[k] < len(gens[k]):
                    for (eng, fn, rd, wr, dma) in gens[k][pos[k]]:
                        au.step(eng, fn, rd=rd, wr=wr, dma=dma)
                    pos[k] += 1
                    rem -= 1

    def mk_st():
        groups = []
        def _st(eng, fn, rd=(), wr=(), dma=False, glue=False):
            if glue and groups:
                groups[-1].append((eng, fn, rd, wr, dma))
            else:
                groups.append([(eng, fn, rd, wr, dma)])
        return groups, _st

    def stats_gens(stp):
        gens = []
        for q in range(4):
            tt = stp * 4 + q
            steps, st = mk_st()
            for nm in TN:
                st("sync", lambda E, nm=nm, tt=tt, q=q: E.dma_start(
                    out=tmb[nm][q][:], in_=SRC[nm][ts(tt, 128), :]),
                    wr=[tmb[nm][q][:]], dma=True)
            for j, nm in enumerate(TN):
                stats_into(st, tmb[nm][q][:], stat4[q], j * 4, trash2[q % 2])
            st("vector", lambda E, q=q: E.tensor_copy(vst[:, ts(q, 4)],
                    stat4[q][:, 12:16]), rd=[stat4[q][:, 12:16]], wr=[vst[:, ts(q, 4)]])
            st("tensor", lambda E, q=q: E.transpose(pst[q % 2][0:16, 0:128],
                    stat4[q][:, 0:16], s.ident[:]), rd=[stat4[q][:, 0:16]],
                    wr=[pst[q % 2][:]])
            st("vector", lambda E, q=q: E.tensor_copy(statT[:, ts(q, 128)],
                    pst[q % 2][0:16, 0:128]), rd=[pst[q % 2][:]],
                    wr=[statT[:, ts(q, 128)]], glue=True)
            gens.append(steps)
        return gens

    for stp in range(NT // 4):
        au.step("sync", lambda E, stp=stp: E.dma_start(out=xlsb[:], in_=s.XLNT[stp]),
                wr=[xlsb[:]], dma=True)
        rr_merge(stats_gens(stp))
        for gsl in range(NCHUNK):
            def mmg(E, gsl=gsl):
                for c in range(NCHUNK):
                    E.matmul(pg[gsl % 2][:], wg1sb[:, c * 1024 + gsl * 128: c * 1024 + (gsl + 1) * 128],
                             xlsb[:, ts(c, 512)], start=(c == 0), stop=False)
                return E.matmul(pg[gsl % 2][:], wg1sb[0:16, 8 * 1024 + gsl * 128: 8 * 1024 + (gsl + 1) * 128],
                                statT[:], start=False, stop=True)
            au.step("tensor", mmg, rd=[xlsb[:], statT[:], (wg1sb.ap().tensor.name, 0, 1 << 40)],
                    wr=[pg[gsl % 2][:]])
            au.step("scalar", lambda E, gsl=gsl: E.activation(g1sb[gsl % 2][:], pg[gsl % 2][:],
                    AF.Gelu, bias=s.bg1c[:, gsl:gsl + 1]),
                    rd=[pg[gsl % 2][:]], wr=[g1sb[gsl % 2][:]])
            au.step("tensor", lambda E, gsl=gsl: E.matmul(plg[0:4, :], s.wg2[:, ts(gsl, 4)],
                    g1sb[gsl % 2][:], start=(gsl == 0), stop=(gsl == 7)),
                    rd=[g1sb[gsl % 2][:], s.wg2[:]], wr=[plg[:]])
        au.step("scalar", lambda E: E.activation(lgT[:], plg[0:4, :], AF.Identity,
                bias=s.bg2f[:, 0:1]), rd=[plg[:]], wr=[lgT[:]])
        if dbg and stp == 0:
            au.step("gpsimd", lambda E: E.dma_start(out=self.DBG_statT, in_=statT[:]),
                    rd=[statT[:]], dma=True)
            au.step("gpsimd", lambda E: E.dma_start(out=self.DBG_lgT, in_=lgT[:]),
                    rd=[lgT[:]], dma=True)
        gens = []
        for q in range(4):
            tt = stp * 4 + q
            lt = ltm[q]
            steps, st = mk_st()
            st("tensor", lambda E, q=q: E.transpose(pst[q % 2][0:128, 0:4],
                    lgT[:, ts(q, 128)], s.ident[0:4, 0:4]),
                    rd=[lgT[:, ts(q, 128)]], wr=[pst[q % 2][:]])
            st("vector", lambda E, lt=lt, q=q: E.tensor_copy(lt[:, 0:4],
                    pst[q % 2][0:128, 0:4]), rd=[pst[q % 2][:]], wr=[lt[:, 0:4]], glue=True)
            st("vector", lambda E, lt=lt: E.reduce_max(lt[:, 12:13], lt[:, 0:4], AX.X,
                    negate=True), rd=[lt[:, 0:4]], wr=[lt[:, 12:13]])
            st("scalar", lambda E, lt=lt: E.activation(lt[:, 4:8], lt[:, 0:4], AF.Exp,
                    bias=lt[:, 12:13]), rd=[lt[:, 0:4], lt[:, 12:13]], wr=[lt[:, 4:8]])
            st("vector", lambda E, lt=lt: E.reduce_sum(lt[:, 13:14], lt[:, 4:8], AX.X),
                    rd=[lt[:, 4:8]], wr=[lt[:, 13:14]])
            st("vector", lambda E, lt=lt: E.reciprocal(lt[:, 14:15], lt[:, 13:14]),
                    rd=[lt[:, 13:14]], wr=[lt[:, 14:15]])
            st("vector", lambda E, lt=lt: E.tensor_scalar(lt[:, 4:8], lt[:, 4:8],
                    lt[:, 14:15], 0.02, ALU.mult, ALU.max),
                    rd=[lt[:, 4:8], lt[:, 14:15]], wr=[lt[:, 4:8]])
            st("vector", lambda E, lt=lt: E.reduce_sum(lt[:, 15:16], lt[:, 4:8], AX.X),
                    rd=[lt[:, 4:8]], wr=[lt[:, 15:16]])
            st("vector", lambda E, lt=lt: E.reciprocal(lt[:, 16:17], lt[:, 15:16]),
                    rd=[lt[:, 15:16]], wr=[lt[:, 16:17]])
            st("vector", lambda E, lt=lt: E.tensor_scalar(lt[:, 4:8], lt[:, 4:8],
                    lt[:, 16:17], None, ALU.mult),
                    rd=[lt[:, 4:8], lt[:, 16:17]], wr=[lt[:, 4:8]])
            st("vector", lambda E, lt=lt, tt=tt: E.tensor_mul(lt[:, 8:12], s.statq[:, 0:4],
                    s.beta_all[:, tt * 8 + 4: tt * 8 + 8]),
                    rd=[s.statq[:], s.beta_all[:, tt * 8 + 4: tt * 8 + 8]], wr=[lt[:, 8:12]])
            st("vector", lambda E, lt=lt: E.tensor_scalar(lt[:, 8:12], lt[:, 8:12],
                    lt[:, 4:5], None, ALU.add), rd=[lt[:, 4:5], lt[:, 8:12]], wr=[lt[:, 8:12]])
            for h in range(4):
                st("vector", lambda E, h=h, q=q, lt=lt: E.tensor_scalar(
                    ob[q][:, ts(h, 256)], tmb["ls"][q][:, ts(h, 256)], lt[:, 8 + h: 9 + h],
                    None, ALU.mult), rd=[tmb["ls"][q][:, ts(h, 256)], lt[:, 8 + h: 9 + h]],
                    wr=[ob[q][:, ts(h, 256)]])
            for nm, pc in (("ll", 5), ("do", 6), ("vd", 7)):
                st("vector", lambda E, nm=nm, pc=pc, q=q, lt=lt: E.scalar_tensor_tensor(
                    ob[q][:], tmb[nm][q][:], lt[:, pc:pc + 1], ob[q][:], ALU.mult, ALU.add),
                    rd=[tmb[nm][q][:], lt[:, pc:pc + 1], ob[q][:]], wr=[ob[q][:]])
            if dbg and stp == 0:
                st("gpsimd", lambda E, q=q: E.dma_start(out=self.DBG_ob[q],
                        in_=ob[q][:]), rd=[ob[q][:]], dma=True)
                st("gpsimd", lambda E, q=q: E.dma_start(out=self.DBG_ltm[q],
                        in_=ltm[q][:]), rd=[ltm[q][:]], dma=True)
            stats_into(st, ob[q][:], stat4[q], 16, o2stg[:, ts(q, 1024)])
            st("vector", lambda E, q=q: E.tensor_copy(stat4[q][:, 20:24],
                    vst[:, ts(q, 4)]), rd=[vst[:, ts(q, 4)]], wr=[stat4[q][:, 20:24]])
            st("tensor", lambda E, q=q: E.transpose(pst[q % 2][0:8, 0:128],
                    stat4[q][:, 16:24], s.ident[:]), rd=[stat4[q][:, 16:24]],
                    wr=[pst[q % 2][:]])
            st("vector", lambda E, q=q: E.tensor_copy(ogT[:, ts(q, 128)],
                    pst[q % 2][0:8, 0:128]), rd=[pst[q % 2][:]], wr=[ogT[:, ts(q, 128)]],
                    glue=True)
            gens.append(steps)
        rr_merge(gens)
        for osl in range(4):
            def mmo(E, osl=osl):
                for c in range(NCHUNK):
                    E.matmul(pg[osl % 2][:], wog1sb[:, c * 512 + osl * 128: c * 512 + (osl + 1) * 128],
                             xlsb[:, ts(c, 512)], start=(c == 0), stop=False)
                return E.matmul(pg[osl % 2][:], wog1sb[0:8, 8 * 512 + osl * 128: 8 * 512 + (osl + 1) * 128],
                                ogT[:], start=False, stop=True)
            au.step("tensor", mmo, rd=[xlsb[:], ogT[:], (wog1sb.ap().tensor.name, 0, 1 << 40)],
                    wr=[pg[osl % 2][:]])
            au.step("scalar", lambda E, osl=osl: E.activation(g1sb[osl % 2][:], pg[osl % 2][:],
                    AF.Gelu, bias=s.bog1c[:, osl:osl + 1]),
                    rd=[pg[osl % 2][:]], wr=[g1sb[osl % 2][:]])
            au.step("tensor", lambda E, osl=osl: E.matmul(plg[32:33, :], s.wog2[:, osl:osl + 1],
                    g1sb[osl % 2][:], start=(osl == 0), stop=(osl == 3)),
                    rd=[g1sb[osl % 2][:], s.wog2[:]], wr=[plg[:]])
        au.step("scalar", lambda E: E.activation(ogsT[:], plg[32:33, :], AF.Sigmoid,
                bias=s.bog2v[:, 0:1]), rd=[plg[:]], wr=[ogsT[:]])
        if dbg and stp == 0:
            au.step("gpsimd", lambda E: E.dma_start(out=self.DBG_ogsT, in_=ogsT[:]),
                    rd=[ogsT[:]], dma=True)
            au.step("gpsimd", lambda E: E.dma_start(out=self.DBG_ogT, in_=ogT[:]),
                    rd=[ogT[:]], dma=True)
        gens = []
        for q in range(4):
            lt = ltm[q]
            W = stat4[q]
            steps, st = mk_st()
            st("tensor", lambda E, q=q: E.transpose(pst[q % 2][0:128, 4:5],
                    ogsT[:, ts(q, 128)], s.ident[0:1, 0:1]),
                    rd=[ogsT[:, ts(q, 128)]], wr=[pst[q % 2][:]])
            st("vector", lambda E, lt=lt, q=q: E.tensor_copy(lt[:, 17:18],
                    pst[q % 2][0:128, 4:5]), rd=[pst[q % 2][:]], wr=[lt[:, 17:18]], glue=True)
            st("vector", lambda E, q=q, lt=lt: E.tensor_scalar(ob[q][:], ob[q][:],
                    lt[:, 17:18], None, ALU.mult), rd=[ob[q][:], lt[:, 17:18]], wr=[ob[q][:]])
            for h in range(4):
                st("scalar", lambda E, q=q, h=h, W=W: E.activation(
                    o2stg[:, q * 1024 + h * 256: q * 1024 + (h + 1) * 256],
                    ob[q][:, ts(h, 256)], AF.Square, accum_out=W[:, 64 + h:65 + h]),
                    rd=[ob[q][:, ts(h, 256)]],
                    wr=[o2stg[:, q * 1024 + h * 256: q * 1024 + (h + 1) * 256],
                        W[:, 64 + h:65 + h]])
            st("vector", lambda E, W=W: E.tensor_scalar(W[:, 64:68],
                    W[:, 64:68], 1.0 / 256, None, ALU.mult),
                    rd=[W[:, 64:68]], wr=[W[:, 64:68]])
            st("scalar", lambda E, W=W: E.activation(W[:, 68:72], W[:, 64:68],
                    AF.Sqrt, bias=s.constc[:, 0:1]), rd=[W[:, 64:68]], wr=[W[:, 68:72]])
            st("vector", lambda E, W=W: E.reciprocal(W[:, 72:76], W[:, 68:72]),
                    rd=[W[:, 68:72]], wr=[W[:, 72:76]])
            for h in range(4):
                st("vector", lambda E, h=h, q=q, W=W: E.tensor_scalar(ob[q][:, ts(h, 256)],
                        ob[q][:, ts(h, 256)], W[:, 72 + h: 73 + h], None, ALU.mult),
                        rd=[ob[q][:, ts(h, 256)], W[:, 72 + h: 73 + h]],
                        wr=[ob[q][:, ts(h, 256)]])
            for c in range(NCHUNK):
                b = (q + c) % 2
                st("tensor", lambda E, c=c, q=q, b=b: E.transpose(
                    ptr[b][:, ts(c % 4, 128)], ob[q][:, ts(c, 128)], s.ident[:]),
                    rd=[ob[q][:, ts(c, 128)]], wr=[ptr[b][:]])
                st("vector" if c % 2 == 0 else "scalar",
                        lambda E, c=c, q=q, b=b: (E.tensor_copy if c % 2 == 0 else (lambda o_, i_: E.activation(o_, i_, AF.Identity)))(
                            o2stg[:, c * 512 + q * 128: c * 512 + (q + 1) * 128],
                            ptr[b][:, ts(c % 4, 128)]),
                        rd=[ptr[b][:]], wr=[o2stg[:, c * 512 + q * 128: c * 512 + (q + 1) * 128]],
                        glue=True)
            gens.append(steps)
        rr_merge(gens)
        au.step("gpsimd", lambda E, stp=stp: E.dma_start(out=self.O2T[stp], in_=o2stg[:]),
                rd=[o2stg[:]], dma=True)
    au.emit(limit=getattr(self, "d1_limit", None))
    _ctx.close()


MK.phase_d1 = _phase_d1


def _phase_w(self):
    nc, s = self.nc, self
    au = Auto(self, "phW")
    _ctx = ExitStack()
    st_ = _ctx.enter_context
    wch = [st_(nc.sbuf_tensor(f"w_ch{p}", [128, 4096], F32)) for p in range(2)]
    tern = st_(nc.sbuf_tensor("w_tern", [128, 4096], F32))
    tb = st_(nc.sbuf_tensor("w_tb", [128, 4096], BF16))
    red = st_(nc.sbuf_tensor("w_red", [128, 40], F32))
    sc1 = st_(nc.sbuf_tensor("w_sc1", [1, 8], F32))
    stg = st_(nc.sbuf_tensor("w_stg", [128, 32768], BF16))
    pw = st_(nc.psum_tensor("w_pw", [128, 512], F32))
    pt = [st_(nc.psum_tensor(f"w_pt{p}", [128, 256], BF16)) for p in range(2)]

    WS = [(s.w_gate, s.WGT, 32, 1024, 0), (s.w_up, s.WUP, 32, 1024, 1),
          (s.w_down, s.WDN, 8, 4096, 2)]
    for w_ap, dst, nch, width, widx in WS:
        ncols = width // 1024
        for c in range(nch):
            au.step("sync", lambda E, c=c, w_ap=w_ap, width=width: E.dma_start(
                out=wch[c % 2][:, 0:width], in_=w_ap[ts(c, 128), :]),
                wr=[wch[c % 2][:, 0:width]], dma=True)
            au.step("vector", lambda E, c=c, width=width: E.reduce_sum(
                red[:, c:c + 1], wch[c % 2][:, 0:width], AX.X, apply_absolute_value=True),
                rd=[wch[c % 2][:, 0:width]], wr=[red[:, c:c + 1]])
        au.step("vector", lambda E, nch=nch: E.reduce_sum(red[:, 32:33], red[:, 0:nch], AX.X),
                rd=[red[:, 0:32]], wr=[red[:, 32:33]])
        au.step("tensor", lambda E: E.matmul(pw[0:1, 0:1], red[:, 32:33], s.ones128[:],
                start=True, stop=True), rd=[red[:, 32:33], s.ones128[:]], wr=[pw[:]])
        au.step("vector", lambda E: E.tensor_scalar(sc1[0:1, 0:1], pw[0:1, 0:1],
                1.0 / (4096 * 1024), 1e-5, ALU.mult, ALU.max), rd=[pw[:]], wr=[sc1[0:1, 0:1]])
        au.step("vector", lambda E: E.reciprocal(sc1[0:1, 1:2], sc1[0:1, 0:1]),
                rd=[sc1[0:1, 0:1]], wr=[sc1[0:1, 1:2]])
        au.step("tensor", lambda E: E.matmul(pw[0:128, 1:2], s.ones_col[0:1, :], sc1[0:1, 0:1],
                start=True, stop=True), rd=[sc1[0:1, 0:1], s.ones_col[:]], wr=[pw[:]])
        au.step("tensor", lambda E: E.matmul(pw[0:128, 2:3], s.ones_col[0:1, :], sc1[0:1, 1:2],
                start=True, stop=True), rd=[sc1[0:1, 1:2], s.ones_col[:]], wr=[pw[:]])
        au.step("vector", lambda E, widx=widx: E.tensor_copy(s.wmean[:, widx:widx + 1],
                pw[0:128, 1:2]), rd=[pw[:]], wr=[s.wmean[:, widx:widx + 1]])
        au.step("vector", lambda E, widx=widx: E.tensor_copy(s.wscl[:, widx:widx + 1],
                pw[0:128, 2:3]), rd=[pw[:]], wr=[s.wscl[:, widx:widx + 1]])
        for c in range(nch):
            au.step("sync", lambda E, c=c, w_ap=w_ap, width=width: E.dma_start(
                out=wch[c % 2][:, 0:width], in_=w_ap[ts(c, 128), :]),
                wr=[wch[c % 2][:, 0:width]], dma=True)
            au.step("vector", lambda E, c=c, width=width, widx=widx: E.tensor_scalar(
                tern[:, 0:width], wch[c % 2][:, 0:width], s.wscl[:, widx:widx + 1],
                None, ALU.mult), rd=[wch[c % 2][:, 0:width], s.wscl[:, widx:widx + 1]],
                wr=[tern[:, 0:width]])
            au.step("vector", lambda E, width=width: E.tensor_scalar(
                tb[:, 0:width], tern[:, 0:width], 0.5, None, ALU.is_ge),
                rd=[tern[:, 0:width]], wr=[tb[:, 0:width]])
            au.step("vector", lambda E, width=width: E.tensor_scalar(
                tern[:, 0:width], tern[:, 0:width], -0.5, None, ALU.is_le),
                rd=[tern[:, 0:width]], wr=[tern[:, 0:width]])
            au.step("vector", lambda E, width=width: E.tensor_sub(
                tb[:, 0:width], tern[:, 0:width], tb[:, 0:width]),
                rd=[tern[:, 0:width], tb[:, 0:width]], wr=[tb[:, 0:width]])
            for b in range(width // 128):
                col = b * (nch * 128) + c * 128
                au.step("tensor", lambda E, b=b: E.transpose(pt[b % 2][:, 0:128],
                        tb[:, ts(b, 128)], s.identb[:]), rd=[tb[:, ts(b, 128)]], wr=[pt[b % 2][:]])
                au.step("vector" if b % 2 == 0 else "scalar",
                        lambda E, b=b, col=col: (E.tensor_copy if b % 2 == 0 else (lambda o_, i_: E.activation(o_, i_, AF.Identity)))(
                            stg[:, col:col + 128], pt[b % 2][:, 0:128]),
                        rd=[pt[b % 2][:]], wr=[stg[:, col:col + 128]])
        for b in range(width // 128):
            au.step("gpsimd", lambda E, b=b, dst=dst, nch=nch: E.dma_start(
                out=dst[ts(b, 128), :], in_=stg[:, b * (nch * 128):(b + 1) * (nch * 128)]),
                rd=[stg[:, b * (nch * 128):(b + 1) * (nch * 128)]], dma=True)
    au.emit()
    _ctx.close()


MK.phase_w = _phase_w


def _phase_d2(self):
    nc, s = self.nc, self
    NT = self.NT
    au = Auto(self, "phD2")
    _ctx = ExitStack()
    st_ = _ctx.enter_context
    wo_sb = st_(nc.sbuf_tensor("d2_wo", [128, 8192], F32))
    o2sb = st_(nc.sbuf_tensor("d2_o2", [128, 4096], F32))
    xb = [st_(nc.sbuf_tensor(f"d2_x{p}", [128, D], F32)) for p in range(2)]
    y1b = [st_(nc.sbuf_tensor(f"d2_y1{p}", [128, D], F32)) for p in range(2)]
    xl2 = [st_(nc.sbuf_tensor(f"d2_xl2{p}", [128, D], F32)) for p in range(2)]
    h16 = [st_(nc.sbuf_tensor(f"d2_h16{p}", [128, D], FP16)) for p in range(2)]
    sqt = [st_(nc.sbuf_tensor(f"d2_sqt{p}", [128, D], F32)) for p in range(2)]
    wk = [st_(nc.sbuf_tensor(f"d2_wk{p}", [128, 16], F32)) for p in range(2)]
    xqstg = st_(nc.sbuf_tensor("d2_xqstg", [128, 4096], BF16))
    xq16 = [st_(nc.sbuf_tensor(f"d2_xq16{p}", [128, D], BF16)) for p in range(2)]
    py = [[st_(nc.psum_tensor(f"d2_py{n}{p}", [128, 512], F32)) for p in range(2)]
          for n in range(2)]
    ptq = [st_(nc.psum_tensor(f"d2_ptq{p}", [128, 128], BF16)) for p in range(2)]

    for c in range(NCHUNK):
        au.step("sync", lambda E, c=c: E.dma_start(out=wo_sb[:, ts(c, 1024)],
                in_=s.Wo[ts(c, 128), :]), wr=[wo_sb[:, ts(c, 1024)]], dma=True)
        au.step("vector", lambda E, c=c: E.tensor_scalar(wo_sb[:, ts(c, 1024)],
                wo_sb[:, ts(c, 1024)], s.nwch[:, c:c + 1], None, ALU.mult),
                rd=[wo_sb[:, ts(c, 1024)], s.nwch[:, c:c + 1]], wr=[wo_sb[:, ts(c, 1024)]])

    def mk_st():
        groups = []
        def _st(eng, fn, rd=(), wr=(), dma=False, glue=False):
            if glue and groups:
                groups[-1].append((eng, fn, rd, wr, dma))
            else:
                groups.append([(eng, fn, rd, wr, dma)])
        return groups, _st

    def rr_merge(gens):
        pos = [0] * len(gens)
        rem = sum(len(g_) for g_ in gens)
        while rem:
            for k in range(len(gens)):
                if pos[k] < len(gens[k]):
                    for (eng, fn, rd, wr, dma) in gens[k][pos[k]]:
                        au.step(eng, fn, rd=rd, wr=wr, dma=dma)
                    pos[k] += 1
                    rem -= 1

    def tt_steps(tt):
        p = tt % 2
        q = tt % 4
        steps, st = mk_st()
        st("sync", lambda E, tt=tt, p=p: E.dma_start(out=xb[p][:],
                in_=s.x_in[ts(tt, 128), :]), wr=[xb[p][:]], dma=True)
        for nch in range(2):
            def mmwo(E, nch=nch, q=q, p=p):
                for c in range(NCHUNK):
                    ins = E.matmul(py[nch][p][:], o2sb[:, c * 512 + q * 128: c * 512 + (q + 1) * 128],
                                   wo_sb[:, c * 1024 + nch * 512: c * 1024 + (nch + 1) * 512],
                                   start=(c == 0), stop=(c == 7))
                return ins
            st("tensor", mmwo, rd=[o2sb[:], (wo_sb.ap().tensor.name, 0, 1 << 40)],
                    wr=[py[nch][p][:]])
            st("vector", lambda E, nch=nch, p=p: E.scalar_tensor_tensor(
                y1b[p][:, ts(nch, 512)], py[nch][p][:], 1.0, xb[p][:, ts(nch, 512)],
                ALU.mult, ALU.add), rd=[py[nch][p][:], xb[p][:, ts(nch, 512)]],
                wr=[y1b[p][:, ts(nch, 512)]])
        st("gpsimd", lambda E, tt=tt, p=p: E.dma_start(out=s.Y1[ts(tt, 128), :],
                in_=y1b[p][:]), rd=[y1b[p][:]], dma=True)
        W = wk[p]
        st("scalar", lambda E, p=p, W=W: E.activation(sqt[p][:], y1b[p][:], AF.Square,
                accum_out=W[:, 0:1]), rd=[y1b[p][:]], wr=[sqt[p][:], W[:, 0:1]])
        st("vector", lambda E, p=p, W=W: E.reduce_sum(W[:, 1:2], y1b[p][:], AX.X,
                negate=True), rd=[y1b[p][:]], wr=[W[:, 1:2]])
        st("vector", lambda E, W=W: E.tensor_scalar(W[:, 2:3], W[:, 1:2], 1.0 / D,
                None, ALU.mult), rd=[W[:, 1:2]], wr=[W[:, 2:3]])
        st("vector", lambda E, W=W: E.tensor_mul(W[:, 3:4], W[:, 2:3], W[:, 2:3]),
                rd=[W[:, 2:3]], wr=[W[:, 3:4]])
        st("vector", lambda E, W=W: E.tensor_scalar(W[:, 4:5], W[:, 0:1], 1.0 / D,
                W[:, 3:4], ALU.mult, ALU.subtract), rd=[W[:, 0:1], W[:, 3:4]], wr=[W[:, 4:5]])
        st("scalar", lambda E, W=W: E.activation(W[:, 5:6], W[:, 4:5], AF.Sqrt,
                bias=s.constc[:, 0:1]), rd=[W[:, 4:5]], wr=[W[:, 5:6]])
        st("vector", lambda E, W=W: E.reciprocal(W[:, 6:7], W[:, 5:6]),
                rd=[W[:, 5:6]], wr=[W[:, 6:7]])
        st("vector", lambda E, p=p, W=W: E.tensor_scalar(xl2[p][:], y1b[p][:],
                W[:, 2:3], W[:, 6:7], ALU.add, ALU.mult),
                rd=[y1b[p][:], W[:, 2:3], W[:, 6:7]], wr=[xl2[p][:]])
        st("vector", lambda E, p=p, W=W: E.reduce_max(W[:, 7:8], xl2[p][:], AX.X,
                apply_absolute_value=True), rd=[xl2[p][:]], wr=[W[:, 7:8]])
        st("vector", lambda E, W=W: E.tensor_scalar(W[:, 8:9], W[:, 7:8], 1e-5,
                None, ALU.max), rd=[W[:, 7:8]], wr=[W[:, 8:9]])
        st("vector", lambda E, W=W: E.reciprocal(W[:, 9:10], W[:, 8:9]),
                rd=[W[:, 8:9]], wr=[W[:, 9:10]])
        st("vector", lambda E, W=W: E.tensor_scalar(W[:, 10:11], W[:, 9:10], 127.0,
                None, ALU.mult), rd=[W[:, 9:10]], wr=[W[:, 10:11]])
        st("vector", lambda E, W=W, tt=tt: E.tensor_scalar(s.dq_x[:, tt:tt + 1],
                W[:, 8:9], 1.0 / 127, None, ALU.mult), rd=[W[:, 8:9]],
                wr=[s.dq_x[:, tt:tt + 1]])
        st("scalar", lambda E, p=p, W=W: E.activation(sqt[p][:], xl2[p][:], AF.Identity,
                scale=W[:, 10:11], bias=s.constc[:, 2:3]),
                rd=[xl2[p][:], W[:, 10:11]], wr=[sqt[p][:]])
        st("vector", lambda E, p=p: E.tensor_copy(h16[p][:], sqt[p][:]),
                rd=[sqt[p][:]], wr=[h16[p][:]])
        st("vector", lambda E, p=p: E.tensor_scalar(xl2[p][:], h16[p][:], -1.0, 1536.0,
                ALU.mult, ALU.add), rd=[h16[p][:]], wr=[xl2[p][:]])
        st("vector", lambda E, p=p: E.tensor_scalar(xq16[p][:], xl2[p][:], -127.0, 128.0,
                ALU.max, ALU.min), rd=[xl2[p][:]], wr=[xq16[p][:]])
        for c in range(NCHUNK):
            b = c % 2
            st("tensor", lambda E, c=c, p=p, b=b: E.transpose(ptq[b][:],
                    xq16[p][:, ts(c, 128)], s.identb[:]), rd=[xq16[p][:, ts(c, 128)]],
                    wr=[ptq[b][:]])
            st("vector" if c % 2 == 0 else "scalar",
                    lambda E, c=c, q=q, b=b: (E.tensor_copy if c % 2 == 0 else (lambda o_, i_: E.activation(o_, i_, AF.Identity)))(
                        xqstg[:, c * 512 + q * 128: c * 512 + (q + 1) * 128], ptq[b][:]),
                    rd=[ptq[b][:]], wr=[xqstg[:, c * 512 + q * 128: c * 512 + (q + 1) * 128]],
                    glue=True)
        if q == 3:
            st("gpsimd", lambda E, tt=tt: E.dma_start(out=s.XQT[tt // 4], in_=xqstg[:]),
                    rd=[xqstg[:]], dma=True)
        return steps

    for tt0 in range(0, NT, 2):
        if tt0 % 4 == 0:
            stq = tt0 // 4
            au.step("sync", lambda E, stq=stq: E.dma_start(out=o2sb[:], in_=s.O2T[stq]),
                    wr=[o2sb[:]], dma=True)
        rr_merge([tt_steps(tt0), tt_steps(tt0 + 1)])
    au.emit()
    _ctx.close()


MK.phase_d2 = _phase_d2


def _phase_e(self):
    nc, s = self.nc, self
    NT = self.NT
    au = Auto(self, "phE")
    _ctx = ExitStack()
    st_ = _ctx.enter_context
    wgt_sb = st_(nc.sbuf_tensor("e_wg", [128, 8 * 4096], BF16))
    wup_sb = st_(nc.sbuf_tensor("e_wu", [128, 8 * 4096], BF16))
    xqsb = [st_(nc.sbuf_tensor(f"e_xq{p}", [128, 4096], BF16)) for p in range(2)]
    gsil = st_(nc.sbuf_tensor("e_gsil", [128, 1024], F32))
    gbuf = st_(nc.sbuf_tensor("e_gbuf", [128, 4096], F32))
    h16 = st_(nc.sbuf_tensor("e_h16", [128, 4096], FP16))
    wk = st_(nc.sbuf_tensor("e_wk", [128, 16], F32))
    gstg = st_(nc.sbuf_tensor("e_gstg", [128, 4096], BF16))
    gb16 = st_(nc.sbuf_tensor("e_gb16", [128, 4096], BF16))
    pga = st_(nc.psum_tensor("e_pga", [128, 1024], F32))
    pup = st_(nc.psum_tensor("e_pup", [128, 1024], F32))
    ptq = [st_(nc.psum_tensor(f"e_ptq{p}", [128, 128], BF16)) for p in range(2)]

    for c in range(NCHUNK):
        au.step("sync", lambda E, c=c: E.dma_start(out=wgt_sb[:, ts(c, 4096)],
                in_=s.WGT[ts(c, 128), :]), wr=[wgt_sb[:, ts(c, 4096)]], dma=True)
        au.step("sync", lambda E, c=c: E.dma_start(out=wup_sb[:, ts(c, 4096)],
                in_=s.WUP[ts(c, 128), :]), wr=[wup_sb[:, ts(c, 4096)]], dma=True)

    for tt in range(NT):
        p = tt % 2
        q = tt % 4
        st = tt // 4
        p2 = st % 2
        if q == 0:
            au.step("sync", lambda E, st=st, p2=p2: E.dma_start(
                out=xqsb[p2][:], in_=s.XQT[st]), wr=[xqsb[p2][:]], dma=True)
        au.step("vector", lambda E, tt=tt: E.tensor_scalar(wk[:, 4:5], s.dq_x[:, tt:tt + 1],
                s.wmean[:, 0:1], None, ALU.mult),
                rd=[s.dq_x[:, tt:tt + 1], s.wmean[:, 0:1]], wr=[wk[:, 4:5]])
        au.step("vector", lambda E, tt=tt: E.tensor_scalar(wk[:, 5:6], s.dq_x[:, tt:tt + 1],
                s.wmean[:, 1:2], None, ALU.mult),
                rd=[s.dq_x[:, tt:tt + 1], s.wmean[:, 1:2]], wr=[wk[:, 5:6]])
        for r in range(4):
            def mmr(E, r=r, q=q, p2=p2, which=0):
                pass
            def mk_mm(which, wsb, pdst):
                def f(E, r=r, q=q, p2=p2, wsb=wsb, pdst=pdst):
                    for c in range(NCHUNK):
                        for n in range(2):
                            ins = E.matmul(pdst[:, ts(n, 512)],
                                           xqsb[p2][:, c * 512 + q * 128: c * 512 + (q + 1) * 128],
                                           wsb[:, c * 4096 + r * 1024 + n * 512:
                                               c * 4096 + r * 1024 + (n + 1) * 512],
                                           start=(c == 0), stop=(c == 7))
                    return ins
                return f
            au.step("tensor", mk_mm(0, wgt_sb, pga),
                    rd=[xqsb[p2][:], (wgt_sb.ap().tensor.name, 0, 1 << 40)], wr=[pga[:]])
            au.step("tensor", mk_mm(1, wup_sb, pup),
                    rd=[xqsb[p2][:], (wup_sb.ap().tensor.name, 0, 1 << 40)], wr=[pup[:]])
            au.step("scalar", lambda E: E.activation(gsil[:], pga[:], AF.Silu,
                    scale=wk[:, 4:5]), rd=[pga[:], wk[:, 4:5]], wr=[gsil[:]])
            au.step("vector", lambda E, r=r: E.scalar_tensor_tensor(
                gbuf[:, ts(r, 1024)], pup[:], wk[:, 5:6], gsil[:],
                ALU.mult, ALU.mult), rd=[pup[:], wk[:, 5:6], gsil[:]],
                wr=[gbuf[:, ts(r, 1024)]])
        au.step("vector", lambda E: E.reduce_max(wk[:, 0:1], gbuf[:], AX.X,
                apply_absolute_value=True), rd=[gbuf[:]], wr=[wk[:, 0:1]])
        au.step("vector", lambda E: E.tensor_scalar(wk[:, 1:2], wk[:, 0:1], 1e-5,
                None, ALU.max), rd=[wk[:, 0:1]], wr=[wk[:, 1:2]])
        au.step("vector", lambda E: E.reciprocal(wk[:, 2:3], wk[:, 1:2]),
                rd=[wk[:, 1:2]], wr=[wk[:, 2:3]])
        au.step("vector", lambda E: E.tensor_scalar(wk[:, 3:4], wk[:, 2:3], 127.0,
                None, ALU.mult), rd=[wk[:, 2:3]], wr=[wk[:, 3:4]])
        au.step("vector", lambda E, tt=tt: E.tensor_scalar(s.dq_g[:, tt:tt + 1], wk[:, 1:2],
                s.wmean[:, 2:3], 1.0 / 127, ALU.mult, ALU.mult),
                rd=[wk[:, 1:2], s.wmean[:, 2:3]], wr=[s.dq_g[:, tt:tt + 1]])
        au.step("scalar", lambda E: E.activation(gbuf[:], gbuf[:], AF.Identity,
                scale=wk[:, 3:4], bias=s.constc[:, 2:3]), rd=[gbuf[:], wk[:, 3:4]],
                wr=[gbuf[:]])
        au.step("vector", lambda E: E.tensor_copy(h16[:], gbuf[:]), rd=[gbuf[:]], wr=[h16[:]])
        au.step("vector", lambda E: E.tensor_scalar(gbuf[:], h16[:], -1.0, 1536.0,
                ALU.mult, ALU.add), rd=[h16[:]], wr=[gbuf[:]])
        au.step("vector", lambda E: E.tensor_scalar(gb16[:], gbuf[:], -127.0, 128.0,
                ALU.max, ALU.min), rd=[gbuf[:]], wr=[gb16[:]])
        for c in range(32):
            b = c % 2
            au.step("tensor", lambda E, c=c, b=b: E.transpose(ptq[b][:],
                    gb16[:, ts(c, 128)], s.identb[:]), rd=[gb16[:, ts(c, 128)]], wr=[ptq[b][:]])
            au.step("vector" if c % 2 == 0 else "scalar",
                    lambda E, c=c, b=b: (E.tensor_copy if c % 2 == 0 else (lambda o_, i_: E.activation(o_, i_, AF.Identity)))(
                        gstg[:, ts(c, 128)], ptq[b][:]),
                    rd=[ptq[b][:]], wr=[gstg[:, ts(c, 128)]])
        au.step("gpsimd", lambda E, tt=tt: E.dma_start(out=s.GQT[tt], in_=gstg[:]),
                rd=[gstg[:]], dma=True)
    au.emit()
    _ctx.close()


MK.phase_e = _phase_e


def _phase_f(self):
    nc, s = self.nc, self
    NT = self.NT
    au = Auto(self, "phF")
    _ctx = ExitStack()
    st_ = _ctx.enter_context
    wdn_sb = st_(nc.sbuf_tensor("f_wd", [128, 32 * 1024], BF16))
    gqb = [st_(nc.sbuf_tensor(f"f_gq{p}", [128, 4096], BF16)) for p in range(2)]
    y1c = [st_(nc.sbuf_tensor(f"f_y1{p}", [128, D], F32)) for p in range(2)]
    yb = [st_(nc.sbuf_tensor(f"f_yb{p}", [128, D], F32)) for p in range(2)]
    dqc = st_(nc.sbuf_tensor("f_dqc", [128, 1], F32))
    py = [st_(nc.psum_tensor(f"f_py{p}", [128, 512], F32)) for p in range(2)]

    for c in range(32):
        au.step("sync", lambda E, c=c: E.dma_start(out=wdn_sb[:, ts(c, 1024)],
                in_=s.WDN[ts(c, 128), :]), wr=[wdn_sb[:, ts(c, 1024)]], dma=True)
    for tt in range(NT):
        p = tt % 2
        au.step("sync", lambda E, tt=tt, p=p: E.dma_start(out=gqb[p][:], in_=s.GQT[tt]),
                wr=[gqb[p][:]], dma=True)
        au.step("sync", lambda E, tt=tt, p=p: E.dma_start(out=y1c[p][:],
                in_=s.Y1[ts(tt, 128), :]), wr=[y1c[p][:]], dma=True)
        for nch in range(2):
            def mmd(E, nch=nch, p=p):
                for c in range(32):
                    ins = E.matmul(py[nch][:], gqb[p][:, ts(c, 128)],
                                   wdn_sb[:, c * 1024 + nch * 512: c * 1024 + (nch + 1) * 512],
                                   start=(c == 0), stop=(c == 31))
                return ins
            au.step("tensor", mmd, rd=[gqb[p][:], (wdn_sb.ap().tensor.name, 0, 1 << 40)],
                    wr=[py[nch][:]])
            au.step("vector", lambda E, nch=nch, p=p, tt=tt: E.scalar_tensor_tensor(
                yb[p][:, ts(nch, 512)], py[nch][:], s.dq_g[:, tt:tt + 1],
                y1c[p][:, ts(nch, 512)], ALU.mult, ALU.add),
                rd=[py[nch][:], s.dq_g[:, tt:tt + 1], y1c[p][:, ts(nch, 512)]],
                wr=[yb[p][:, ts(nch, 512)]])
        au.step("gpsimd", lambda E, tt=tt, p=p: E.dma_start(out=s.y_out[ts(tt, 128), :],
                in_=yb[p][:]), rd=[yb[p][:]], wr=[("YOUT", tt, tt + 1)], dma=True)
    au.step("gpsimd", lambda E: E.memset(dqc[:], 0.0), rd=[("YOUT", 0, 1 << 40)])
    au.emit()
    _ctx.close()


MK.phase_f = _phase_f


def build_full(nseq=2, L=4096, d1_coarse=False):
    m = MK(nseq=nseq, L=L)
    m.d1_coarse = d1_coarse
    m.phase_prep()
    m.barrier()
    m.phase_w()
    m.barrier()
    m.phase_a()
    m.barrier()
    m.phase_b()
    m.barrier()
    m.phase_c()
    m.barrier()
    m.phase_d1()
    m.barrier()
    m.phase_d2()
    m.barrier()
    m.phase_e()
    m.barrier()
    m.phase_f()
    m.stack.close()
    return m


def _phase_a(self):
    nc, s = self.nc, self
    NT = self.NT
    au = Auto(self, "phA")
    _ctx = ExitStack()
    st_ = _ctx.enter_context
    xbuf = [st_(nc.sbuf_tensor(f"a_x{p}", [128, D], F32)) for p in range(2)]
    xlnb = [st_(nc.sbuf_tensor(f"a_ln{p}", [128, D], F32)) for p in range(2)]
    stg = [st_(nc.sbuf_tensor(f"a_st{p}", [128, D], F32)) for p in range(2)]
    spar = [st_(nc.sbuf_tensor(f"a_sp{p}", [128, 8], F32)) for p in range(2)]
    strash = st_(nc.sbuf_tensor("a_trash", [128, D], F32))
    ptr = [st_(nc.psum_tensor(f"a_ptr{p}", [128, 512], F32)) for p in range(4)]
    pbeta = st_(nc.psum_tensor("a_pbeta", [128, 8], F32))

    for i in range(NT):
        p = i % 2
        W = spar[p]
        au.step("sync", lambda E, i=i, p=p: E.dma_start(out=xbuf[p][:],
                in_=s.x_in[ts(i, 128), :]), wr=[xbuf[p][:]], dma=True)
        au.step("scalar", lambda E, p=p, W=W: E.activation(strash[:], xbuf[p][:], AF.Square,
                accum_out=W[:, 0:1]), rd=[xbuf[p][:]], wr=[strash[:], W[:, 0:1]])
        au.step("vector", lambda E, p=p, W=W: E.reduce_sum(W[:, 1:2], xbuf[p][:], AX.X,
                negate=True), rd=[xbuf[p][:]], wr=[W[:, 1:2]])
        au.step("vector", lambda E, W=W: E.tensor_scalar(W[:, 2:3], W[:, 1:2], 1.0 / D,
                None, ALU.mult), rd=[W[:, 1:2]], wr=[W[:, 2:3]])
        au.step("vector", lambda E, W=W: E.tensor_mul(W[:, 3:4], W[:, 2:3], W[:, 2:3]),
                rd=[W[:, 2:3]], wr=[W[:, 3:4]])
        au.step("vector", lambda E, W=W: E.tensor_scalar(W[:, 4:5], W[:, 0:1], 1.0 / D,
                W[:, 3:4], ALU.mult, ALU.subtract), rd=[W[:, 0:1], W[:, 3:4]], wr=[W[:, 4:5]])
        au.step("scalar", lambda E, W=W: E.activation(W[:, 5:6], W[:, 4:5], AF.Sqrt,
                bias=s.constc[:, 0:1]), rd=[W[:, 4:5]], wr=[W[:, 5:6]])
        au.step("vector", lambda E, W=W: E.reciprocal(W[:, 6:7], W[:, 5:6]),
                rd=[W[:, 5:6]], wr=[W[:, 6:7]])
        au.step("vector", lambda E, p=p, W=W: E.tensor_scalar(xlnb[p][:], xbuf[p][:],
                W[:, 2:3], W[:, 6:7], ALU.add, ALU.mult),
                rd=[xbuf[p][:], W[:, 2:3], W[:, 6:7]], wr=[xlnb[p][:]])
        bp = p * 2
        def tr4(E, p=p, bp=bp, half=0):
            for c in range(4):
                ins = E.transpose(ptr[bp + half][:, ts(c, 128)],
                                  xlnb[p][:, ts(half * 4 + c, 128)], s.ident[:])
            return ins
        au.step("tensor", lambda E, p=p, bp=bp: tr4(E, p, bp, 0),
                rd=[xlnb[p][:]], wr=[ptr[bp][:]])
        au.step("tensor", lambda E, p=p, bp=bp: tr4(E, p, bp, 1),
                rd=[xlnb[p][:]], wr=[ptr[bp + 1][:]])
        au.step("vector", lambda E, p=p, bp=bp: E.tensor_copy(stg[p][:, 0:512], ptr[bp][:]),
                rd=[ptr[bp][:]], wr=[stg[p][:, 0:512]])
        au.step("scalar", lambda E, p=p, bp=bp: E.activation(stg[p][:, 512:1024],
                ptr[bp + 1][:], AF.Identity), rd=[ptr[bp + 1][:]], wr=[stg[p][:, 512:1024]])
        def betamm(E, p=p):
            for c in range(NCHUNK):
                E.matmul(pbeta[:], stg[p][:, ts(c, 128)], s.wbres[:, ts(c, 8)],
                         start=(c == 0), stop=False)
            return E.matmul(pbeta[:], s.ones_col[0:1, :], s.bres_row[0:1, :],
                            start=False, stop=True)
        au.step("tensor", betamm, rd=[stg[p][:], s.wbres[:], s.ones_col[:], s.bres_row[:]],
                wr=[pbeta[:]])
        au.step("scalar", lambda E, i=i: E.activation(s.beta_all[:, i * 8:(i + 1) * 8],
                pbeta[:], AF.Sigmoid), rd=[pbeta[:]],
                wr=[s.beta_all[:, i * 8:(i + 1) * 8]])
        for c in range(NCHUNK):
            au.step("gpsimd", lambda E, c=c, i=i, p=p: E.dma_start(
                out=s.XLNT[i // 4][:, c * 512 + (i % 4) * 128: c * 512 + (i % 4 + 1) * 128],
                in_=stg[p][:, ts(c, 128)]), rd=[stg[p][:, ts(c, 128)]], dma=True)
    au.emit()
    _ctx.close()


MK.phase_a = _phase_a


def _pass_proj(self, wname, w_ap, conv_off, K, dst_tm, with_fir,
               fir64_own=(), fir64_jobs=()):
    nc, s = self.nc, self
    NSTR_T = self.nseq * self.NSTR
    SEQ_STRIPES = self.NSTR
    au = Auto(self, f"phB{wname}")
    _ctx = ExitStack()
    st_ = _ctx.enter_context
    wbuf = st_(nc.sbuf_tensor(f"b_w_{wname}", [128, 8192], F32))
    xsb = [st_(nc.sbuf_tensor(f"b_x{p}_{wname}", [128, 4096], F32)) for p in range(2)]
    prebufC = st_(nc.sbuf_tensor(f"b_pre_{wname}", [128, 8 * 516], F32))
    acc8 = [st_(nc.sbuf_tensor(f"b_acc{c}_{wname}", [128, 512], F32)) for c in range(8)]
    tailb = st_(nc.sbuf_tensor(f"b_tail_{wname}", [128, 32], F32))
    stage = st_(nc.sbuf_tensor(f"b_stage_{wname}", [128, 4096], F32))
    ppj = [st_(nc.psum_tensor(f"b_ppj{p}_{wname}", [128, 512], F32)) for p in range(2)]
    ptrb = [st_(nc.psum_tensor(f"b_ptr{p}_{wname}", [128, 512], F32)) for p in range(2)]
    if with_fir:
        vact = st_(nc.sbuf_tensor(f"b_vact_{wname}", [128, 4608], F32))
        facc8 = [st_(nc.sbuf_tensor(f"b_fac{c}_{wname}", [128, 512], F32)) for c in range(8)]
        fstage = st_(nc.sbuf_tensor(f"b_fstage_{wname}", [128, 4096], F32))
        pfir = [st_(nc.psum_tensor(f"b_pf{p}_{wname}", [128, 512], F32)) for p in range(2)]
    else:
        acts = st_(nc.sbuf_tensor(f"b_acts_{wname}", [128, 4096], F32))
        if fir64_jobs:
            vbuf2 = [st_(nc.sbuf_tensor(f"b_vb{p}_{wname}", [128, 4608], F32)) for p in range(2)]
            facc8 = [st_(nc.sbuf_tensor(f"b_fac{c}_{wname}", [128, 512], F32)) for c in range(8)]
            fstage = st_(nc.sbuf_tensor(f"b_fstage_{wname}", [128, 4096], F32))
            pfir = [st_(nc.psum_tensor(f"b_pf{p}_{wname}", [128, 512], F32)) for p in range(2)]

    def emit_fir64(fs, vsrc):
        for j in range(64):
            for c in range(NCHUNK):
                wcol = s.firl[:, c * 64 + j: c * 64 + j + 1]
                base = c * 576 + 1
                rdr = [vsrc[:, base + j: base + j + 512], wcol]
                if j == 0:
                    au.step("vector", lambda E, c=c, wcol=wcol, base=base, vsrc=vsrc:
                            E.tensor_scalar(facc8[c][:], vsrc[:, base: base + 512],
                                            wcol, None, ALU.mult),
                            rd=rdr, wr=[facc8[c][:]])
                else:
                    au.step("vector", lambda E, c=c, wcol=wcol, base=base, j=j, vsrc=vsrc:
                            E.scalar_tensor_tensor(facc8[c][:],
                                                   vsrc[:, base + j: base + j + 512],
                                                   wcol, facc8[c][:], ALU.mult, ALU.add),
                            rd=rdr + [facc8[c][:]], wr=[facc8[c][:]])
        frow0 = (fs // SEQ_STRIPES) * self.L + (fs % SEQ_STRIPES) * 512
        emit_tr_block(lambda c, tt: facc8[c][:, ts(tt, 128)],
                      lambda c: facc8[c][:], fstage, pfir, s.LLTM, frow0)

    def emit_tr_block(srcf, srdf, stg, pbank, dst, row0):
        for c in range(NCHUNK):
            def trp(E, c=c, srcf=srcf):
                for tt in range(4):
                    ins = E.transpose(pbank[c % 2][:, ts(tt, 128)], srcf(c, tt), s.ident[:])
                return ins
            au.step("tensor", trp, rd=[srdf(c)], wr=[pbank[c % 2][:]])
            def evq(E, c=c, stg=stg):
                for tt in range(4):
                    ins = E.activation(stg[:, tt * 1024 + c * 128: tt * 1024 + (c + 1) * 128],
                                       pbank[c % 2][:, ts(tt, 128)], AF.Identity)
                return ins
            au.step("scalar", evq, rd=[pbank[c % 2][:]],
                    wr=[stg[:, tt * 1024 + c * 128: tt * 1024 + (c + 1) * 128]
                        for tt in range(4)])
        for tt in range(4):
            au.step("gpsimd", lambda E, tt=tt, row0=row0, dst=dst, stg=stg: E.dma_start(
                out=dst[row0 + tt * 128: row0 + (tt + 1) * 128, :],
                in_=stg[:, ts(tt, 1024)]), rd=[stg[:]], dma=True)

    jobs = list(fir64_jobs)

    for c in range(NCHUNK):
        au.step("sync", lambda E, c=c: E.dma_start(out=wbuf[:, ts(c, 1024)],
                in_=w_ap[ts(c, 128), :]), wr=[wbuf[:, ts(c, 1024)]], dma=True)

    for st in range(NSTR_T):
        seq_start = (st % SEQ_STRIPES) == 0
        au.step("sync", lambda E, st=st: E.dma_start(out=xsb[st % 2][:], in_=s.XLNT[st]),
                wr=[xsb[st % 2][:]], dma=True)
        if not with_fir and jobs:
            while jobs and len(jobs) > ((NSTR_T - 1 - st) * len(fir64_jobs)) // NSTR_T:
                fs = jobs.pop(0)
                jp = (len(fir64_jobs) - len(jobs)) % 2
                au.step("sync", lambda E, fs=fs, jp=jp: E.dma_start(out=vbuf2[jp][:],
                        in_=s.VCM[fs]), wr=[vbuf2[jp][:]], dma=True)
                emit_fir64(fs, vbuf2[jp])
        for c in range(NCHUNK):
            def mmp(E, c=c, st=st):
                for dch in range(NCHUNK):
                    ins = E.matmul(ppj[c % 2][:],
                                   wbuf[:, dch * 1024 + c * 128: dch * 1024 + (c + 1) * 128],
                                   xsb[st % 2][:, ts(dch, 512)],
                                   start=(dch == 0), stop=(dch == 7))
                return ins
            au.step("tensor", mmp, rd=[xsb[st % 2][:], (wbuf.ap().tensor.name, 0, 1 << 40)],
                    wr=[ppj[c % 2][:]])
            au.step("scalar", lambda E, c=c: E.activation(
                prebufC[:, c * 516 + 4: c * 516 + 516], ppj[c % 2][:], AF.Identity),
                rd=[ppj[c % 2][:]], wr=[prebufC[:, c * 516 + 4: c * 516 + 516]])
        for c in range(NCHUNK):
            au.step("scalar", lambda E, c=c, seq_start=seq_start: E.activation(
                prebufC[:, c * 516: c * 516 + 4],
                s.zero64[:, 0:4] if seq_start else tailb[:, ts(c, 4)], AF.Identity),
                rd=[] if seq_start else [tailb[:, ts(c, 4)]],
                wr=[prebufC[:, c * 516: c * 516 + 4]])
            au.step("scalar", lambda E, c=c: E.activation(tailb[:, ts(c, 4)],
                    prebufC[:, c * 516 + 512: c * 516 + 516], AF.Identity),
                    rd=[prebufC[:, c * 516 + 512: c * 516 + 516]], wr=[tailb[:, ts(c, 4)]])
        for j in range(K):
            for c in range(NCHUNK):
                wcol = s.convw[:, c * 12 + conv_off + j: c * 12 + conv_off + j + 1]
                lo = c * 516 + 1 + j
                if j == 0:
                    au.step("vector", lambda E, c=c, wcol=wcol, lo=lo: E.tensor_scalar(
                        acc8[c][:], prebufC[:, lo:lo + 512], wcol, None, ALU.mult),
                        rd=[prebufC[:, lo:lo + 512], wcol], wr=[acc8[c][:]])
                else:
                    au.step("vector", lambda E, c=c, wcol=wcol, lo=lo: E.scalar_tensor_tensor(
                        acc8[c][:], prebufC[:, lo:lo + 512], wcol, acc8[c][:],
                        ALU.mult, ALU.add), rd=[prebufC[:, lo:lo + 512], wcol, acc8[c][:]],
                        wr=[acc8[c][:]])
        for c in range(NCHUNK):
            if with_fir:
                au.step("scalar", lambda E, c=c, seq_start=seq_start: E.activation(
                    vact[:, c * 576: c * 576 + 64],
                    s.zero64[:, 0:64] if seq_start else vact[:, c * 576 + 512: c * 576 + 576],
                    AF.Identity),
                    rd=[] if seq_start else [vact[:, c * 576 + 512: c * 576 + 576]],
                    wr=[vact[:, c * 576: c * 576 + 64]])
                au.step("scalar", lambda E, c=c: E.activation(
                    vact[:, c * 576 + 64: (c + 1) * 576], acc8[c][:], AF.Silu),
                    rd=[acc8[c][:]], wr=[vact[:, c * 576 + 64: (c + 1) * 576]])
            else:
                au.step("scalar", lambda E, c=c: E.activation(acts[:, ts(c, 512)],
                        acc8[c][:], AF.Silu), rd=[acc8[c][:]], wr=[acts[:, ts(c, 512)]])
        seq = st // SEQ_STRIPES
        row0 = seq * self.L + (st % SEQ_STRIPES) * 512
        src_of = (lambda c, tt: vact[:, c * 576 + 64 + tt * 128: c * 576 + 64 + (tt + 1) * 128]) \
            if with_fir else (lambda c, tt: acts[:, c * 512 + tt * 128: c * 512 + (tt + 1) * 128])
        srd_of = (lambda c: vact[:, c * 576 + 64: (c + 1) * 576]) if with_fir \
            else (lambda c: acts[:, ts(c, 512)])
        emit_tr_block(src_of, srd_of, stage, ptrb, dst_tm, row0)
        if with_fir:
            au.step("gpsimd", lambda E, st=st: E.dma_start(out=s.VCM[st], in_=vact[:]),
                    rd=[vact[:]], dma=True)
            taps, wtile, wid, dst = 5, s.firs, 5, s.LSTM
            for j in range(taps):
                for c in range(NCHUNK):
                    wcol = wtile[:, c * wid + j: c * wid + j + 1]
                    base = c * 576 + 64 - (taps - 1)
                    rdr = [vact[:, base + j: base + j + 512], wcol]
                    if j == 0:
                        au.step("vector", lambda E, c=c, wcol=wcol, base=base: E.tensor_scalar(
                            facc8[c][:], vact[:, base: base + 512], wcol, None, ALU.mult),
                            rd=rdr, wr=[facc8[c][:]])
                    else:
                        au.step("vector", lambda E, c=c, wcol=wcol, base=base, j=j:
                                E.scalar_tensor_tensor(facc8[c][:],
                                                       vact[:, base + j: base + j + 512],
                                                       wcol, facc8[c][:], ALU.mult, ALU.add),
                                rd=rdr + [facc8[c][:]], wr=[facc8[c][:]])
            emit_tr_block(lambda c, tt: facc8[c][:, ts(tt, 128)],
                          lambda c: facc8[c][:], fstage, pfir, dst, row0)
            if st in fir64_own:
                emit_fir64(st, vact)
    au.emit()
    _ctx.close()


def _phase_b(self):
    n = self.nseq * self.NSTR
    qn = (n * 3) // 8
    kn = (n * 3) // 8
    stripes = list(range(n))
    qjobs = stripes[:qn]
    kjobs = stripes[qn:qn + kn]
    own = stripes[qn + kn:]
    self._pass_proj("v", self.Wv, 8, 4, self.VTM, True, fir64_own=own)
    self.barrier()
    self._pass_proj("q", self.Wq, 0, 4, self.QTM, False, fir64_jobs=qjobs)
    self.barrier()
    self._pass_proj("k", self.Wk, 4, 4, self.KTM, False, fir64_jobs=kjobs)
    self.barrier()


MK._pass_proj = _pass_proj
MK.phase_b = _phase_b




class _AuProxy:
    def __init__(self, st):
        self.step = st

def _phase_c(self):
    nc, s = self.nc, self
    NSTRM = self.nseq * 2
    NCHK = self.NTS
    au = Auto(self, "phC")
    _ctx = ExitStack()
    st_ = _ctx.enter_context
    ST = []
    for k in range(NSTRM):
        d = {}
        d["S"] = [st_(nc.sbuf_tensor(f"c_S{k}_{h}", [128, 256], F32)) for h in range(2)]
        d["qt"] = [st_(nc.sbuf_tensor(f"c_qt{k}{p}", [128, 256], F32)) for p in range(2)]
        d["kt"] = [st_(nc.sbuf_tensor(f"c_kt{k}{p}", [128, 256], F32)) for p in range(2)]
        d["vt"] = [st_(nc.sbuf_tensor(f"c_vt{k}{p}", [128, 256], F32)) for p in range(2)]
        for nm in ("qn", "kn", "vb", "kb", "qnf", "knf", "kbpf", "kbnf", "z", "ost"):
            d[nm] = st_(nc.sbuf_tensor(f"c_{nm}{k}", [128, 256], F32))
        for nm in ("M", "Xa", "Xb", "Y", "at"):
            d[nm] = st_(nc.sbuf_tensor(f"c_{nm}{k}", [128, 128], F32))
        d["sq"] = st_(nc.sbuf_tensor(f"c_sq{k}", [128, 264], F32))
        d["pa"] = st_(nc.psum_tensor(f"c_pa{k}", [128, 512], F32))
        d["pb"] = st_(nc.psum_tensor(f"c_pb{k}", [128, 512], F32))
        ST.append(d)

    NG = 2

    def rows(seq, i):
        r0 = seq * self.L + i * 128
        return slice(r0, r0 + 128)

    def Xj(d, j):
        if j == 0:
            return d["M"]
        return d["Xa"] if j % 2 == 1 else d["Xb"]

    def chunk_steps(g, i, k):
        steps = []
        def _st(eng, fn, rd=(), wr=(), dma=False):
            steps.append((eng, fn, rd, wr, dma))
        au = _AuProxy(_st)
        d = ST[k]
        seq, hh = k // 2, k % 2
        head = g * 2 + hh
        p = i % 2
        cs = slice(head * 256, (head + 1) * 256)
        tt = seq * self.NTS + i
        bcol = s.beta_all[:, tt * 8 + head: tt * 8 + head + 1]
        sq = d["sq"]
        pa, pb = d["pa"], d["pb"]
        au.step("sync", lambda E, d=d, p=p, seq=seq, i=i, cs=cs: E.dma_start(
            out=d["qt"][p][:], in_=s.QTM[rows(seq, i), cs]),
            wr=[d["qt"][p][:]], dma=True)
        au.step("sync", lambda E, d=d, p=p, seq=seq, i=i, cs=cs: E.dma_start(
            out=d["kt"][p][:], in_=s.KTM[rows(seq, i), cs]),
            wr=[d["kt"][p][:]], dma=True)
        au.step("sync", lambda E, d=d, p=p, seq=seq, i=i, cs=cs: E.dma_start(
            out=d["vt"][p][:], in_=s.VTM[rows(seq, i), cs]),
            wr=[d["vt"][p][:]], dma=True)
        au.step("scalar", lambda E, d=d, p=p, sq=sq: E.activation(sq[:, 0:256],
                d["qt"][p][:], AF.Square, accum_out=sq[:, 256:257]),
                rd=[d["qt"][p][:]], wr=[sq[:, 0:256], sq[:, 256:257]])
        au.step("scalar", lambda E, d=d, p=p, sq=sq: E.activation(sq[:, 0:256],
                d["kt"][p][:], AF.Square, accum_out=sq[:, 257:258]),
                rd=[d["kt"][p][:]], wr=[sq[:, 0:256], sq[:, 257:258]])
        au.step("scalar", lambda E, sq=sq: E.activation(sq[:, 258:259],
                sq[:, 256:257], AF.Sqrt, bias=s.constc[:, 1:2]),
                rd=[sq[:, 256:257]], wr=[sq[:, 258:259]])
        au.step("scalar", lambda E, sq=sq: E.activation(sq[:, 259:260],
                sq[:, 257:258], AF.Sqrt, bias=s.constc[:, 1:2]),
                rd=[sq[:, 257:258]], wr=[sq[:, 259:260]])
        au.step("vector", lambda E, sq=sq: E.reciprocal(sq[:, 260:261],
                sq[:, 258:259]), rd=[sq[:, 258:259]], wr=[sq[:, 260:261]])
        au.step("vector", lambda E, sq=sq: E.reciprocal(sq[:, 261:262],
                sq[:, 259:260]), rd=[sq[:, 259:260]], wr=[sq[:, 261:262]])
        au.step("vector", lambda E, d=d, p=p, sq=sq: E.tensor_scalar(d["qn"][:],
                d["qt"][p][:], sq[:, 260:261], None, ALU.mult),
                rd=[d["qt"][p][:], sq[:, 260:261]], wr=[d["qn"][:]])
        au.step("vector", lambda E, d=d, p=p, sq=sq: E.tensor_scalar(d["kn"][:],
                d["kt"][p][:], sq[:, 261:262], None, ALU.mult),
                rd=[d["kt"][p][:], sq[:, 261:262]], wr=[d["kn"][:]])
        au.step("vector", lambda E, d=d, bcol=bcol: E.tensor_scalar(d["kb"][:],
                d["kn"][:], bcol, None, ALU.mult),
                rd=[d["kn"][:], bcol], wr=[d["kb"][:]])
        au.step("vector", lambda E, d=d, p=p, bcol=bcol: E.tensor_scalar(d["vb"][:],
                d["vt"][p][:], bcol, None, ALU.mult),
                rd=[d["vt"][p][:], bcol], wr=[d["vb"][:]])
        def tr_qk(E, d=d):
            for cch in range(2):
                E.transpose(d["pa"][:, cch * 128:(cch + 1) * 128],
                            d["qn"][:, ts(cch, 128)], s.ident[:])
            for cch in range(2):
                ins = E.transpose(d["pa"][:, 256 + cch * 128: 384 + cch * 128],
                                  d["kn"][:, ts(cch, 128)], s.ident[:])
            return ins
        au.step("tensor", tr_qk, rd=[d["qn"][:], d["kn"][:]], wr=[pa[:]])
        def tr_kb(E, d=d):
            E.transpose(d["pb"][:, 0:128], d["kb"][:, 0:128], s.ident[:])
            return E.transpose(d["pb"][:, 128:256], d["kb"][:, 128:256], s.ident[:])
        au.step("tensor", tr_kb, rd=[d["kb"][:]], wr=[pb[:]])
        au.step("vector", lambda E, d=d: E.tensor_copy(d["qnf"][:], d["pa"][:, 0:256]),
                rd=[pa[:]], wr=[d["qnf"][:]])
        au.step("vector", lambda E, d=d: E.tensor_copy(d["knf"][:], d["pa"][:, 256:512]),
                rd=[pa[:]], wr=[d["knf"][:]])
        au.step("scalar", lambda E, d=d: E.activation(d["kbpf"][:], d["pb"][:, 0:256],
                AF.Identity), rd=[pb[:]], wr=[d["kbpf"][:]])
        au.step("scalar", lambda E, d=d: E.activation(d["kbnf"][:], d["pb"][:, 0:256],
                AF.Identity, scale=s.constc[:, 3:4]), rd=[pb[:]], wr=[d["kbnf"][:]])
        def mm_m(E, d=d):
            for cch in range(2):
                ins = E.matmul(d["pa"][:, 0:128], d["knf"][:, ts(cch, 128)],
                               d["kbpf"][:, ts(cch, 128)],
                               start=(cch == 0), stop=(cch == 1))
            return ins
        au.step("tensor", mm_m, rd=[d["knf"][:], d["kbpf"][:]], wr=[pa[:]])
        au.step("vector", lambda E, d=d: E.tensor_mul(d["M"][:], d["pa"][:, 0:128],
                s.maskMn[:]), rd=[pa[:], s.maskMn[:]], wr=[d["M"][:]])
        for kstep in range(8):
            if kstep == 0:
                if i == 0:
                    au.step("tensor", lambda E, d=d: E.matmul(d["pb"][:, 0:256],
                            s.ident[:], d["vb"][:], start=True, stop=True),
                            rd=[s.ident[:], d["vb"][:]], wr=[pb[:]])
                else:
                    def mm_z0(E, d=d):
                        E.matmul(d["pb"][:, 0:256], s.ident[:], d["vb"][:],
                                 start=True, stop=False)
                        E.matmul(d["pb"][:, 0:256], d["kbnf"][:, 0:128],
                                 d["S"][0][:], start=False, stop=False)
                        return E.matmul(d["pb"][:, 0:256], d["kbnf"][:, 128:256],
                                        d["S"][1][:], start=False, stop=True)
                    au.step("tensor", mm_z0,
                            rd=[s.ident[:], d["vb"][:], d["kbnf"][:],
                                d["S"][0][:], d["S"][1][:]], wr=[pb[:]])
                au.step("vector", lambda E, d=d: E.tensor_copy(d["z"][:],
                        d["pb"][:, 0:256]), rd=[pb[:]], wr=[d["z"][:]])
            else:
                X = Xj(d, kstep - 1)
                au.step("tensor", lambda E, d=d, X=X: E.matmul(d["pb"][:, 0:256],
                        X[:], d["z"][:], start=True, stop=True),
                        rd=[X[:], d["z"][:]], wr=[pb[:]])
                au.step("vector", lambda E, d=d: E.tensor_add(d["z"][:], d["z"][:],
                        d["pb"][:, 0:256]), rd=[pb[:], d["z"][:]], wr=[d["z"][:]])
            if kstep < 6:
                j2 = kstep + 1
                Xp, Xn = Xj(d, j2 - 1), Xj(d, j2)
                au.step("tensor", lambda E, d=d, Xp=Xp: E.transpose(
                        d["pa"][:, 128:256], Xp[:], s.ident[:]),
                        rd=[Xp[:]], wr=[pa[:]])
                au.step("vector", lambda E, d=d: E.tensor_copy(d["Y"][:],
                        d["pa"][:, 128:256]), rd=[pa[:]], wr=[d["Y"][:]])
                au.step("tensor", lambda E, d=d, Xp=Xp: E.matmul(d["pa"][:, 0:128],
                        d["Y"][:], Xp[:], start=True, stop=True),
                        rd=[d["Y"][:], Xp[:]], wr=[pa[:]])
                au.step("vector", lambda E, d=d, Xn=Xn: E.tensor_copy(Xn[:],
                        d["pa"][:, 0:128]), rd=[pa[:]], wr=[Xn[:]])
        def mm_a(E, d=d):
            for cch in range(2):
                ins = E.matmul(d["pa"][:, 0:128], d["knf"][:, ts(cch, 128)],
                               d["qnf"][:, ts(cch, 128)],
                               start=(cch == 0), stop=(cch == 1))
            return ins
        au.step("tensor", mm_a, rd=[d["knf"][:], d["qnf"][:]], wr=[pa[:]])
        au.step("vector", lambda E, d=d: E.tensor_mul(d["at"][:], d["pa"][:, 0:128],
                s.maskUI[:]), rd=[pa[:], s.maskUI[:]], wr=[d["at"][:]])
        if i == 0:
            au.step("tensor", lambda E, d=d: E.matmul(d["pb"][:, 256:512],
                    d["at"][:], d["z"][:], start=True, stop=True),
                    rd=[d["at"][:], d["z"][:]], wr=[pb[:]])
        else:
            def mm_o(E, d=d):
                E.matmul(d["pb"][:, 256:512], d["qnf"][:, 0:128], d["S"][0][:],
                         start=True, stop=False)
                E.matmul(d["pb"][:, 256:512], d["qnf"][:, 128:256], d["S"][1][:],
                         start=False, stop=False)
                return E.matmul(d["pb"][:, 256:512], d["at"][:], d["z"][:],
                                start=False, stop=True)
            au.step("tensor", mm_o, rd=[d["qnf"][:], d["S"][0][:], d["S"][1][:],
                    d["at"][:], d["z"][:]], wr=[pb[:]])
        au.step("scalar", lambda E, d=d: E.activation(d["ost"][:], d["pb"][:, 256:512],
                AF.Identity), rd=[pb[:]], wr=[d["ost"][:]])
        def mm_su(E, d=d):
            E.matmul(d["pa"][:, 0:256], d["kn"][:, 0:128], d["z"][:],
                     start=True, stop=True)
            return E.matmul(d["pa"][:, 256:512], d["kn"][:, 128:256], d["z"][:],
                            start=True, stop=True)
        au.step("tensor", mm_su, rd=[d["kn"][:], d["z"][:]], wr=[pa[:]])
        if i == 0:
            au.step("vector", lambda E, d=d: E.tensor_copy(d["S"][0][:],
                    d["pa"][:, 0:256]), rd=[pa[:]], wr=[d["S"][0][:]])
            au.step("vector", lambda E, d=d: E.tensor_copy(d["S"][1][:],
                    d["pa"][:, 256:512]), rd=[pa[:]], wr=[d["S"][1][:]])
        else:
            au.step("vector", lambda E, d=d: E.tensor_add(d["S"][0][:], d["S"][0][:],
                    d["pa"][:, 0:256]), rd=[pa[:], d["S"][0][:]], wr=[d["S"][0][:]])
            au.step("vector", lambda E, d=d: E.tensor_add(d["S"][1][:], d["S"][1][:],
                    d["pa"][:, 256:512]), rd=[pa[:], d["S"][1][:]], wr=[d["S"][1][:]])
        au.step("gpsimd", lambda E, d=d, seq=seq, i=i, head=head: E.dma_start(
                out=s.DOTM[rows(seq, i), head * 256:(head + 1) * 256],
                in_=d["ost"][:]), rd=[d["ost"][:]], dma=True)
        return steps

    for g in range(NG):
        for i in range(NCHK):
            gens = [chunk_steps(g, i, k) for k in range(NSTRM)]
            pos = [0] * NSTRM
            rem = sum(len(g_) for g_ in gens)
            while rem:
                for k in range(NSTRM):
                    if pos[k] < len(gens[k]):
                        eng, fn, rd, wr, dma = gens[k][pos[k]]
                        au.step(eng, fn, rd=rd, wr=wr, dma=dma)
                        pos[k] += 1
                        rem -= 1
    au.emit()
    _ctx.close()


MK.phase_c = _phase_c


def _phase_prep(self):
    nc, s = self.nc, self
    au = Auto(self, "prep")
    _ctx = ExitStack()
    st_ = _ctx.enter_context
    tmp = st_(nc.sbuf_tensor("p_tmp", [128, 8], F32))
    tmp1 = st_(nc.sbuf_tensor("p_tmp1", [1, 32], F32))
    pp = st_(nc.psum_tensor("p_pp", [128, 16], F32))

    def ld(fn, wr_ap):
        au.step("sync", fn, wr=[wr_ap], dma=True)
    for c in range(NCHUNK):
        ld(lambda E, c=c: E.dma_start(out=s.wbres[:, c * 8:c * 8 + 4], in_=s.Wb[ts(c, 128), :]), s.wbres[:, c * 8:c * 8 + 4])
        ld(lambda E, c=c: E.dma_start(out=s.wbres[:, c * 8 + 4:c * 8 + 8], in_=s.Wres[ts(c, 128), :]), s.wbres[:, c * 8 + 4:c * 8 + 8])
        ld(lambda E, c=c: E.dma_start(out=s.firl[:, ts(c, 64)], in_=s.fir_long[ts(c, 128), :]), s.firl[:, ts(c, 64)])
        ld(lambda E, c=c: E.dma_start(out=s.firs[:, ts(c, 5)], in_=s.fir_short[ts(c, 128), :]), s.firs[:, ts(c, 5)])
        ld(lambda E, c=c: E.dma_start(out=s.wg2[:, ts(c, 4)], in_=s.Wg2[ts(c, 128), :]), s.wg2[:, ts(c, 4)])
        ld(lambda E, c=c: E.dma_start(out=s.convw[:, c * 12:c * 12 + 4], in_=s.wq_conv[ts(c, 128), :]), s.convw[:, c * 12:c * 12 + 4])
        ld(lambda E, c=c: E.dma_start(out=s.convw[:, c * 12 + 4:c * 12 + 8], in_=s.wk_conv[ts(c, 128), :]), s.convw[:, c * 12 + 4:c * 12 + 8])
        ld(lambda E, c=c: E.dma_start(out=s.convw[:, c * 12 + 8:c * 12 + 12], in_=s.wv_conv[ts(c, 128), :]), s.convw[:, c * 12 + 8:c * 12 + 12])
        ld(lambda E, c=c: E.dma_start(out=s.bg1c[:, c:c + 1], in_=s.bg1[ts(c, 128)].unsqueeze(1)), s.bg1c[:, c:c + 1])
        off = (c * 128) % 256
        ld(lambda E, c=c, off=off: E.dma_start(out=s.nwch[:, c:c + 1], in_=s.norm_w[off:off + 128].unsqueeze(1)), s.nwch[:, c:c + 1])
    for c in range(4):
        ld(lambda E, c=c: E.dma_start(out=s.wog2[:, c:c + 1], in_=s.Wog2[ts(c, 128), :]), s.wog2[:, c:c + 1])
        ld(lambda E, c=c: E.dma_start(out=s.bog1c[:, c:c + 1], in_=s.bog1[ts(c, 128)].unsqueeze(1)), s.bog1c[:, c:c + 1])
    ld(lambda E: E.dma_start(out=s.bres_row[0:1, 4:8], in_=s.bres.unsqueeze(0)), s.bres_row[0:1, 4:8])
    ld(lambda E: E.dma_start(out=s.bg2f[0:4, :], in_=s.bg2.unsqueeze(1)), s.bg2f[:])
    ld(lambda E: E.dma_start(out=s.bog2v[0:1, :], in_=s.bog2.unsqueeze(0)), s.bog2v[:])
    ld(lambda E: E.dma_start(out=tmp1[0:1, 0:1], in_=s.logit_temp.unsqueeze(0)), tmp1[0:1, 0:1])
    ld(lambda E: E.dma_start(out=tmp1[0:1, 8:12], in_=s.conv_res_logit.unsqueeze(0)), tmp1[0:1, 8:12])

    G = lambda fn, rd_, wr_: au.step("gpsimd", fn, rd=rd_, wr=wr_)
    G(lambda E: E.memset(s.ident[:], 0.0), [], [s.ident[:]])
    G(lambda E: E.memset(s.maskMn[:], 0.0), [], [s.maskMn[:]])
    G(lambda E: E.memset(s.maskUI[:], 0.0), [], [s.maskUI[:]])
    G(lambda E: E.memset(s.ones_col[:], 1.0), [], [s.ones_col[:]])
    G(lambda E: E.memset(s.zero64[:], 0.0), [], [s.zero64[:]])
    G(lambda E: E.memset(s.bres_row[0:1, 0:4], 0.0), [], [s.bres_row[0:1, 0:4]])
    G(lambda E: E.memset(s.constc[:, 0:1], 1e-5), [], [s.constc[:, 0:1]])
    G(lambda E: E.memset(s.constc[:, 1:2], 1e-6), [], [s.constc[:, 1:2]])
    G(lambda E: E.memset(s.constc[:, 2:3], 1536.0), [], [s.constc[:, 2:3]])
    G(lambda E: E.memset(s.constc[:, 3:4], -1.0), [], [s.constc[:, 3:4]])
    G(lambda E: E.memset(s.constc[:, 4:5], 1.0), [], [s.constc[:, 4:5]])
    G(lambda E: E.memset(s.ones128[:], 1.0), [], [s.ones128[:]])
    G(lambda E: E.affine_select(out=s.ident[:], in_=s.ident[:], compare_op=ALU.not_equal,
                                fill=1.0, base=0, pattern=[[-1, 128]], channel_multiplier=1),
      [s.ident[:]], [s.ident[:]])
    G(lambda E: E.affine_select(out=s.maskMn[:], in_=s.maskMn[:], compare_op=ALU.is_ge,
                                fill=-1.0, base=0, pattern=[[-1, 128]], channel_multiplier=1),
      [s.maskMn[:]], [s.maskMn[:]])
    G(lambda E: E.affine_select(out=s.maskUI[:], in_=s.maskUI[:], compare_op=ALU.is_ge,
                                fill=1.0, base=-1, pattern=[[-1, 128]], channel_multiplier=1),
      [s.maskUI[:]], [s.maskUI[:]])
    au.step("vector", lambda E: E.tensor_copy(s.identb[:], s.ident[:]),
            rd=[s.ident[:]], wr=[s.identb[:]])

    au.step("scalar", lambda E: E.activation(tmp1[0:1, 1:2], tmp1[0:1, 0:1], AF.Exp),
            rd=[tmp1[0:1, 0:1]], wr=[tmp1[0:1, 1:2]])
    au.step("scalar", lambda E: E.activation(tmp1[0:1, 16:20], tmp1[0:1, 8:12], AF.Sigmoid),
            rd=[tmp1[0:1, 8:12]], wr=[tmp1[0:1, 16:20]])
    au.step("vector", lambda E: E.tensor_scalar(tmp1[0:1, 20:21], tmp1[0:1, 1:2], 1.0, None,
            ALU.add), rd=[tmp1[0:1, 1:2]], wr=[tmp1[0:1, 20:21]])
    au.step("vector", lambda E: E.tensor_scalar(tmp1[0:1, 2:3], tmp1[0:1, 0:1], 0.0, 0.35,
            ALU.max, ALU.add), rd=[tmp1[0:1, 0:1]], wr=[tmp1[0:1, 2:3]])
    for it in range(7):
        au.step("scalar", lambda E: E.activation(tmp1[0:1, 22:23], tmp1[0:1, 2:3], AF.Exp,
                scale=s.constc[0:1, 3:4]), rd=[tmp1[0:1, 2:3]], wr=[tmp1[0:1, 22:23]])
        au.step("vector", lambda E: E.tensor_scalar(tmp1[0:1, 23:24], tmp1[0:1, 22:23],
                tmp1[0:1, 20:21], -1.0, ALU.mult, ALU.add),
                rd=[tmp1[0:1, 22:23], tmp1[0:1, 20:21]], wr=[tmp1[0:1, 23:24]])
        au.step("vector", lambda E: E.tensor_add(tmp1[0:1, 2:3], tmp1[0:1, 2:3],
                tmp1[0:1, 23:24]), rd=[tmp1[0:1, 2:3], tmp1[0:1, 23:24]], wr=[tmp1[0:1, 2:3]])
    au.step("vector", lambda E: E.reciprocal(tmp1[0:1, 3:4], tmp1[0:1, 2:3]),
            rd=[tmp1[0:1, 2:3]], wr=[tmp1[0:1, 3:4]])
    au.step("tensor", lambda E: E.matmul(pp[:, 0:1], s.ones_col[0:1, :], tmp1[0:1, 3:4],
            start=True, stop=True), rd=[s.ones_col[:], tmp1[0:1, 3:4]], wr=[pp[:]])
    au.step("scalar", lambda E: E.activation(tmp[:, 0:1], pp[:, 0:1], AF.Identity),
            rd=[pp[:]], wr=[tmp[:, 0:1]])
    au.step("tensor", lambda E: E.matmul(pp[:, 4:8], s.ones_col[0:1, :], tmp1[0:1, 16:20],
            start=True, stop=True), rd=[s.ones_col[:], tmp1[0:1, 16:20]], wr=[pp[:]])
    au.step("scalar", lambda E: E.activation(s.statq[:, 0:4], pp[:, 4:8], AF.Identity),
            rd=[pp[:]], wr=[s.statq[:]])
    au.step("vector", lambda E: E.tensor_scalar(s.bg2f[0:4, :], s.bg2f[0:4, :], tmp[0:4, 0:1],
            None, ALU.mult), rd=[s.bg2f[:], tmp[:, 0:1]], wr=[s.bg2f[:]])
    for c in range(NCHUNK):
        au.step("vector", lambda E, c=c: E.tensor_scalar(s.wg2[:, ts(c, 4)], s.wg2[:, ts(c, 4)],
                tmp[:, 0:1], None, ALU.mult), rd=[s.wg2[:, ts(c, 4)], tmp[:, 0:1]],
                wr=[s.wg2[:, ts(c, 4)]])
    au.emit()
    _ctx.close()


MK.phase_prep = _phase_prep
from concourse.bass_utils import run_bass_kernel_spmd

_NC_CACHE = {}


def _get_m():
    if "m" not in _NC_CACHE:
        _NC_CACHE["m"] = build_full(nseq=2, L=4096)
    return _NC_CACHE["m"]


def kernel(**inputs):
    m = _get_m()
    B, L = 16, 4096
    ncore = 8
    x = np.ascontiguousarray(inputs["L0_attn_hidden_states"], dtype=np.float32)
    ren = {
        "Wq": "L0_attn_Wq", "Wk": "L0_attn_Wk", "Wv": "L0_attn_Wv", "Wb": "L0_attn_Wb",
        "wq_conv": "L0_attn_wq_conv", "wk_conv": "L0_attn_wk_conv", "wv_conv": "L0_attn_wv_conv",
        "Wg1": "L0_attn_Wg1", "bg1": "L0_attn_bg1", "Wg2": "L0_attn_Wg2", "bg2": "L0_attn_bg2",
        "logit_temp": "L0_attn_logit_temp", "conv_res_logit": "L0_attn_conv_res_logit",
        "Wres": "L0_attn_Wres", "bres": "L0_attn_bres",
        "Wog1": "L0_attn_Wog1", "bog1": "L0_attn_bog1",
        "Wog2": "L0_attn_Wog2", "bog2": "L0_attn_bog2",
        "norm_w": "L0_attn_norm_w", "Wo": "L0_attn_Wo",
        "w_gate": "L0_ffn_w_gate", "w_up": "L0_ffn_w_up", "w_down": "L0_ffn_w_down",
    }
    shared = {k: np.ascontiguousarray(inputs[v], dtype=np.float32) for k, v in ren.items()}
    shared["fir_long"] = np.ascontiguousarray(
        inputs["L0_attn_fir_long"], dtype=np.float32).reshape(1024, 64)
    shared["fir_short"] = np.ascontiguousarray(
        inputs["L0_attn_fir_short"], dtype=np.float32).reshape(1024, 5)
    in_maps = []
    for c in range(ncore):
        im = dict(shared)
        im["x"] = np.ascontiguousarray(x[2 * c: 2 * c + 2].reshape(2 * L, 1024))
        in_maps.append(im)
    res = run_bass_kernel_spmd(m.nc, in_maps, core_ids=list(range(ncore)))
    out = np.empty((B, L, 1024), np.float32)
    for c in range(ncore):
        out[2 * c: 2 * c + 2] = np.asarray(res.results[c]["y"]).reshape(2, L, 1024)
    return out


# version 20 (final)
# speedup vs baseline: 1.0287x; 1.0083x over previous
import numpy as np
import re
from contextlib import ExitStack

import concourse.bass as bass
import concourse.mybir as mybir

F32 = mybir.dt.float32
BF16 = mybir.dt.bfloat16
FP16 = mybir.dt.float16
AF = mybir.ActivationFunctionType
ALU = mybir.AluOpType
AX = mybir.AxisListType

D = 1024
H = 4
DV = 256
NCHUNK = 8


def ts(i, n):
    return slice(i * n, (i + 1) * n)


class MK:
    def __init__(self, nseq=2, L=4096, dbg=False, dbg_out=(), dbg_in=()):
        self.dbg_out = set(dbg_out)
        self.dbg_in = set(dbg_in)
        self.nseq = nseq
        self.L = L
        self.T = nseq * L
        self.NT = self.T // 128
        self.NTS = self.L // 128
        self.dbg = dbg
        assert self.L % 512 == 0
        self.NSTR = self.L // 512

        nc = bass.Bass("TRN2", target_bir_lowering=False, detect_race_conditions=False)
        self.nc = nc
        dt = nc.dram_tensor
        self.x_in = dt("x", [self.T, D], F32, kind="ExternalInput").ap()
        for nm, shp in [("Wq", [D, D]), ("Wk", [D, D]), ("Wv", [D, D]), ("Wb", [D, H]),
                        ("wq_conv", [D, 4]), ("wk_conv", [D, 4]), ("wv_conv", [D, 4]),
                        ("fir_long", [D, 64]), ("fir_short", [D, 5]),
                        ("Wg1", [D + 16, D]), ("bg1", [D]), ("Wg2", [D, H]), ("bg2", [H]),
                        ("logit_temp", [1]), ("conv_res_logit", [H]),
                        ("Wres", [D, H]), ("bres", [H]),
                        ("Wog1", [D + 8, D // 2]), ("bog1", [D // 2]),
                        ("Wog2", [D // 2, 1]), ("bog2", [1]),
                        ("norm_w", [DV]), ("Wo", [D, D]),
                        ("w_gate", [4 * D, D]), ("w_up", [4 * D, D]), ("w_down", [D, 4 * D])]:
            setattr(self, nm, dt(nm, shp, F32, kind="ExternalInput").ap())
        self.y_out = dt("y", [self.T, D], F32, kind="ExternalOutput").ap()

        NT = self.NT
        self.XLNT = dt("XLNT", [NT // 4, 128, 4096], F32, kind=("ExternalInput" if "XLNT" in self.dbg_in else ("ExternalOutput" if "XLNT" in self.dbg_out else "Internal"))).ap()
        self.QTM = dt("QTM", [self.T, D], F32, kind=("ExternalInput" if "QTM" in self.dbg_in else ("ExternalOutput" if "QTM" in self.dbg_out else "Internal"))).ap()
        self.KTM = dt("KTM", [self.T, D], F32, kind=("ExternalInput" if "KTM" in self.dbg_in else ("ExternalOutput" if "KTM" in self.dbg_out else "Internal"))).ap()
        self.VTM = dt("VTM", [self.T, D], F32, kind=("ExternalInput" if "VTM" in self.dbg_in else ("ExternalOutput" if "VTM" in self.dbg_out else "Internal"))).ap()
        self.LSTM = dt("LSTM", [self.T, D], F32, kind=("ExternalInput" if "LSTM" in self.dbg_in else ("ExternalOutput" if "LSTM" in self.dbg_out else "Internal"))).ap()
        self.LLTM = dt("LLTM", [self.T, D], F32, kind=("ExternalInput" if "LLTM" in self.dbg_in else ("ExternalOutput" if "LLTM" in self.dbg_out else "Internal"))).ap()
        self.VCM = dt("VCM", [self.NT // 4, 128, 4608], F32, kind="Internal").ap()
        self.LLCM = dt("LLCM", [self.NT // 4, 128, 4096], F32, kind=("ExternalOutput" if "LLCM" in self.dbg_out else "Internal")).ap()
        self.DOTM = dt("DOTM", [self.T, D], F32, kind=("ExternalInput" if "DOTM" in self.dbg_in else ("ExternalOutput" if "DOTM" in self.dbg_out else "Internal"))).ap()
        self.Y1 = dt("Y1", [self.T, D], F32, kind=("ExternalInput" if "Y1" in self.dbg_in else ("ExternalOutput" if "Y1" in self.dbg_out else "Internal"))).ap()
        self.XQT = dt("XQT", [NT // 4, 128, 4096], BF16, kind=("ExternalInput" if "XQT" in self.dbg_in else ("ExternalOutput" if "XQT" in self.dbg_out else "Internal"))).ap()
        self.GQT = dt("GQT", [NT, 128, 4096], BF16, kind=("ExternalInput" if "GQT" in self.dbg_in else ("ExternalOutput" if "GQT" in self.dbg_out else "Internal"))).ap()
        self.WGT = dt("WGT", [D, 4 * D], BF16, kind=("ExternalInput" if "WGT" in self.dbg_in else ("ExternalOutput" if "WGT" in self.dbg_out else "Internal"))).ap()
        self.WUP = dt("WUP", [D, 4 * D], BF16, kind=("ExternalInput" if "WUP" in self.dbg_in else ("ExternalOutput" if "WUP" in self.dbg_out else "Internal"))).ap()
        self.WDN = dt("WDN", [4 * D, D], BF16, kind=("ExternalInput" if "WDN" in self.dbg_in else ("ExternalOutput" if "WDN" in self.dbg_out else "Internal"))).ap()

        self.stack = ExitStack()
        self._sems = {}
        self._qcounts = {}
        self._persist()

    def sb(self, name, shape, dtype=F32):
        return self.stack.enter_context(self.nc.sbuf_tensor(name, shape, dtype))

    def ps(self, name, shape, dtype=F32):
        return self.stack.enter_context(self.nc.psum_tensor(name, shape, dtype))

    def sem(self, name):
        return self.stack.enter_context(self.nc.semaphore(name))

    def _persist(self):
        s = self.sb
        self.ident = s("ident", [128, 128], F32)
        self.maskMn = s("maskMn", [128, 128], F32)
        self.maskUI = s("maskUI", [128, 128], F32)
        self.ones_col = s("ones_col", [1, 128], F32)
        self.small = s("small", [128, 64], F32)
        self.wbres = s("wbres", [128, 8 * NCHUNK], F32)
        self.bres_row = s("bres_row", [1, 8], F32)
        self.convw = s("convw", [128, 12 * NCHUNK], F32)
        self.firl = s("firl", [128, 64 * NCHUNK], F32)
        self.firs = s("firs", [128, 5 * NCHUNK], F32)
        self.wg2 = s("wg2", [128, 4 * NCHUNK], F32)
        self.bg2f = s("bg2f", [4, 1], F32)
        self.wog2 = s("wog2", [128, 1 * 4], F32)
        self.bog2v = s("bog2v", [1, 1], F32)
        self.statq = s("statq", [128, 4], F32)
        self.nwch = s("nwch", [128, NCHUNK], F32)
        self.bg1c = s("bg1c", [128, NCHUNK], F32)
        self.bog1c = s("bog1c", [128, 4], F32)
        self.beta_all = s("beta_all", [128, 8 * self.NT], F32)
        self.dq_x = s("dq_x", [128, self.NT], F32)
        self.dq_g = s("dq_g", [128, self.NT], F32)
        self.wmean = s("wmean", [128, 4], F32)
        self.wscl = s("wscl", [128, 4], F32)
        self.ones128 = s("ones128", [128, 1], F32)
        self.zero64 = s("zero64", [128, 64], F32)
        self.constc = s("constc", [128, 8], F32)
        self.identb = s("identb", [128, 128], BF16)

    def phase_prep(self):
        nc, s = self.nc, self
        with ExitStack() as ctx:
            e = ctx.enter_context
            ld = e(nc.semaphore("p_ld"))
            gp = e(nc.semaphore("p_gp"))
            ve = e(nc.semaphore("p_ve"))
            mm = e(nc.semaphore("p_mm"))
            sc = e(nc.semaphore("p_sc"))
            tmp = e(nc.sbuf_tensor("p_tmp", [128, 8], F32))
            tmp1 = e(nc.sbuf_tensor("p_tmp1", [1, 32], F32))
            pp = e(nc.psum_tensor("p_pp", [128, 16], F32))
            blk = e(nc.Block("prep"))

            NLD = 10 * NCHUNK + 2 * 4 + 5

            @blk.sync
            def _(sync):
                for c in range(NCHUNK):
                    sync.dma_start(out=s.wbres[:, c * 8:c * 8 + 4], in_=s.Wb[ts(c, 128), :]).then_inc(ld, 16)
                    sync.dma_start(out=s.wbres[:, c * 8 + 4:c * 8 + 8], in_=s.Wres[ts(c, 128), :]).then_inc(ld, 16)
                    sync.dma_start(out=s.firl[:, ts(c, 64)], in_=s.fir_long[ts(c, 128), :]).then_inc(ld, 16)
                    sync.dma_start(out=s.firs[:, ts(c, 5)], in_=s.fir_short[ts(c, 128), :]).then_inc(ld, 16)
                    sync.dma_start(out=s.wg2[:, ts(c, 4)], in_=s.Wg2[ts(c, 128), :]).then_inc(ld, 16)
                    sync.dma_start(out=s.convw[:, c * 12:c * 12 + 4], in_=s.wq_conv[ts(c, 128), :]).then_inc(ld, 16)
                    sync.dma_start(out=s.convw[:, c * 12 + 4:c * 12 + 8], in_=s.wk_conv[ts(c, 128), :]).then_inc(ld, 16)
                    sync.dma_start(out=s.convw[:, c * 12 + 8:c * 12 + 12], in_=s.wv_conv[ts(c, 128), :]).then_inc(ld, 16)
                    sync.dma_start(out=s.bg1c[:, c:c + 1], in_=s.bg1[ts(c, 128)].unsqueeze(1)).then_inc(ld, 16)
                    off = (c * 128) % 256
                    sync.dma_start(out=s.nwch[:, c:c + 1], in_=s.norm_w[off:off + 128].unsqueeze(1)).then_inc(ld, 16)
                for c in range(4):
                    sync.dma_start(out=s.wog2[:, c:c + 1], in_=s.Wog2[ts(c, 128), :]).then_inc(ld, 16)
                    sync.dma_start(out=s.bog1c[:, c:c + 1], in_=s.bog1[ts(c, 128)].unsqueeze(1)).then_inc(ld, 16)
                sync.dma_start(out=s.bres_row[0:1, 4:8], in_=s.bres.unsqueeze(0)).then_inc(ld, 16)
                sync.dma_start(out=s.bg2f[0:4, :], in_=s.bg2.unsqueeze(1)).then_inc(ld, 16)
                sync.dma_start(out=s.bog2v[0:1, :], in_=s.bog2.unsqueeze(0)).then_inc(ld, 16)
                sync.dma_start(out=tmp1[0:1, 0:1], in_=s.logit_temp.unsqueeze(0)).then_inc(ld, 16)
                sync.dma_start(out=tmp1[0:1, 8:12], in_=s.conv_res_logit.unsqueeze(0)).then_inc(ld, 16)

            @blk.gpsimd
            def _(gpsimd):
                gpsimd.memset(s.ident[:], 0.0).then_inc(gp, 1)
                gpsimd.memset(s.maskMn[:], 0.0).then_inc(gp, 1)
                gpsimd.memset(s.maskUI[:], 0.0).then_inc(gp, 1)
                gpsimd.memset(s.ones_col[:], 1.0).then_inc(gp, 1)
                gpsimd.memset(s.zero64[:], 0.0).then_inc(gp, 1)
                gpsimd.memset(s.bres_row[0:1, 0:4], 0.0).then_inc(gp, 1)
                gpsimd.memset(s.constc[:, 0:1], 1e-5).then_inc(gp, 1)
                gpsimd.memset(s.constc[:, 1:2], 1e-6).then_inc(gp, 1)
                gpsimd.memset(s.constc[:, 2:3], 1536.0).then_inc(gp, 1)
                gpsimd.memset(s.constc[:, 3:4], -1.0).then_inc(gp, 1)
                gpsimd.memset(s.constc[:, 4:5], 1.0).then_inc(gp, 1)
                gpsimd.memset(s.ones128[:], 1.0).then_inc(gp, 1)
                gpsimd.wait_ge(gp, 12)
                gpsimd.affine_select(out=s.ident[:], in_=s.ident[:], compare_op=ALU.not_equal,
                                     fill=1.0, base=0, pattern=[[-1, 128]],
                                     channel_multiplier=1).then_inc(gp, 1)
                gpsimd.affine_select(out=s.maskMn[:], in_=s.maskMn[:], compare_op=ALU.is_ge,
                                     fill=-1.0, base=0, pattern=[[-1, 128]],
                                     channel_multiplier=1).then_inc(gp, 1)
                gpsimd.affine_select(out=s.maskUI[:], in_=s.maskUI[:], compare_op=ALU.is_ge,
                                     fill=1.0, base=-1, pattern=[[-1, 128]],
                                     channel_multiplier=1).then_inc(gp, 1)

            @blk.scalar
            def _(scalar):
                scalar.wait_ge(ld, 16 * NLD)
                scalar.wait_ge(gp, 15)
                scalar.activation(tmp1[0:1, 1:2], tmp1[0:1, 0:1], AF.Exp)
                scalar.activation(tmp1[0:1, 16:20], tmp1[0:1, 8:12], AF.Sigmoid).then_inc(sc, 1)
                for it in range(7):
                    scalar.wait_ge(ve, it + 1)
                    scalar.activation(tmp1[0:1, 22:23], tmp1[0:1, 2:3], AF.Exp,
                                      scale=s.constc[0:1, 3:4]).then_inc(sc, 1)
                scalar.wait_ge(mm, 1)
                scalar.activation(tmp[:, 0:1], pp[:, 0:1], AF.Identity)
                scalar.activation(s.statq[:, 0:4], pp[:, 4:8], AF.Identity).then_inc(sc, 100)

            @blk.vector
            def _(vector):
                vector.wait_ge(gp, 15)
                vector.tensor_copy(s.identb[:], s.ident[:])
                vector.wait_ge(sc, 1)
                vector.tensor_scalar(tmp1[0:1, 20:21], tmp1[0:1, 1:2], 1.0, None, ALU.add)
                vector.tensor_scalar(tmp1[0:1, 2:3], tmp1[0:1, 0:1], 0.0, 0.35,
                                     ALU.max, ALU.add).then_inc(ve, 1)
                for it in range(7):
                    vector.wait_ge(sc, it + 2)
                    vector.tensor_scalar(tmp1[0:1, 23:24], tmp1[0:1, 22:23],
                                         tmp1[0:1, 20:21], -1.0, ALU.mult, ALU.add)
                    vector.tensor_add(tmp1[0:1, 2:3], tmp1[0:1, 2:3],
                                      tmp1[0:1, 23:24]).then_inc(ve, 1)
                vector.reciprocal(tmp1[0:1, 3:4], tmp1[0:1, 2:3]).then_inc(ve, 100)
                vector.wait_ge(sc, 108)
                vector.tensor_scalar(s.bg2f[0:4, :], s.bg2f[0:4, :], tmp[0:4, 0:1], None, ALU.mult)
                for c in range(NCHUNK):
                    vector.tensor_scalar(s.wg2[:, ts(c, 4)], s.wg2[:, ts(c, 4)],
                                         tmp[:, 0:1], None, ALU.mult)
                vector.nop().then_inc(ve, 2)

            @blk.tensor
            def _(tensor):
                tensor.wait_ge(gp, 15)
                tensor.wait_ge(ve, 108)
                tensor.matmul(pp[:, 0:1], s.ones_col[0:1, :], tmp1[0:1, 3:4],
                              start=True, stop=True)
                tensor.matmul(pp[:, 4:8], s.ones_col[0:1, :], tmp1[0:1, 16:20],
                              start=True, stop=True).then_inc(mm, 1)

    def barrier(self):
        self.nc.all_engine_barrier()

class Step:
    __slots__ = ("eng", "fn", "rd", "wr", "dma", "waits", "inc", "qkey")
    def __init__(self, eng, fn, rd, wr, dma):
        self.eng, self.fn, self.rd, self.wr, self.dma = eng, fn, rd, wr, dma


class Auto:
    ENGS = ("tensor", "vector", "scalar", "gpsimd", "sync")

    def __init__(self, m, name, coarse=False):
        self.m = m
        self.nc = m.nc
        self.name = name
        self.steps = []
        self.coarse = coarse

    @staticmethod
    def _regions(aps):
        out = []
        for a in aps:
            if a is None:
                continue
            if isinstance(a, tuple):
                if len(a) == 3:
                    out.append((a[0], 0, 1 << 30, a[1], a[2]))
                else:
                    out.append(a)
                continue
            t = a.tensor
            nm = t.name
            try:
                off = int(a.offset)
                shp = list(a.shape)
            except Exception:
                out.append((nm, 0, 1 << 30, 0, 1 << 60))
                continue
            sp = str(getattr(t, "space", ""))
            if sp.endswith("SBUF") or sp.endswith("PSUM"):
                rowlen = int(np.prod(t.shape[1:])) if len(t.shape) > 1 else 1
                width = int(np.prod(shp[1:])) if len(shp) > 1 else 1
                p_lo, c_lo = off // rowlen, off % rowlen
                p_hi, c_hi = p_lo + int(shp[0]), off % rowlen + width
                if c_hi > rowlen or p_hi > int(t.shape[0]):
                    out.append((nm, 0, 1 << 30, 0, 1 << 60))
                else:
                    out.append((nm, p_lo, p_hi, c_lo, c_hi))
            else:
                out.append((nm, 0, 1 << 30, off, off + int(np.prod(shp))))
        return out

    _PREP_BUFS = {"wbres", "firl", "firs", "wg2", "convw", "bg1c", "nwch", "wog2",
                  "bog1c", "bres_row", "bg2f", "bog2v", "p_tmp1"}

    @classmethod
    def _qkey_map(cls, name):
        if name in cls._PREP_BUFS:
            return "prep"
        m = re.match(r"b_fac\d+_([qkv])$", name)
        if m:
            return "b_fac_" + m.group(1)
        m = re.match(r"c_[qkv]t(\d)(\d)$", name)
        if m:
            return "c_ld" + m.group(1) + m.group(2)
        m = re.match(r"d_(?:ls|ll|do|vd)(\d)$", name)
        if m:
            return "d_tm" + m.group(1)
        return name

    @classmethod
    def _dma_buf(cls, rd, wr):
        for aps in (wr, rd):
            for a in aps:
                if a is None or isinstance(a, tuple):
                    continue
                if str(getattr(a.tensor, "space", "")).endswith("SBUF"):
                    return cls._qkey_map(a.tensor.name)
        return "misc"

    def step(self, eng, fn, rd=(), wr=(), dma=False):
        key = self._dma_buf(rd, wr) if dma else None
        r, w = self._regions(rd), self._regions(wr)
        co = self.coarse
        if co is True or (isinstance(co, (set, frozenset)) and
                          any(nm in co for (nm, *_x) in r + w)):
            r = [(nm, 0, 1 << 30, 0, 1 << 60) for (nm, *_x) in r]
            w = [(nm, 0, 1 << 30, 0, 1 << 60) for (nm, *_x) in w]
        st = Step(eng, fn, r, w, dma)
        st.qkey = key
        self.steps.append(st)

    def emit(self, limit=None):
        nc = self.nc
        if limit is not None:
            self.steps = self.steps[:limit]
        qname = lambda st: ("q_" + st.eng + "_" + st.qkey) if st.dma else st.eng
        counts = self.m._qcounts
        base_counts = dict(counts)
        writers = {}
        readers = {}
        for s_ in self.steps:
            q = qname(s_)
            n = counts.get(q, 0) + (16 if s_.dma else 1)
            counts[q] = n
            evt = (q, n)
            waits = {}

            def need(e2):
                q2, n2 = e2
                if s_.dma and (q2 == q or q2 == s_.eng):
                    return
                if q2.startswith("q_"):
                    n2 = counts.get(q2, 0)
                if waits.get(q2, 0) < n2:
                    waits[q2] = n2

            for (nm, pl, ph, lo, hi) in s_.rd:
                for (pl2, ph2, lo2, hi2, e2) in writers.get(nm, ()):
                    if lo < hi2 and lo2 < hi and pl < ph2 and pl2 < ph:
                        need(e2)
            for (nm, pl, ph, lo, hi) in s_.wr:
                for (pl2, ph2, lo2, hi2, e2) in writers.get(nm, ()):
                    if lo < hi2 and lo2 < hi and pl < ph2 and pl2 < ph:
                        need(e2)
                for (pl2, ph2, lo2, hi2, e2, q2) in readers.get(nm, ()):
                    if lo < hi2 and lo2 < hi and pl < ph2 and pl2 < ph:
                        need(e2)
            s_.waits = waits
            s_.inc = evt
            for (nm, pl, ph, lo, hi) in s_.wr:
                wl = writers.setdefault(nm, [])
                wl[:] = [w for w in wl
                         if not (pl <= w[0] and w[1] <= ph and lo <= w[2] and w[3] <= hi)]
                wl.append((pl, ph, lo, hi, evt))
                rl = readers.setdefault(nm, [])
                rl[:] = [r for r in rl
                         if not (pl <= r[0] and r[1] <= ph and lo <= r[2] and r[3] <= hi)]
            for (nm, pl, ph, lo, hi) in s_.rd:
                rl = readers.setdefault(nm, [])
                rl[:] = [r for r in rl
                         if not (pl <= r[0] and r[1] <= ph and lo <= r[2] and r[3] <= hi
                                 and r[5] == q)]
                rl.append((pl, ph, lo, hi, evt, q))
        with ExitStack() as ctx:
            e = ctx.enter_context
            sems = self.m._sems
            for q in counts:
                if q not in sems:
                    sems[q] = self.m.stack.enter_context(nc.semaphore(f"au_{q}"))
            blk = e(nc.Block(self.name))
            by_eng = {en: [s_ for s_ in self.steps if s_.eng == en] for en in self.ENGS}
            last_waited = {}

            def body(engine, en):
                for s_ in by_eng[en]:
                    for q2, n2 in sorted(s_.waits.items()):
                        if last_waited.get((en, q2), -1) >= n2:
                            continue
                for s_ in []:
                    pass

            def run_body(engine, en):
                for s_ in by_eng[en]:
                    for q2, n2 in sorted(s_.waits.items()):
                        if last_waited.get((en, q2), -1) >= n2:
                            continue
                        engine.wait_ge(sems[q2], n2)
                        last_waited[(en, q2)] = n2
                    ins = s_.fn(engine)
                    assert ins is not None, f"step fn on {en} returned None in {self.name}"
                    ins.then_inc(sems[s_.inc[0]], 16 if s_.dma else 1)
                pref = "q_" + en + "_"
                for qn in counts:
                    if qn.startswith(pref) and counts[qn] != base_counts.get(qn, 0):
                        engine.wait_ge(sems[qn], counts[qn])

            for en in self.ENGS:
                if by_eng[en]:
                    getattr(blk, en)(lambda engine, _en=en: run_body(engine, _en))

    def finish_wait(self):
        pass


def _phase_d1(self):
    nc, s = self.nc, self
    NT = self.NT
    self.O2T = nc.dram_tensor("O2T", [NT // 4, 128, 4096], F32, kind=("ExternalInput" if "O2T" in self.dbg_in else ("ExternalOutput" if "O2T" in self.dbg_out else "Internal"))).ap()
    au = Auto(self, "phD1", coarse=getattr(self, "d1_coarse", False))
    dbg = "D1DBG" in self.dbg_out
    if dbg:
        self.DBG_statT = nc.dram_tensor("DBG_statT", [16, 512], F32, kind="ExternalOutput").ap()
        self.DBG_lgT = nc.dram_tensor("DBG_lgT", [4, 512], F32, kind="ExternalOutput").ap()
        self.DBG_ltm = nc.dram_tensor("DBG_ltm", [4, 128, 24], F32, kind="ExternalOutput").ap()
        self.DBG_ob = nc.dram_tensor("DBG_ob", [4, 128, 1024], F32, kind="ExternalOutput").ap()
        self.DBG_ogsT = nc.dram_tensor("DBG_ogsT", [1, 512], F32, kind="ExternalOutput").ap()
        self.DBG_ogT = nc.dram_tensor("DBG_ogT", [8, 512], F32, kind="ExternalOutput").ap()
    _ctx = ExitStack()
    st_ = _ctx.enter_context
    wg1sb = st_(nc.sbuf_tensor("d_wg1", [128, 9 * 1024], F32))
    wog1sb = st_(nc.sbuf_tensor("d_wog1", [128, 9 * 512], F32))
    xlsb = st_(nc.sbuf_tensor("d_xl", [128, 4096], F32))
    TN = ("ls", "ll", "do", "vd")
    SRC = {"ls": s.LSTM, "ll": s.LLTM, "do": s.DOTM, "vd": s.VTM}
    tmb = {nm: [st_(nc.sbuf_tensor(f"d_{nm}{q}", [128, D], F32)) for q in range(4)]
           for nm in TN}
    stat4 = [st_(nc.sbuf_tensor(f"d_stat{q}", [128, 80], F32)) for q in range(4)]
    trash2 = [st_(nc.sbuf_tensor(f"d_tr{p}", [128, 1024], F32)) for p in range(2)]
    vst = st_(nc.sbuf_tensor("d_vst", [128, 16], F32))
    statT = st_(nc.sbuf_tensor("d_statT", [16, 512], F32))
    ogT = st_(nc.sbuf_tensor("d_ogT", [8, 512], F32))
    g1sb = [st_(nc.sbuf_tensor(f"d_g1{p}", [128, 512], F32)) for p in range(2)]
    ltm = [st_(nc.sbuf_tensor(f"d_ltm{p}", [128, 24], F32)) for p in range(4)]
    lgT = st_(nc.sbuf_tensor("d_lgT", [4, 512], F32))
    ogsT = st_(nc.sbuf_tensor("d_ogsT", [1, 512], F32))
    ob = [st_(nc.sbuf_tensor(f"d_ob{p}", [128, D], F32)) for p in range(4)]
    o2stg = st_(nc.sbuf_tensor("d_o2stg", [128, 4096], F32))
    pst = [st_(nc.psum_tensor(f"d_pst{p}", [128, 512], F32)) for p in range(2)]
    pg = [st_(nc.psum_tensor(f"d_pg{p}", [128, 512], F32)) for p in range(2)]
    plg = st_(nc.psum_tensor("d_plg", [128, 512], F32))
    ptr = [st_(nc.psum_tensor(f"d_ptr{p}", [128, 512], F32)) for p in range(2)]

    for c in range(NCHUNK):
        au.step("sync", lambda E, c=c: E.dma_start(out=wg1sb[:, ts(c, 1024)],
                in_=s.Wg1[ts(c, 128), :]), wr=[wg1sb[:, ts(c, 1024)]], dma=True)
    au.step("sync", lambda E: E.dma_start(out=wg1sb[0:16, 8 * 1024:9 * 1024],
            in_=s.Wg1[1024:1040, :]), wr=[wg1sb[0:16, 8 * 1024:9 * 1024]], dma=True)
    for c in range(NCHUNK):
        au.step("sync", lambda E, c=c: E.dma_start(out=wog1sb[:, ts(c, 512)],
                in_=s.Wog1[ts(c, 128), :]), wr=[wog1sb[:, ts(c, 512)]], dma=True)
    au.step("sync", lambda E: E.dma_start(out=wog1sb[0:8, 8 * 512:9 * 512],
            in_=s.Wog1[1024:1032, :]), wr=[wog1sb[0:8, 8 * 512:9 * 512]], dma=True)

    def stats_into(st, src_ap, W, cols, trash):
        base = bass.AP(src_ap.tensor, src_ap.offset, [[D, 128], [1, D]])
        for h in range(4):
            st("scalar", lambda E, h=h: E.activation(trash[:, ts(h, 256)],
                    base[:, ts(h, 256)], AF.Square, accum_out=W[:, 28 + h:29 + h]),
                    rd=[src_ap], wr=[trash[:, ts(h, 256)], W[:, 28 + h:29 + h]])
            st("vector", lambda E, h=h: E.reduce_sum(W[:, 24 + h:25 + h],
                    base[:, ts(h, 256)], AX.X), rd=[src_ap], wr=[W[:, 24 + h:25 + h]])
        st("vector", lambda E: E.reduce_sum(W[:, 32:33], src_ap, AX.X,
                apply_absolute_value=True), rd=[src_ap], wr=[W[:, 32:33]])
        st("vector", lambda E: E.tensor_scalar(W[:, 33:37], W[:, 24:28], 1.0 / 256, None,
                ALU.mult), rd=[W[:, 24:28]], wr=[W[:, 33:37]])
        st("vector", lambda E: E.reduce_sum(W[:, 37:38], W[:, 33:37], AX.X),
                rd=[W[:, 33:37]], wr=[W[:, 37:38]])
        st("vector", lambda E: E.tensor_scalar(W[:, cols:cols + 1], W[:, 37:38], 0.25,
                None, ALU.mult), rd=[W[:, 37:38]], wr=[W[:, cols:cols + 1]])
        st("vector", lambda E: E.tensor_mul(W[:, 38:42], W[:, 33:37], W[:, 33:37]),
                rd=[W[:, 33:37]], wr=[W[:, 38:42]])
        st("vector", lambda E: E.tensor_scalar(W[:, 42:46], W[:, 28:32], 1.0 / 256, None,
                ALU.mult), rd=[W[:, 28:32]], wr=[W[:, 42:46]])
        st("vector", lambda E: E.tensor_sub(W[:, 42:46], W[:, 42:46], W[:, 38:42]),
                rd=[W[:, 38:42], W[:, 42:46]], wr=[W[:, 42:46]])
        st("vector", lambda E: E.reduce_sum(W[:, 46:47], W[:, 42:46], AX.X),
                rd=[W[:, 42:46]], wr=[W[:, 46:47]])
        st("vector", lambda E: E.tensor_scalar(W[:, cols + 1:cols + 2], W[:, 46:47], 0.25,
                None, ALU.mult), rd=[W[:, 46:47]], wr=[W[:, cols + 1:cols + 2]])
        st("vector", lambda E: E.tensor_scalar(W[:, cols + 2:cols + 3], W[:, 32:33],
                1.0 / 1024, None, ALU.mult), rd=[W[:, 32:33]], wr=[W[:, cols + 2:cols + 3]])
        st("scalar", lambda E: E.activation(W[:, 48:52], W[:, 28:32], AF.Sqrt),
                rd=[W[:, 28:32]], wr=[W[:, 48:52]])
        st("vector", lambda E: E.reduce_sum(W[:, 52:53], W[:, 48:52], AX.X),
                rd=[W[:, 48:52]], wr=[W[:, 52:53]])
        st("vector", lambda E: E.tensor_scalar(W[:, cols + 3:cols + 4], W[:, 52:53], 0.25,
                None, ALU.mult), rd=[W[:, 52:53]], wr=[W[:, cols + 3:cols + 4]])

    def rr_merge(gens):
        pos = [0] * len(gens)
        rem = sum(len(g_) for g_ in gens)
        while rem:
            for k in range(len(gens)):
                if pos[k] < len(gens[k]):
                    for (eng, fn, rd, wr, dma) in gens[k][pos[k]]:
                        au.step(eng, fn, rd=rd, wr=wr, dma=dma)
                    pos[k] += 1
                    rem -= 1

    def mk_st():
        groups = []
        def _st(eng, fn, rd=(), wr=(), dma=False, glue=False):
            if glue and groups:
                groups[-1].append((eng, fn, rd, wr, dma))
            else:
                groups.append([(eng, fn, rd, wr, dma)])
        return groups, _st

    def stats_gens(stp):
        gens = []
        for q in range(4):
            tt = stp * 4 + q
            steps, st = mk_st()
            for nm in TN:
                st("sync", lambda E, nm=nm, tt=tt, q=q: E.dma_start(
                    out=tmb[nm][q][:], in_=SRC[nm][ts(tt, 128), :]),
                    wr=[tmb[nm][q][:]], dma=True)
            for j, nm in enumerate(TN):
                stats_into(st, tmb[nm][q][:], stat4[q], j * 4, trash2[q % 2])
            st("vector", lambda E, q=q: E.tensor_copy(vst[:, ts(q, 4)],
                    stat4[q][:, 12:16]), rd=[stat4[q][:, 12:16]], wr=[vst[:, ts(q, 4)]])
            st("tensor", lambda E, q=q: E.transpose(pst[q % 2][0:16, 0:128],
                    stat4[q][:, 0:16], s.ident[:]), rd=[stat4[q][:, 0:16]],
                    wr=[pst[q % 2][:]])
            st("vector", lambda E, q=q: E.tensor_copy(statT[:, ts(q, 128)],
                    pst[q % 2][0:16, 0:128]), rd=[pst[q % 2][:]],
                    wr=[statT[:, ts(q, 128)]], glue=True)
            gens.append(steps)
        return gens

    for stp in range(NT // 4):
        au.step("sync", lambda E, stp=stp: E.dma_start(out=xlsb[:], in_=s.XLNT[stp]),
                wr=[xlsb[:]], dma=True)
        rr_merge(stats_gens(stp))
        for gsl in range(NCHUNK):
            def mmg(E, gsl=gsl):
                for c in range(NCHUNK):
                    E.matmul(pg[gsl % 2][:], wg1sb[:, c * 1024 + gsl * 128: c * 1024 + (gsl + 1) * 128],
                             xlsb[:, ts(c, 512)], start=(c == 0), stop=False)
                return E.matmul(pg[gsl % 2][:], wg1sb[0:16, 8 * 1024 + gsl * 128: 8 * 1024 + (gsl + 1) * 128],
                                statT[:], start=False, stop=True)
            au.step("tensor", mmg, rd=[xlsb[:], statT[:], (wg1sb.ap().tensor.name, 0, 1 << 40)],
                    wr=[pg[gsl % 2][:]])
            au.step("scalar", lambda E, gsl=gsl: E.activation(g1sb[gsl % 2][:], pg[gsl % 2][:],
                    AF.Gelu, bias=s.bg1c[:, gsl:gsl + 1]),
                    rd=[pg[gsl % 2][:]], wr=[g1sb[gsl % 2][:]])
            au.step("tensor", lambda E, gsl=gsl: E.matmul(plg[0:4, :], s.wg2[:, ts(gsl, 4)],
                    g1sb[gsl % 2][:], start=(gsl == 0), stop=(gsl == 7)),
                    rd=[g1sb[gsl % 2][:], s.wg2[:]], wr=[plg[:]])
        au.step("scalar", lambda E: E.activation(lgT[:], plg[0:4, :], AF.Identity,
                bias=s.bg2f[:, 0:1]), rd=[plg[:]], wr=[lgT[:]])
        if dbg and stp == 0:
            au.step("gpsimd", lambda E: E.dma_start(out=self.DBG_statT, in_=statT[:]),
                    rd=[statT[:]], dma=True)
            au.step("gpsimd", lambda E: E.dma_start(out=self.DBG_lgT, in_=lgT[:]),
                    rd=[lgT[:]], dma=True)
        gens = []
        for q in range(4):
            tt = stp * 4 + q
            lt = ltm[q]
            steps, st = mk_st()
            st("tensor", lambda E, q=q: E.transpose(pst[q % 2][0:128, 0:4],
                    lgT[:, ts(q, 128)], s.ident[0:4, 0:4]),
                    rd=[lgT[:, ts(q, 128)]], wr=[pst[q % 2][:]])
            st("vector", lambda E, lt=lt, q=q: E.tensor_copy(lt[:, 0:4],
                    pst[q % 2][0:128, 0:4]), rd=[pst[q % 2][:]], wr=[lt[:, 0:4]], glue=True)
            st("vector", lambda E, lt=lt: E.reduce_max(lt[:, 12:13], lt[:, 0:4], AX.X,
                    negate=True), rd=[lt[:, 0:4]], wr=[lt[:, 12:13]])
            st("scalar", lambda E, lt=lt: E.activation(lt[:, 4:8], lt[:, 0:4], AF.Exp,
                    bias=lt[:, 12:13]), rd=[lt[:, 0:4], lt[:, 12:13]], wr=[lt[:, 4:8]])
            st("vector", lambda E, lt=lt: E.reduce_sum(lt[:, 13:14], lt[:, 4:8], AX.X),
                    rd=[lt[:, 4:8]], wr=[lt[:, 13:14]])
            st("vector", lambda E, lt=lt: E.reciprocal(lt[:, 14:15], lt[:, 13:14]),
                    rd=[lt[:, 13:14]], wr=[lt[:, 14:15]])
            st("vector", lambda E, lt=lt: E.tensor_scalar(lt[:, 4:8], lt[:, 4:8],
                    lt[:, 14:15], 0.02, ALU.mult, ALU.max),
                    rd=[lt[:, 4:8], lt[:, 14:15]], wr=[lt[:, 4:8]])
            st("vector", lambda E, lt=lt: E.reduce_sum(lt[:, 15:16], lt[:, 4:8], AX.X),
                    rd=[lt[:, 4:8]], wr=[lt[:, 15:16]])
            st("vector", lambda E, lt=lt: E.reciprocal(lt[:, 16:17], lt[:, 15:16]),
                    rd=[lt[:, 15:16]], wr=[lt[:, 16:17]])
            st("vector", lambda E, lt=lt: E.tensor_scalar(lt[:, 4:8], lt[:, 4:8],
                    lt[:, 16:17], None, ALU.mult),
                    rd=[lt[:, 4:8], lt[:, 16:17]], wr=[lt[:, 4:8]])
            st("vector", lambda E, lt=lt, tt=tt: E.tensor_mul(lt[:, 8:12], s.statq[:, 0:4],
                    s.beta_all[:, tt * 8 + 4: tt * 8 + 8]),
                    rd=[s.statq[:], s.beta_all[:, tt * 8 + 4: tt * 8 + 8]], wr=[lt[:, 8:12]])
            st("vector", lambda E, lt=lt: E.tensor_scalar(lt[:, 8:12], lt[:, 8:12],
                    lt[:, 4:5], None, ALU.add), rd=[lt[:, 4:5], lt[:, 8:12]], wr=[lt[:, 8:12]])
            for h in range(4):
                st("vector", lambda E, h=h, q=q, lt=lt: E.tensor_scalar(
                    ob[q][:, ts(h, 256)], tmb["ls"][q][:, ts(h, 256)], lt[:, 8 + h: 9 + h],
                    None, ALU.mult), rd=[tmb["ls"][q][:, ts(h, 256)], lt[:, 8 + h: 9 + h]],
                    wr=[ob[q][:, ts(h, 256)]])
            for nm, pc in (("ll", 5), ("do", 6), ("vd", 7)):
                st("vector", lambda E, nm=nm, pc=pc, q=q, lt=lt: E.scalar_tensor_tensor(
                    ob[q][:], tmb[nm][q][:], lt[:, pc:pc + 1], ob[q][:], ALU.mult, ALU.add),
                    rd=[tmb[nm][q][:], lt[:, pc:pc + 1], ob[q][:]], wr=[ob[q][:]])
            if dbg and stp == 0:
                st("gpsimd", lambda E, q=q: E.dma_start(out=self.DBG_ob[q],
                        in_=ob[q][:]), rd=[ob[q][:]], dma=True)
                st("gpsimd", lambda E, q=q: E.dma_start(out=self.DBG_ltm[q],
                        in_=ltm[q][:]), rd=[ltm[q][:]], dma=True)
            stats_into(st, ob[q][:], stat4[q], 16, o2stg[:, ts(q, 1024)])
            st("vector", lambda E, q=q: E.tensor_copy(stat4[q][:, 20:24],
                    vst[:, ts(q, 4)]), rd=[vst[:, ts(q, 4)]], wr=[stat4[q][:, 20:24]])
            st("tensor", lambda E, q=q: E.transpose(pst[q % 2][0:8, 0:128],
                    stat4[q][:, 16:24], s.ident[:]), rd=[stat4[q][:, 16:24]],
                    wr=[pst[q % 2][:]])
            st("vector", lambda E, q=q: E.tensor_copy(ogT[:, ts(q, 128)],
                    pst[q % 2][0:8, 0:128]), rd=[pst[q % 2][:]], wr=[ogT[:, ts(q, 128)]],
                    glue=True)
            gens.append(steps)
        rr_merge(gens)
        for osl in range(4):
            def mmo(E, osl=osl):
                for c in range(NCHUNK):
                    E.matmul(pg[osl % 2][:], wog1sb[:, c * 512 + osl * 128: c * 512 + (osl + 1) * 128],
                             xlsb[:, ts(c, 512)], start=(c == 0), stop=False)
                return E.matmul(pg[osl % 2][:], wog1sb[0:8, 8 * 512 + osl * 128: 8 * 512 + (osl + 1) * 128],
                                ogT[:], start=False, stop=True)
            au.step("tensor", mmo, rd=[xlsb[:], ogT[:], (wog1sb.ap().tensor.name, 0, 1 << 40)],
                    wr=[pg[osl % 2][:]])
            au.step("scalar", lambda E, osl=osl: E.activation(g1sb[osl % 2][:], pg[osl % 2][:],
                    AF.Gelu, bias=s.bog1c[:, osl:osl + 1]),
                    rd=[pg[osl % 2][:]], wr=[g1sb[osl % 2][:]])
            au.step("tensor", lambda E, osl=osl: E.matmul(plg[32:33, :], s.wog2[:, osl:osl + 1],
                    g1sb[osl % 2][:], start=(osl == 0), stop=(osl == 3)),
                    rd=[g1sb[osl % 2][:], s.wog2[:]], wr=[plg[:]])
        au.step("scalar", lambda E: E.activation(ogsT[:], plg[32:33, :], AF.Sigmoid,
                bias=s.bog2v[:, 0:1]), rd=[plg[:]], wr=[ogsT[:]])
        if dbg and stp == 0:
            au.step("gpsimd", lambda E: E.dma_start(out=self.DBG_ogsT, in_=ogsT[:]),
                    rd=[ogsT[:]], dma=True)
            au.step("gpsimd", lambda E: E.dma_start(out=self.DBG_ogT, in_=ogT[:]),
                    rd=[ogT[:]], dma=True)
        gens = []
        for q in range(4):
            lt = ltm[q]
            W = stat4[q]
            steps, st = mk_st()
            st("tensor", lambda E, q=q: E.transpose(pst[q % 2][0:128, 4:5],
                    ogsT[:, ts(q, 128)], s.ident[0:1, 0:1]),
                    rd=[ogsT[:, ts(q, 128)]], wr=[pst[q % 2][:]])
            st("vector", lambda E, lt=lt, q=q: E.tensor_copy(lt[:, 17:18],
                    pst[q % 2][0:128, 4:5]), rd=[pst[q % 2][:]], wr=[lt[:, 17:18]], glue=True)
            st("vector", lambda E, q=q, lt=lt: E.tensor_scalar(ob[q][:], ob[q][:],
                    lt[:, 17:18], None, ALU.mult), rd=[ob[q][:], lt[:, 17:18]], wr=[ob[q][:]])
            for h in range(4):
                st("scalar", lambda E, q=q, h=h, W=W: E.activation(
                    o2stg[:, q * 1024 + h * 256: q * 1024 + (h + 1) * 256],
                    ob[q][:, ts(h, 256)], AF.Square, accum_out=W[:, 64 + h:65 + h]),
                    rd=[ob[q][:, ts(h, 256)]],
                    wr=[o2stg[:, q * 1024 + h * 256: q * 1024 + (h + 1) * 256],
                        W[:, 64 + h:65 + h]])
            st("vector", lambda E, W=W: E.tensor_scalar(W[:, 64:68],
                    W[:, 64:68], 1.0 / 256, None, ALU.mult),
                    rd=[W[:, 64:68]], wr=[W[:, 64:68]])
            st("scalar", lambda E, W=W: E.activation(W[:, 68:72], W[:, 64:68],
                    AF.Sqrt, bias=s.constc[:, 0:1]), rd=[W[:, 64:68]], wr=[W[:, 68:72]])
            st("vector", lambda E, W=W: E.reciprocal(W[:, 72:76], W[:, 68:72]),
                    rd=[W[:, 68:72]], wr=[W[:, 72:76]])
            for h in range(4):
                st("vector", lambda E, h=h, q=q, W=W: E.tensor_scalar(ob[q][:, ts(h, 256)],
                        ob[q][:, ts(h, 256)], W[:, 72 + h: 73 + h], None, ALU.mult),
                        rd=[ob[q][:, ts(h, 256)], W[:, 72 + h: 73 + h]],
                        wr=[ob[q][:, ts(h, 256)]])
            for c in range(NCHUNK):
                b = (q + c) % 2
                st("tensor", lambda E, c=c, q=q, b=b: E.transpose(
                    ptr[b][:, ts(c % 4, 128)], ob[q][:, ts(c, 128)], s.ident[:]),
                    rd=[ob[q][:, ts(c, 128)]], wr=[ptr[b][:]])
                st("vector" if c % 2 == 0 else "scalar",
                        lambda E, c=c, q=q, b=b: (E.tensor_copy if c % 2 == 0 else (lambda o_, i_: E.activation(o_, i_, AF.Identity)))(
                            o2stg[:, c * 512 + q * 128: c * 512 + (q + 1) * 128],
                            ptr[b][:, ts(c % 4, 128)]),
                        rd=[ptr[b][:]], wr=[o2stg[:, c * 512 + q * 128: c * 512 + (q + 1) * 128]],
                        glue=True)
            gens.append(steps)
        rr_merge(gens)
        au.step("gpsimd", lambda E, stp=stp: E.dma_start(out=self.O2T[stp], in_=o2stg[:]),
                rd=[o2stg[:]], dma=True)
    au.emit(limit=getattr(self, "d1_limit", None))
    _ctx.close()


MK.phase_d1 = _phase_d1


def _phase_w(self):
    nc, s = self.nc, self
    au = Auto(self, "phW")
    _ctx = ExitStack()
    st_ = _ctx.enter_context
    wch = [st_(nc.sbuf_tensor(f"w_ch{p}", [128, 4096], F32)) for p in range(2)]
    tern = st_(nc.sbuf_tensor("w_tern", [128, 4096], F32))
    tb = st_(nc.sbuf_tensor("w_tb", [128, 4096], BF16))
    red = st_(nc.sbuf_tensor("w_red", [128, 40], F32))
    sc1 = st_(nc.sbuf_tensor("w_sc1", [1, 8], F32))
    stg = st_(nc.sbuf_tensor("w_stg", [128, 32768], BF16))
    pw = st_(nc.psum_tensor("w_pw", [128, 512], F32))
    pt = [st_(nc.psum_tensor(f"w_pt{p}", [128, 256], BF16)) for p in range(2)]

    WS = [(s.w_gate, s.WGT, 32, 1024, 0), (s.w_up, s.WUP, 32, 1024, 1),
          (s.w_down, s.WDN, 8, 4096, 2)]
    for w_ap, dst, nch, width, widx in WS:
        ncols = width // 1024
        for c in range(nch):
            au.step("sync", lambda E, c=c, w_ap=w_ap, width=width: E.dma_start(
                out=wch[c % 2][:, 0:width], in_=w_ap[ts(c, 128), :]),
                wr=[wch[c % 2][:, 0:width]], dma=True)
            au.step("vector", lambda E, c=c, width=width: E.reduce_sum(
                red[:, c:c + 1], wch[c % 2][:, 0:width], AX.X, apply_absolute_value=True),
                rd=[wch[c % 2][:, 0:width]], wr=[red[:, c:c + 1]])
        au.step("vector", lambda E, nch=nch: E.reduce_sum(red[:, 32:33], red[:, 0:nch], AX.X),
                rd=[red[:, 0:32]], wr=[red[:, 32:33]])
        au.step("tensor", lambda E: E.matmul(pw[0:1, 0:1], red[:, 32:33], s.ones128[:],
                start=True, stop=True), rd=[red[:, 32:33], s.ones128[:]], wr=[pw[:]])
        au.step("vector", lambda E: E.tensor_scalar(sc1[0:1, 0:1], pw[0:1, 0:1],
                1.0 / (4096 * 1024), 1e-5, ALU.mult, ALU.max), rd=[pw[:]], wr=[sc1[0:1, 0:1]])
        au.step("vector", lambda E: E.reciprocal(sc1[0:1, 1:2], sc1[0:1, 0:1]),
                rd=[sc1[0:1, 0:1]], wr=[sc1[0:1, 1:2]])
        au.step("tensor", lambda E: E.matmul(pw[0:128, 1:2], s.ones_col[0:1, :], sc1[0:1, 0:1],
                start=True, stop=True), rd=[sc1[0:1, 0:1], s.ones_col[:]], wr=[pw[:]])
        au.step("tensor", lambda E: E.matmul(pw[0:128, 2:3], s.ones_col[0:1, :], sc1[0:1, 1:2],
                start=True, stop=True), rd=[sc1[0:1, 1:2], s.ones_col[:]], wr=[pw[:]])
        au.step("vector", lambda E, widx=widx: E.tensor_copy(s.wmean[:, widx:widx + 1],
                pw[0:128, 1:2]), rd=[pw[:]], wr=[s.wmean[:, widx:widx + 1]])
        au.step("vector", lambda E, widx=widx: E.tensor_copy(s.wscl[:, widx:widx + 1],
                pw[0:128, 2:3]), rd=[pw[:]], wr=[s.wscl[:, widx:widx + 1]])
        for c in range(nch):
            au.step("sync", lambda E, c=c, w_ap=w_ap, width=width: E.dma_start(
                out=wch[c % 2][:, 0:width], in_=w_ap[ts(c, 128), :]),
                wr=[wch[c % 2][:, 0:width]], dma=True)
            au.step("vector", lambda E, c=c, width=width, widx=widx: E.tensor_scalar(
                tern[:, 0:width], wch[c % 2][:, 0:width], s.wscl[:, widx:widx + 1],
                None, ALU.mult), rd=[wch[c % 2][:, 0:width], s.wscl[:, widx:widx + 1]],
                wr=[tern[:, 0:width]])
            au.step("vector", lambda E, width=width: E.tensor_scalar(
                tb[:, 0:width], tern[:, 0:width], 0.5, None, ALU.is_ge),
                rd=[tern[:, 0:width]], wr=[tb[:, 0:width]])
            au.step("vector", lambda E, width=width: E.tensor_scalar(
                tern[:, 0:width], tern[:, 0:width], -0.5, None, ALU.is_le),
                rd=[tern[:, 0:width]], wr=[tern[:, 0:width]])
            au.step("vector", lambda E, width=width: E.tensor_sub(
                tb[:, 0:width], tern[:, 0:width], tb[:, 0:width]),
                rd=[tern[:, 0:width], tb[:, 0:width]], wr=[tb[:, 0:width]])
            for b in range(width // 128):
                col = b * (nch * 128) + c * 128
                au.step("tensor", lambda E, b=b: E.transpose(pt[b % 2][:, 0:128],
                        tb[:, ts(b, 128)], s.identb[:]), rd=[tb[:, ts(b, 128)]], wr=[pt[b % 2][:]])
                au.step("vector" if b % 2 == 0 else "scalar",
                        lambda E, b=b, col=col: (E.tensor_copy if b % 2 == 0 else (lambda o_, i_: E.activation(o_, i_, AF.Identity)))(
                            stg[:, col:col + 128], pt[b % 2][:, 0:128]),
                        rd=[pt[b % 2][:]], wr=[stg[:, col:col + 128]])
        for b in range(width // 128):
            au.step("gpsimd", lambda E, b=b, dst=dst, nch=nch: E.dma_start(
                out=dst[ts(b, 128), :], in_=stg[:, b * (nch * 128):(b + 1) * (nch * 128)]),
                rd=[stg[:, b * (nch * 128):(b + 1) * (nch * 128)]], dma=True)
    au.emit()
    _ctx.close()


MK.phase_w = _phase_w


def _phase_d2(self):
    nc, s = self.nc, self
    NT = self.NT
    au = Auto(self, "phD2")
    _ctx = ExitStack()
    st_ = _ctx.enter_context
    wo_sb = st_(nc.sbuf_tensor("d2_wo", [128, 8192], F32))
    o2sb = st_(nc.sbuf_tensor("d2_o2", [128, 4096], F32))
    xb = [st_(nc.sbuf_tensor(f"d2_x{p}", [128, D], F32)) for p in range(2)]
    y1b = [st_(nc.sbuf_tensor(f"d2_y1{p}", [128, D], F32)) for p in range(2)]
    xl2 = [st_(nc.sbuf_tensor(f"d2_xl2{p}", [128, D], F32)) for p in range(2)]
    h16 = [st_(nc.sbuf_tensor(f"d2_h16{p}", [128, D], FP16)) for p in range(2)]
    sqt = [st_(nc.sbuf_tensor(f"d2_sqt{p}", [128, D], F32)) for p in range(2)]
    wk = [st_(nc.sbuf_tensor(f"d2_wk{p}", [128, 16], F32)) for p in range(2)]
    xqstg = st_(nc.sbuf_tensor("d2_xqstg", [128, 4096], BF16))
    xq16 = [st_(nc.sbuf_tensor(f"d2_xq16{p}", [128, D], BF16)) for p in range(2)]
    py = [[st_(nc.psum_tensor(f"d2_py{n}{p}", [128, 512], F32)) for p in range(2)]
          for n in range(2)]
    ptq = [st_(nc.psum_tensor(f"d2_ptq{p}", [128, 128], BF16)) for p in range(2)]

    for c in range(NCHUNK):
        au.step("sync", lambda E, c=c: E.dma_start(out=wo_sb[:, ts(c, 1024)],
                in_=s.Wo[ts(c, 128), :]), wr=[wo_sb[:, ts(c, 1024)]], dma=True)
        au.step("vector", lambda E, c=c: E.tensor_scalar(wo_sb[:, ts(c, 1024)],
                wo_sb[:, ts(c, 1024)], s.nwch[:, c:c + 1], None, ALU.mult),
                rd=[wo_sb[:, ts(c, 1024)], s.nwch[:, c:c + 1]], wr=[wo_sb[:, ts(c, 1024)]])

    def mk_st():
        groups = []
        def _st(eng, fn, rd=(), wr=(), dma=False, glue=False):
            if glue and groups:
                groups[-1].append((eng, fn, rd, wr, dma))
            else:
                groups.append([(eng, fn, rd, wr, dma)])
        return groups, _st

    def rr_merge(gens):
        pos = [0] * len(gens)
        rem = sum(len(g_) for g_ in gens)
        while rem:
            for k in range(len(gens)):
                if pos[k] < len(gens[k]):
                    for (eng, fn, rd, wr, dma) in gens[k][pos[k]]:
                        au.step(eng, fn, rd=rd, wr=wr, dma=dma)
                    pos[k] += 1
                    rem -= 1

    def tt_steps(tt):
        p = tt % 2
        q = tt % 4
        steps, st = mk_st()
        st("sync", lambda E, tt=tt, p=p: E.dma_start(out=xb[p][:],
                in_=s.x_in[ts(tt, 128), :]), wr=[xb[p][:]], dma=True)
        for nch in range(2):
            def mmwo(E, nch=nch, q=q, p=p):
                for c in range(NCHUNK):
                    ins = E.matmul(py[nch][p][:], o2sb[:, c * 512 + q * 128: c * 512 + (q + 1) * 128],
                                   wo_sb[:, c * 1024 + nch * 512: c * 1024 + (nch + 1) * 512],
                                   start=(c == 0), stop=(c == 7))
                return ins
            st("tensor", mmwo, rd=[o2sb[:], (wo_sb.ap().tensor.name, 0, 1 << 40)],
                    wr=[py[nch][p][:]])
            st("vector", lambda E, nch=nch, p=p: E.scalar_tensor_tensor(
                y1b[p][:, ts(nch, 512)], py[nch][p][:], 1.0, xb[p][:, ts(nch, 512)],
                ALU.mult, ALU.add), rd=[py[nch][p][:], xb[p][:, ts(nch, 512)]],
                wr=[y1b[p][:, ts(nch, 512)]])
        st("gpsimd", lambda E, tt=tt, p=p: E.dma_start(out=s.Y1[ts(tt, 128), :],
                in_=y1b[p][:]), rd=[y1b[p][:]], dma=True)
        W = wk[p]
        st("scalar", lambda E, p=p, W=W: E.activation(sqt[p][:], y1b[p][:], AF.Square,
                accum_out=W[:, 0:1]), rd=[y1b[p][:]], wr=[sqt[p][:], W[:, 0:1]])
        st("vector", lambda E, p=p, W=W: E.reduce_sum(W[:, 1:2], y1b[p][:], AX.X,
                negate=True), rd=[y1b[p][:]], wr=[W[:, 1:2]])
        st("vector", lambda E, W=W: E.tensor_scalar(W[:, 2:3], W[:, 1:2], 1.0 / D,
                None, ALU.mult), rd=[W[:, 1:2]], wr=[W[:, 2:3]])
        st("vector", lambda E, W=W: E.tensor_mul(W[:, 3:4], W[:, 2:3], W[:, 2:3]),
                rd=[W[:, 2:3]], wr=[W[:, 3:4]])
        st("vector", lambda E, W=W: E.tensor_scalar(W[:, 4:5], W[:, 0:1], 1.0 / D,
                W[:, 3:4], ALU.mult, ALU.subtract), rd=[W[:, 0:1], W[:, 3:4]], wr=[W[:, 4:5]])
        st("scalar", lambda E, W=W: E.activation(W[:, 5:6], W[:, 4:5], AF.Sqrt,
                bias=s.constc[:, 0:1]), rd=[W[:, 4:5]], wr=[W[:, 5:6]])
        st("vector", lambda E, W=W: E.reciprocal(W[:, 6:7], W[:, 5:6]),
                rd=[W[:, 5:6]], wr=[W[:, 6:7]])
        st("vector", lambda E, p=p, W=W: E.tensor_scalar(xl2[p][:], y1b[p][:],
                W[:, 2:3], W[:, 6:7], ALU.add, ALU.mult),
                rd=[y1b[p][:], W[:, 2:3], W[:, 6:7]], wr=[xl2[p][:]])
        st("vector", lambda E, p=p, W=W: E.reduce_max(W[:, 7:8], xl2[p][:], AX.X,
                apply_absolute_value=True), rd=[xl2[p][:]], wr=[W[:, 7:8]])
        st("vector", lambda E, W=W: E.tensor_scalar(W[:, 8:9], W[:, 7:8], 1e-5,
                None, ALU.max), rd=[W[:, 7:8]], wr=[W[:, 8:9]])
        st("vector", lambda E, W=W: E.reciprocal(W[:, 9:10], W[:, 8:9]),
                rd=[W[:, 8:9]], wr=[W[:, 9:10]])
        st("vector", lambda E, W=W: E.tensor_scalar(W[:, 10:11], W[:, 9:10], 127.0,
                None, ALU.mult), rd=[W[:, 9:10]], wr=[W[:, 10:11]])
        st("vector", lambda E, W=W, tt=tt: E.tensor_scalar(s.dq_x[:, tt:tt + 1],
                W[:, 8:9], 1.0 / 127, None, ALU.mult), rd=[W[:, 8:9]],
                wr=[s.dq_x[:, tt:tt + 1]])
        st("scalar", lambda E, p=p, W=W: E.activation(sqt[p][:], xl2[p][:], AF.Identity,
                scale=W[:, 10:11], bias=s.constc[:, 2:3]),
                rd=[xl2[p][:], W[:, 10:11]], wr=[sqt[p][:]])
        st("vector", lambda E, p=p: E.tensor_copy(h16[p][:], sqt[p][:]),
                rd=[sqt[p][:]], wr=[h16[p][:]])
        st("vector", lambda E, p=p: E.tensor_scalar(xl2[p][:], h16[p][:], -1.0, 1536.0,
                ALU.mult, ALU.add), rd=[h16[p][:]], wr=[xl2[p][:]])
        st("vector", lambda E, p=p: E.tensor_scalar(xq16[p][:], xl2[p][:], -127.0, 128.0,
                ALU.max, ALU.min), rd=[xl2[p][:]], wr=[xq16[p][:]])
        for c in range(NCHUNK):
            b = c % 2
            st("tensor", lambda E, c=c, p=p, b=b: E.transpose(ptq[b][:],
                    xq16[p][:, ts(c, 128)], s.identb[:]), rd=[xq16[p][:, ts(c, 128)]],
                    wr=[ptq[b][:]])
            st("vector" if c % 2 == 0 else "scalar",
                    lambda E, c=c, q=q, b=b: (E.tensor_copy if c % 2 == 0 else (lambda o_, i_: E.activation(o_, i_, AF.Identity)))(
                        xqstg[:, c * 512 + q * 128: c * 512 + (q + 1) * 128], ptq[b][:]),
                    rd=[ptq[b][:]], wr=[xqstg[:, c * 512 + q * 128: c * 512 + (q + 1) * 128]],
                    glue=True)
        if q == 3:
            st("gpsimd", lambda E, tt=tt: E.dma_start(out=s.XQT[tt // 4], in_=xqstg[:]),
                    rd=[xqstg[:]], dma=True)
        return steps

    for tt0 in range(0, NT, 2):
        if tt0 % 4 == 0:
            stq = tt0 // 4
            au.step("sync", lambda E, stq=stq: E.dma_start(out=o2sb[:], in_=s.O2T[stq]),
                    wr=[o2sb[:]], dma=True)
        rr_merge([tt_steps(tt0), tt_steps(tt0 + 1)])
    au.emit()
    _ctx.close()


MK.phase_d2 = _phase_d2


def _phase_e(self):
    nc, s = self.nc, self
    NT = self.NT
    au = Auto(self, "phE")
    _ctx = ExitStack()
    st_ = _ctx.enter_context
    wgt_sb = st_(nc.sbuf_tensor("e_wg", [128, 8 * 4096], BF16))
    wup_sb = st_(nc.sbuf_tensor("e_wu", [128, 8 * 4096], BF16))
    xqsb = [st_(nc.sbuf_tensor(f"e_xq{p}", [128, 4096], BF16)) for p in range(2)]
    gsil = st_(nc.sbuf_tensor("e_gsil", [128, 1024], F32))
    gbuf = st_(nc.sbuf_tensor("e_gbuf", [128, 4096], F32))
    h16 = st_(nc.sbuf_tensor("e_h16", [128, 4096], FP16))
    wk = st_(nc.sbuf_tensor("e_wk", [128, 16], F32))
    gstg = st_(nc.sbuf_tensor("e_gstg", [128, 4096], BF16))
    gb16 = st_(nc.sbuf_tensor("e_gb16", [128, 4096], BF16))
    pga = st_(nc.psum_tensor("e_pga", [128, 1024], F32))
    pup = st_(nc.psum_tensor("e_pup", [128, 1024], F32))
    ptq = [st_(nc.psum_tensor(f"e_ptq{p}", [128, 128], BF16)) for p in range(2)]

    for c in range(NCHUNK):
        au.step("sync", lambda E, c=c: E.dma_start(out=wgt_sb[:, ts(c, 4096)],
                in_=s.WGT[ts(c, 128), :]), wr=[wgt_sb[:, ts(c, 4096)]], dma=True)
        au.step("sync", lambda E, c=c: E.dma_start(out=wup_sb[:, ts(c, 4096)],
                in_=s.WUP[ts(c, 128), :]), wr=[wup_sb[:, ts(c, 4096)]], dma=True)

    for tt in range(NT):
        p = tt % 2
        q = tt % 4
        st = tt // 4
        p2 = st % 2
        if q == 0:
            au.step("sync", lambda E, st=st, p2=p2: E.dma_start(
                out=xqsb[p2][:], in_=s.XQT[st]), wr=[xqsb[p2][:]], dma=True)
        au.step("vector", lambda E, tt=tt: E.tensor_scalar(wk[:, 4:5], s.dq_x[:, tt:tt + 1],
                s.wmean[:, 0:1], None, ALU.mult),
                rd=[s.dq_x[:, tt:tt + 1], s.wmean[:, 0:1]], wr=[wk[:, 4:5]])
        au.step("vector", lambda E, tt=tt: E.tensor_scalar(wk[:, 5:6], s.dq_x[:, tt:tt + 1],
                s.wmean[:, 1:2], None, ALU.mult),
                rd=[s.dq_x[:, tt:tt + 1], s.wmean[:, 1:2]], wr=[wk[:, 5:6]])
        for r in range(4):
            def mmr(E, r=r, q=q, p2=p2, which=0):
                pass
            def mk_mm(which, wsb, pdst):
                def f(E, r=r, q=q, p2=p2, wsb=wsb, pdst=pdst):
                    for c in range(NCHUNK):
                        for n in range(2):
                            ins = E.matmul(pdst[:, ts(n, 512)],
                                           xqsb[p2][:, c * 512 + q * 128: c * 512 + (q + 1) * 128],
                                           wsb[:, c * 4096 + r * 1024 + n * 512:
                                               c * 4096 + r * 1024 + (n + 1) * 512],
                                           start=(c == 0), stop=(c == 7))
                    return ins
                return f
            au.step("tensor", mk_mm(0, wgt_sb, pga),
                    rd=[xqsb[p2][:], (wgt_sb.ap().tensor.name, 0, 1 << 40)], wr=[pga[:]])
            au.step("tensor", mk_mm(1, wup_sb, pup),
                    rd=[xqsb[p2][:], (wup_sb.ap().tensor.name, 0, 1 << 40)], wr=[pup[:]])
            au.step("scalar", lambda E: E.activation(gsil[:], pga[:], AF.Silu,
                    scale=wk[:, 4:5]), rd=[pga[:], wk[:, 4:5]], wr=[gsil[:]])
            au.step("vector", lambda E, r=r: E.scalar_tensor_tensor(
                gbuf[:, ts(r, 1024)], pup[:], wk[:, 5:6], gsil[:],
                ALU.mult, ALU.mult), rd=[pup[:], wk[:, 5:6], gsil[:]],
                wr=[gbuf[:, ts(r, 1024)]])
        au.step("vector", lambda E: E.reduce_max(wk[:, 0:1], gbuf[:], AX.X,
                apply_absolute_value=True), rd=[gbuf[:]], wr=[wk[:, 0:1]])
        au.step("vector", lambda E: E.tensor_scalar(wk[:, 1:2], wk[:, 0:1], 1e-5,
                None, ALU.max), rd=[wk[:, 0:1]], wr=[wk[:, 1:2]])
        au.step("vector", lambda E: E.reciprocal(wk[:, 2:3], wk[:, 1:2]),
                rd=[wk[:, 1:2]], wr=[wk[:, 2:3]])
        au.step("vector", lambda E: E.tensor_scalar(wk[:, 3:4], wk[:, 2:3], 127.0,
                None, ALU.mult), rd=[wk[:, 2:3]], wr=[wk[:, 3:4]])
        au.step("vector", lambda E, tt=tt: E.tensor_scalar(s.dq_g[:, tt:tt + 1], wk[:, 1:2],
                s.wmean[:, 2:3], 1.0 / 127, ALU.mult, ALU.mult),
                rd=[wk[:, 1:2], s.wmean[:, 2:3]], wr=[s.dq_g[:, tt:tt + 1]])
        au.step("scalar", lambda E: E.activation(gbuf[:], gbuf[:], AF.Identity,
                scale=wk[:, 3:4], bias=s.constc[:, 2:3]), rd=[gbuf[:], wk[:, 3:4]],
                wr=[gbuf[:]])
        au.step("vector", lambda E: E.tensor_copy(h16[:], gbuf[:]), rd=[gbuf[:]], wr=[h16[:]])
        au.step("vector", lambda E: E.tensor_scalar(gbuf[:], h16[:], -1.0, 1536.0,
                ALU.mult, ALU.add), rd=[h16[:]], wr=[gbuf[:]])
        au.step("vector", lambda E: E.tensor_scalar(gb16[:], gbuf[:], -127.0, 128.0,
                ALU.max, ALU.min), rd=[gbuf[:]], wr=[gb16[:]])
        for c in range(32):
            b = c % 2
            au.step("tensor", lambda E, c=c, b=b: E.transpose(ptq[b][:],
                    gb16[:, ts(c, 128)], s.identb[:]), rd=[gb16[:, ts(c, 128)]], wr=[ptq[b][:]])
            au.step("vector" if c % 2 == 0 else "scalar",
                    lambda E, c=c, b=b: (E.tensor_copy if c % 2 == 0 else (lambda o_, i_: E.activation(o_, i_, AF.Identity)))(
                        gstg[:, ts(c, 128)], ptq[b][:]),
                    rd=[ptq[b][:]], wr=[gstg[:, ts(c, 128)]])
        au.step("gpsimd", lambda E, tt=tt: E.dma_start(out=s.GQT[tt], in_=gstg[:]),
                rd=[gstg[:]], dma=True)
    au.emit()
    _ctx.close()


MK.phase_e = _phase_e


def _phase_f(self):
    nc, s = self.nc, self
    NT = self.NT
    au = Auto(self, "phF")
    _ctx = ExitStack()
    st_ = _ctx.enter_context
    wdn_sb = st_(nc.sbuf_tensor("f_wd", [128, 32 * 1024], BF16))
    gqb = [st_(nc.sbuf_tensor(f"f_gq{p}", [128, 4096], BF16)) for p in range(2)]
    y1c = [st_(nc.sbuf_tensor(f"f_y1{p}", [128, D], F32)) for p in range(2)]
    yb = [st_(nc.sbuf_tensor(f"f_yb{p}", [128, D], F32)) for p in range(2)]
    dqc = st_(nc.sbuf_tensor("f_dqc", [128, 1], F32))
    py = [st_(nc.psum_tensor(f"f_py{p}", [128, 512], F32)) for p in range(2)]

    for c in range(32):
        au.step("sync", lambda E, c=c: E.dma_start(out=wdn_sb[:, ts(c, 1024)],
                in_=s.WDN[ts(c, 128), :]), wr=[wdn_sb[:, ts(c, 1024)]], dma=True)
    for tt in range(NT):
        p = tt % 2
        au.step("sync", lambda E, tt=tt, p=p: E.dma_start(out=gqb[p][:], in_=s.GQT[tt]),
                wr=[gqb[p][:]], dma=True)
        au.step("sync", lambda E, tt=tt, p=p: E.dma_start(out=y1c[p][:],
                in_=s.Y1[ts(tt, 128), :]), wr=[y1c[p][:]], dma=True)
        for nch in range(2):
            def mmd(E, nch=nch, p=p):
                for c in range(32):
                    ins = E.matmul(py[nch][:], gqb[p][:, ts(c, 128)],
                                   wdn_sb[:, c * 1024 + nch * 512: c * 1024 + (nch + 1) * 512],
                                   start=(c == 0), stop=(c == 31))
                return ins
            au.step("tensor", mmd, rd=[gqb[p][:], (wdn_sb.ap().tensor.name, 0, 1 << 40)],
                    wr=[py[nch][:]])
            au.step("vector", lambda E, nch=nch, p=p, tt=tt: E.scalar_tensor_tensor(
                yb[p][:, ts(nch, 512)], py[nch][:], s.dq_g[:, tt:tt + 1],
                y1c[p][:, ts(nch, 512)], ALU.mult, ALU.add),
                rd=[py[nch][:], s.dq_g[:, tt:tt + 1], y1c[p][:, ts(nch, 512)]],
                wr=[yb[p][:, ts(nch, 512)]])
        au.step("gpsimd", lambda E, tt=tt, p=p: E.dma_start(out=s.y_out[ts(tt, 128), :],
                in_=yb[p][:]), rd=[yb[p][:]], wr=[("YOUT", tt, tt + 1)], dma=True)
    au.step("gpsimd", lambda E: E.memset(dqc[:], 0.0), rd=[("YOUT", 0, 1 << 40)])
    au.emit()
    _ctx.close()


MK.phase_f = _phase_f


def build_full(nseq=2, L=4096, d1_coarse=False):
    m = MK(nseq=nseq, L=L)
    m.d1_coarse = d1_coarse
    m.phase_prep()
    m.barrier()
    m.phase_w()
    m.barrier()
    m.phase_a()
    m.barrier()
    m.phase_b()
    m.barrier()
    m.phase_c()
    m.barrier()
    m.phase_d1()
    m.barrier()
    m.phase_d2()
    m.barrier()
    m.phase_e()
    m.barrier()
    m.phase_f()
    m.stack.close()
    return m


def _phase_a(self):
    nc, s = self.nc, self
    NT = self.NT
    au = Auto(self, "phA")
    _ctx = ExitStack()
    st_ = _ctx.enter_context
    xbuf = [st_(nc.sbuf_tensor(f"a_x{p}", [128, D], F32)) for p in range(2)]
    xlnb = [st_(nc.sbuf_tensor(f"a_ln{p}", [128, D], F32)) for p in range(2)]
    stg = [st_(nc.sbuf_tensor(f"a_st{p}", [128, D], F32)) for p in range(2)]
    spar = [st_(nc.sbuf_tensor(f"a_sp{p}", [128, 8], F32)) for p in range(2)]
    strash = [st_(nc.sbuf_tensor(f"a_trash{p}", [128, D], F32)) for p in range(2)]
    ptr = [st_(nc.psum_tensor(f"a_ptr{p}", [128, 512], F32)) for p in range(4)]
    pbeta = st_(nc.psum_tensor("a_pbeta", [128, 8], F32))

    def mk_st():
        groups = []
        def _st(eng, fn, rd=(), wr=(), dma=False, glue=False):
            if glue and groups:
                groups[-1].append((eng, fn, rd, wr, dma))
            else:
                groups.append([(eng, fn, rd, wr, dma)])
        return groups, _st

    def rr_merge(gens):
        pos = [0] * len(gens)
        rem = sum(len(g_) for g_ in gens)
        while rem:
            for k in range(len(gens)):
                if pos[k] < len(gens[k]):
                    for (eng, fn, rd, wr, dma) in gens[k][pos[k]]:
                        au.step(eng, fn, rd=rd, wr=wr, dma=dma)
                    pos[k] += 1
                    rem -= 1

    def tt_steps(i):
        p = i % 2
        W = spar[p]
        steps, st = mk_st()
        st("sync", lambda E, i=i, p=p: E.dma_start(out=xbuf[p][:],
                in_=s.x_in[ts(i, 128), :]), wr=[xbuf[p][:]], dma=True)
        st("scalar", lambda E, p=p, W=W: E.activation(strash[p][:], xbuf[p][:], AF.Square,
                accum_out=W[:, 0:1]), rd=[xbuf[p][:]], wr=[strash[p][:], W[:, 0:1]])
        st("vector", lambda E, p=p, W=W: E.reduce_sum(W[:, 1:2], xbuf[p][:], AX.X,
                negate=True), rd=[xbuf[p][:]], wr=[W[:, 1:2]])
        st("vector", lambda E, W=W: E.tensor_scalar(W[:, 2:3], W[:, 1:2], 1.0 / D,
                None, ALU.mult), rd=[W[:, 1:2]], wr=[W[:, 2:3]])
        st("vector", lambda E, W=W: E.tensor_mul(W[:, 3:4], W[:, 2:3], W[:, 2:3]),
                rd=[W[:, 2:3]], wr=[W[:, 3:4]])
        st("vector", lambda E, W=W: E.tensor_scalar(W[:, 4:5], W[:, 0:1], 1.0 / D,
                W[:, 3:4], ALU.mult, ALU.subtract), rd=[W[:, 0:1], W[:, 3:4]], wr=[W[:, 4:5]])
        st("scalar", lambda E, W=W: E.activation(W[:, 5:6], W[:, 4:5], AF.Sqrt,
                bias=s.constc[:, 0:1]), rd=[W[:, 4:5]], wr=[W[:, 5:6]])
        st("vector", lambda E, W=W: E.reciprocal(W[:, 6:7], W[:, 5:6]),
                rd=[W[:, 5:6]], wr=[W[:, 6:7]])
        st("vector", lambda E, p=p, W=W: E.tensor_scalar(xlnb[p][:], xbuf[p][:],
                W[:, 2:3], W[:, 6:7], ALU.add, ALU.mult),
                rd=[xbuf[p][:], W[:, 2:3], W[:, 6:7]], wr=[xlnb[p][:]])
        bp = p * 2
        def tr4(E, p=p, bp=bp, half=0):
            for c in range(4):
                ins = E.transpose(ptr[bp + half][:, ts(c, 128)],
                                  xlnb[p][:, ts(half * 4 + c, 128)], s.ident[:])
            return ins
        st("tensor", lambda E, p=p, bp=bp: tr4(E, p, bp, 0),
                rd=[xlnb[p][:]], wr=[ptr[bp][:]])
        st("tensor", lambda E, p=p, bp=bp: tr4(E, p, bp, 1),
                rd=[xlnb[p][:]], wr=[ptr[bp + 1][:]])
        st("vector", lambda E, p=p, bp=bp: E.tensor_copy(stg[p][:, 0:512], ptr[bp][:]),
                rd=[ptr[bp][:]], wr=[stg[p][:, 0:512]])
        st("scalar", lambda E, p=p, bp=bp: E.activation(stg[p][:, 512:1024],
                ptr[bp + 1][:], AF.Identity), rd=[ptr[bp + 1][:]], wr=[stg[p][:, 512:1024]])
        def betamm(E, p=p):
            for c in range(NCHUNK):
                E.matmul(pbeta[:], stg[p][:, ts(c, 128)], s.wbres[:, ts(c, 8)],
                         start=(c == 0), stop=False)
            return E.matmul(pbeta[:], s.ones_col[0:1, :], s.bres_row[0:1, :],
                            start=False, stop=True)
        st("tensor", betamm, rd=[stg[p][:], s.wbres[:], s.ones_col[:], s.bres_row[:]],
                wr=[pbeta[:]])
        st("scalar", lambda E, i=i: E.activation(s.beta_all[:, i * 8:(i + 1) * 8],
                pbeta[:], AF.Sigmoid), rd=[pbeta[:]],
                wr=[s.beta_all[:, i * 8:(i + 1) * 8]], glue=True)
        for c in range(NCHUNK):
            st("gpsimd", lambda E, c=c, i=i, p=p: E.dma_start(
                out=s.XLNT[i // 4][:, c * 512 + (i % 4) * 128: c * 512 + (i % 4 + 1) * 128],
                in_=stg[p][:, ts(c, 128)]), rd=[stg[p][:, ts(c, 128)]], dma=True)
        return steps

    for i0_ in range(0, NT, 2):
        rr_merge([tt_steps(i0_), tt_steps(i0_ + 1)])
    au.emit()
    _ctx.close()


MK.phase_a = _phase_a


def _pass_proj(self, wname, w_ap, conv_off, K, dst_tm, with_fir,
               fir64_own=(), fir64_jobs=()):
    nc, s = self.nc, self
    NSTR_T = self.nseq * self.NSTR
    SEQ_STRIPES = self.NSTR
    au = Auto(self, f"phB{wname}")
    _ctx = ExitStack()
    st_ = _ctx.enter_context
    wbuf = st_(nc.sbuf_tensor(f"b_w_{wname}", [128, 8192], F32))
    xsb = [st_(nc.sbuf_tensor(f"b_x{p}_{wname}", [128, 4096], F32)) for p in range(2)]
    prebufC = st_(nc.sbuf_tensor(f"b_pre_{wname}", [128, 8 * 516], F32))
    acc8 = [st_(nc.sbuf_tensor(f"b_acc{c}_{wname}", [128, 512], F32)) for c in range(8)]
    tailb = st_(nc.sbuf_tensor(f"b_tail_{wname}", [128, 32], F32))
    stage = st_(nc.sbuf_tensor(f"b_stage_{wname}", [128, 4096], F32))
    ppj = [st_(nc.psum_tensor(f"b_ppj{p}_{wname}", [128, 512], F32)) for p in range(2)]
    ptrb = [st_(nc.psum_tensor(f"b_ptr{p}_{wname}", [128, 512], F32)) for p in range(2)]
    if with_fir:
        vact = st_(nc.sbuf_tensor(f"b_vact_{wname}", [128, 4608], F32))
        facc8 = [st_(nc.sbuf_tensor(f"b_fac{c}_{wname}", [128, 512], F32)) for c in range(8)]
        fstage = st_(nc.sbuf_tensor(f"b_fstage_{wname}", [128, 4096], F32))
        pfir = [st_(nc.psum_tensor(f"b_pf{p}_{wname}", [128, 512], F32)) for p in range(2)]
    else:
        acts = st_(nc.sbuf_tensor(f"b_acts_{wname}", [128, 4096], F32))
        if fir64_jobs:
            vbuf2 = [st_(nc.sbuf_tensor(f"b_vb{p}_{wname}", [128, 4608], F32)) for p in range(2)]
            facc8 = [st_(nc.sbuf_tensor(f"b_fac{c}_{wname}", [128, 512], F32)) for c in range(8)]
            fstage = st_(nc.sbuf_tensor(f"b_fstage_{wname}", [128, 4096], F32))
            pfir = [st_(nc.psum_tensor(f"b_pf{p}_{wname}", [128, 512], F32)) for p in range(2)]

    def emit_fir64(fs, vsrc):
        for j in range(64):
            for c in range(NCHUNK):
                wcol = s.firl[:, c * 64 + j: c * 64 + j + 1]
                base = c * 576 + 1
                rdr = [vsrc[:, base + j: base + j + 512], wcol]
                if j == 0:
                    au.step("vector", lambda E, c=c, wcol=wcol, base=base, vsrc=vsrc:
                            E.tensor_scalar(facc8[c][:], vsrc[:, base: base + 512],
                                            wcol, None, ALU.mult),
                            rd=rdr, wr=[facc8[c][:]])
                else:
                    au.step("vector", lambda E, c=c, wcol=wcol, base=base, j=j, vsrc=vsrc:
                            E.scalar_tensor_tensor(facc8[c][:],
                                                   vsrc[:, base + j: base + j + 512],
                                                   wcol, facc8[c][:], ALU.mult, ALU.add),
                            rd=rdr + [facc8[c][:]], wr=[facc8[c][:]])
        frow0 = (fs // SEQ_STRIPES) * self.L + (fs % SEQ_STRIPES) * 512
        emit_tr_block(lambda c, tt: facc8[c][:, ts(tt, 128)],
                      lambda c: facc8[c][:], fstage, pfir, s.LLTM, frow0)

    def emit_tr_block(srcf, srdf, stg, pbank, dst, row0):
        for c in range(NCHUNK):
            def trp(E, c=c, srcf=srcf):
                for tt in range(4):
                    ins = E.transpose(pbank[c % 2][:, ts(tt, 128)], srcf(c, tt), s.ident[:])
                return ins
            au.step("tensor", trp, rd=[srdf(c)], wr=[pbank[c % 2][:]])
            def evq(E, c=c, stg=stg):
                for tt in range(4):
                    ins = E.activation(stg[:, tt * 1024 + c * 128: tt * 1024 + (c + 1) * 128],
                                       pbank[c % 2][:, ts(tt, 128)], AF.Identity)
                return ins
            au.step("scalar", evq, rd=[pbank[c % 2][:]],
                    wr=[stg[:, tt * 1024 + c * 128: tt * 1024 + (c + 1) * 128]
                        for tt in range(4)])
        for tt in range(4):
            au.step("gpsimd", lambda E, tt=tt, row0=row0, dst=dst, stg=stg: E.dma_start(
                out=dst[row0 + tt * 128: row0 + (tt + 1) * 128, :],
                in_=stg[:, ts(tt, 1024)]), rd=[stg[:]], dma=True)

    jobs = list(fir64_jobs)

    for c in range(NCHUNK):
        au.step("sync", lambda E, c=c: E.dma_start(out=wbuf[:, ts(c, 1024)],
                in_=w_ap[ts(c, 128), :]), wr=[wbuf[:, ts(c, 1024)]], dma=True)

    for st in range(NSTR_T):
        seq_start = (st % SEQ_STRIPES) == 0
        au.step("sync", lambda E, st=st: E.dma_start(out=xsb[st % 2][:], in_=s.XLNT[st]),
                wr=[xsb[st % 2][:]], dma=True)
        if not with_fir and jobs:
            while jobs and len(jobs) > ((NSTR_T - 1 - st) * len(fir64_jobs)) // NSTR_T:
                fs = jobs.pop(0)
                jp = (len(fir64_jobs) - len(jobs)) % 2
                au.step("sync", lambda E, fs=fs, jp=jp: E.dma_start(out=vbuf2[jp][:],
                        in_=s.VCM[fs]), wr=[vbuf2[jp][:]], dma=True)
                emit_fir64(fs, vbuf2[jp])
        for c in range(NCHUNK):
            def mmp(E, c=c, st=st):
                for dch in range(NCHUNK):
                    ins = E.matmul(ppj[c % 2][:],
                                   wbuf[:, dch * 1024 + c * 128: dch * 1024 + (c + 1) * 128],
                                   xsb[st % 2][:, ts(dch, 512)],
                                   start=(dch == 0), stop=(dch == 7))
                return ins
            au.step("tensor", mmp, rd=[xsb[st % 2][:], (wbuf.ap().tensor.name, 0, 1 << 40)],
                    wr=[ppj[c % 2][:]])
            au.step("scalar", lambda E, c=c: E.activation(
                prebufC[:, c * 516 + 4: c * 516 + 516], ppj[c % 2][:], AF.Identity),
                rd=[ppj[c % 2][:]], wr=[prebufC[:, c * 516 + 4: c * 516 + 516]])
        for c in range(NCHUNK):
            au.step("scalar", lambda E, c=c, seq_start=seq_start: E.activation(
                prebufC[:, c * 516: c * 516 + 4],
                s.zero64[:, 0:4] if seq_start else tailb[:, ts(c, 4)], AF.Identity),
                rd=[] if seq_start else [tailb[:, ts(c, 4)]],
                wr=[prebufC[:, c * 516: c * 516 + 4]])
            au.step("scalar", lambda E, c=c: E.activation(tailb[:, ts(c, 4)],
                    prebufC[:, c * 516 + 512: c * 516 + 516], AF.Identity),
                    rd=[prebufC[:, c * 516 + 512: c * 516 + 516]], wr=[tailb[:, ts(c, 4)]])
        for j in range(K):
            for c in range(NCHUNK):
                wcol = s.convw[:, c * 12 + conv_off + j: c * 12 + conv_off + j + 1]
                lo = c * 516 + 1 + j
                if j == 0:
                    au.step("vector", lambda E, c=c, wcol=wcol, lo=lo: E.tensor_scalar(
                        acc8[c][:], prebufC[:, lo:lo + 512], wcol, None, ALU.mult),
                        rd=[prebufC[:, lo:lo + 512], wcol], wr=[acc8[c][:]])
                else:
                    au.step("vector", lambda E, c=c, wcol=wcol, lo=lo: E.scalar_tensor_tensor(
                        acc8[c][:], prebufC[:, lo:lo + 512], wcol, acc8[c][:],
                        ALU.mult, ALU.add), rd=[prebufC[:, lo:lo + 512], wcol, acc8[c][:]],
                        wr=[acc8[c][:]])
        for c in range(NCHUNK):
            if with_fir:
                au.step("scalar", lambda E, c=c, seq_start=seq_start: E.activation(
                    vact[:, c * 576: c * 576 + 64],
                    s.zero64[:, 0:64] if seq_start else vact[:, c * 576 + 512: c * 576 + 576],
                    AF.Identity),
                    rd=[] if seq_start else [vact[:, c * 576 + 512: c * 576 + 576]],
                    wr=[vact[:, c * 576: c * 576 + 64]])
                au.step("scalar", lambda E, c=c: E.activation(
                    vact[:, c * 576 + 64: (c + 1) * 576], acc8[c][:], AF.Silu),
                    rd=[acc8[c][:]], wr=[vact[:, c * 576 + 64: (c + 1) * 576]])
            else:
                au.step("scalar", lambda E, c=c: E.activation(acts[:, ts(c, 512)],
                        acc8[c][:], AF.Silu), rd=[acc8[c][:]], wr=[acts[:, ts(c, 512)]])
        seq = st // SEQ_STRIPES
        row0 = seq * self.L + (st % SEQ_STRIPES) * 512
        src_of = (lambda c, tt: vact[:, c * 576 + 64 + tt * 128: c * 576 + 64 + (tt + 1) * 128]) \
            if with_fir else (lambda c, tt: acts[:, c * 512 + tt * 128: c * 512 + (tt + 1) * 128])
        srd_of = (lambda c: vact[:, c * 576 + 64: (c + 1) * 576]) if with_fir \
            else (lambda c: acts[:, ts(c, 512)])
        emit_tr_block(src_of, srd_of, stage, ptrb, dst_tm, row0)
        if with_fir:
            au.step("gpsimd", lambda E, st=st: E.dma_start(out=s.VCM[st], in_=vact[:]),
                    rd=[vact[:]], dma=True)
            taps, wtile, wid, dst = 5, s.firs, 5, s.LSTM
            for j in range(taps):
                for c in range(NCHUNK):
                    wcol = wtile[:, c * wid + j: c * wid + j + 1]
                    base = c * 576 + 64 - (taps - 1)
                    rdr = [vact[:, base + j: base + j + 512], wcol]
                    if j == 0:
                        au.step("vector", lambda E, c=c, wcol=wcol, base=base: E.tensor_scalar(
                            facc8[c][:], vact[:, base: base + 512], wcol, None, ALU.mult),
                            rd=rdr, wr=[facc8[c][:]])
                    else:
                        au.step("vector", lambda E, c=c, wcol=wcol, base=base, j=j:
                                E.scalar_tensor_tensor(facc8[c][:],
                                                       vact[:, base + j: base + j + 512],
                                                       wcol, facc8[c][:], ALU.mult, ALU.add),
                                rd=rdr + [facc8[c][:]], wr=[facc8[c][:]])
            emit_tr_block(lambda c, tt: facc8[c][:, ts(tt, 128)],
                          lambda c: facc8[c][:], fstage, pfir, dst, row0)
            if st in fir64_own:
                emit_fir64(st, vact)
    au.emit()
    _ctx.close()


def _phase_b(self):
    n = self.nseq * self.NSTR
    qn = (n * 3) // 8
    kn = (n * 3) // 8
    stripes = list(range(n))
    qjobs = stripes[:qn]
    kjobs = stripes[qn:qn + kn]
    own = stripes[qn + kn:]
    self._pass_proj("v", self.Wv, 8, 4, self.VTM, True, fir64_own=own)
    self.barrier()
    self._pass_proj("q", self.Wq, 0, 4, self.QTM, False, fir64_jobs=qjobs)
    self.barrier()
    self._pass_proj("k", self.Wk, 4, 4, self.KTM, False, fir64_jobs=kjobs)
    self.barrier()


MK._pass_proj = _pass_proj
MK.phase_b = _phase_b




class _AuProxy:
    def __init__(self, st):
        self.step = st

def _phase_c(self):
    nc, s = self.nc, self
    NSTRM = self.nseq * 2
    NCHK = self.NTS
    au = Auto(self, "phC")
    _ctx = ExitStack()
    st_ = _ctx.enter_context
    ST = []
    for k in range(NSTRM):
        d = {}
        d["S"] = [st_(nc.sbuf_tensor(f"c_S{k}_{h}", [128, 256], F32)) for h in range(2)]
        d["qt"] = [st_(nc.sbuf_tensor(f"c_qt{k}{p}", [128, 256], F32)) for p in range(2)]
        d["kt"] = [st_(nc.sbuf_tensor(f"c_kt{k}{p}", [128, 256], F32)) for p in range(2)]
        d["vt"] = [st_(nc.sbuf_tensor(f"c_vt{k}{p}", [128, 256], F32)) for p in range(2)]
        for nm in ("qn", "kn", "vb", "kb", "qnf", "knf", "kbpf", "kbnf", "z", "ost"):
            d[nm] = st_(nc.sbuf_tensor(f"c_{nm}{k}", [128, 256], F32))
        for nm in ("M", "Xa", "Xb", "Y", "at"):
            d[nm] = st_(nc.sbuf_tensor(f"c_{nm}{k}", [128, 128], F32))
        d["sq"] = st_(nc.sbuf_tensor(f"c_sq{k}", [128, 264], F32))
        d["pa"] = st_(nc.psum_tensor(f"c_pa{k}", [128, 512], F32))
        d["pb"] = st_(nc.psum_tensor(f"c_pb{k}", [128, 512], F32))
        ST.append(d)

    NG = 2

    def rows(seq, i):
        r0 = seq * self.L + i * 128
        return slice(r0, r0 + 128)

    def Xj(d, j):
        if j == 0:
            return d["M"]
        return d["Xa"] if j % 2 == 1 else d["Xb"]

    def chunk_steps(g, i, k):
        steps = []
        def _st(eng, fn, rd=(), wr=(), dma=False):
            steps.append((eng, fn, rd, wr, dma))
        au = _AuProxy(_st)
        d = ST[k]
        seq, hh = k // 2, k % 2
        head = g * 2 + hh
        p = i % 2
        cs = slice(head * 256, (head + 1) * 256)
        tt = seq * self.NTS + i
        bcol = s.beta_all[:, tt * 8 + head: tt * 8 + head + 1]
        sq = d["sq"]
        pa, pb = d["pa"], d["pb"]
        au.step("sync", lambda E, d=d, p=p, seq=seq, i=i, cs=cs: E.dma_start(
            out=d["qt"][p][:], in_=s.QTM[rows(seq, i), cs]),
            wr=[d["qt"][p][:]], dma=True)
        au.step("sync", lambda E, d=d, p=p, seq=seq, i=i, cs=cs: E.dma_start(
            out=d["kt"][p][:], in_=s.KTM[rows(seq, i), cs]),
            wr=[d["kt"][p][:]], dma=True)
        au.step("sync", lambda E, d=d, p=p, seq=seq, i=i, cs=cs: E.dma_start(
            out=d["vt"][p][:], in_=s.VTM[rows(seq, i), cs]),
            wr=[d["vt"][p][:]], dma=True)
        au.step("scalar", lambda E, d=d, p=p, sq=sq: E.activation(sq[:, 0:256],
                d["qt"][p][:], AF.Square, accum_out=sq[:, 256:257]),
                rd=[d["qt"][p][:]], wr=[sq[:, 0:256], sq[:, 256:257]])
        au.step("scalar", lambda E, d=d, p=p, sq=sq: E.activation(sq[:, 0:256],
                d["kt"][p][:], AF.Square, accum_out=sq[:, 257:258]),
                rd=[d["kt"][p][:]], wr=[sq[:, 0:256], sq[:, 257:258]])
        au.step("scalar", lambda E, sq=sq: E.activation(sq[:, 258:259],
                sq[:, 256:257], AF.Sqrt, bias=s.constc[:, 1:2]),
                rd=[sq[:, 256:257]], wr=[sq[:, 258:259]])
        au.step("scalar", lambda E, sq=sq: E.activation(sq[:, 259:260],
                sq[:, 257:258], AF.Sqrt, bias=s.constc[:, 1:2]),
                rd=[sq[:, 257:258]], wr=[sq[:, 259:260]])
        au.step("vector", lambda E, sq=sq: E.reciprocal(sq[:, 260:261],
                sq[:, 258:259]), rd=[sq[:, 258:259]], wr=[sq[:, 260:261]])
        au.step("vector", lambda E, sq=sq: E.reciprocal(sq[:, 261:262],
                sq[:, 259:260]), rd=[sq[:, 259:260]], wr=[sq[:, 261:262]])
        au.step("vector", lambda E, d=d, p=p, sq=sq: E.tensor_scalar(d["qn"][:],
                d["qt"][p][:], sq[:, 260:261], None, ALU.mult),
                rd=[d["qt"][p][:], sq[:, 260:261]], wr=[d["qn"][:]])
        au.step("vector", lambda E, d=d, p=p, sq=sq: E.tensor_scalar(d["kn"][:],
                d["kt"][p][:], sq[:, 261:262], None, ALU.mult),
                rd=[d["kt"][p][:], sq[:, 261:262]], wr=[d["kn"][:]])
        au.step("vector", lambda E, d=d, bcol=bcol: E.tensor_scalar(d["kb"][:],
                d["kn"][:], bcol, None, ALU.mult),
                rd=[d["kn"][:], bcol], wr=[d["kb"][:]])
        au.step("vector", lambda E, d=d, p=p, bcol=bcol: E.tensor_scalar(d["vb"][:],
                d["vt"][p][:], bcol, None, ALU.mult),
                rd=[d["vt"][p][:], bcol], wr=[d["vb"][:]])
        def tr_qk(E, d=d):
            for cch in range(2):
                E.transpose(d["pa"][:, cch * 128:(cch + 1) * 128],
                            d["qn"][:, ts(cch, 128)], s.ident[:])
            for cch in range(2):
                ins = E.transpose(d["pa"][:, 256 + cch * 128: 384 + cch * 128],
                                  d["kn"][:, ts(cch, 128)], s.ident[:])
            return ins
        au.step("tensor", tr_qk, rd=[d["qn"][:], d["kn"][:]], wr=[pa[:]])
        def tr_kb(E, d=d):
            E.transpose(d["pb"][:, 0:128], d["kb"][:, 0:128], s.ident[:])
            return E.transpose(d["pb"][:, 128:256], d["kb"][:, 128:256], s.ident[:])
        au.step("tensor", tr_kb, rd=[d["kb"][:]], wr=[pb[:]])
        au.step("vector", lambda E, d=d: E.tensor_copy(d["qnf"][:], d["pa"][:, 0:256]),
                rd=[pa[:]], wr=[d["qnf"][:]])
        au.step("vector", lambda E, d=d: E.tensor_copy(d["knf"][:], d["pa"][:, 256:512]),
                rd=[pa[:]], wr=[d["knf"][:]])
        au.step("scalar", lambda E, d=d: E.activation(d["kbpf"][:], d["pb"][:, 0:256],
                AF.Identity), rd=[pb[:]], wr=[d["kbpf"][:]])
        au.step("scalar", lambda E, d=d: E.activation(d["kbnf"][:], d["pb"][:, 0:256],
                AF.Identity, scale=s.constc[:, 3:4]), rd=[pb[:]], wr=[d["kbnf"][:]])
        def mm_m(E, d=d):
            for cch in range(2):
                ins = E.matmul(d["pa"][:, 0:128], d["knf"][:, ts(cch, 128)],
                               d["kbpf"][:, ts(cch, 128)],
                               start=(cch == 0), stop=(cch == 1))
            return ins
        au.step("tensor", mm_m, rd=[d["knf"][:], d["kbpf"][:]], wr=[pa[:]])
        au.step("vector", lambda E, d=d: E.tensor_mul(d["M"][:], d["pa"][:, 0:128],
                s.maskMn[:]), rd=[pa[:], s.maskMn[:]], wr=[d["M"][:]])
        for kstep in range(8):
            if kstep == 0:
                if i == 0:
                    au.step("tensor", lambda E, d=d: E.matmul(d["pb"][:, 0:256],
                            s.ident[:], d["vb"][:], start=True, stop=True),
                            rd=[s.ident[:], d["vb"][:]], wr=[pb[:]])
                else:
                    def mm_z0(E, d=d):
                        E.matmul(d["pb"][:, 0:256], s.ident[:], d["vb"][:],
                                 start=True, stop=False)
                        E.matmul(d["pb"][:, 0:256], d["kbnf"][:, 0:128],
                                 d["S"][0][:], start=False, stop=False)
                        return E.matmul(d["pb"][:, 0:256], d["kbnf"][:, 128:256],
                                        d["S"][1][:], start=False, stop=True)
                    au.step("tensor", mm_z0,
                            rd=[s.ident[:], d["vb"][:], d["kbnf"][:],
                                d["S"][0][:], d["S"][1][:]], wr=[pb[:]])
                au.step("vector", lambda E, d=d: E.tensor_copy(d["z"][:],
                        d["pb"][:, 0:256]), rd=[pb[:]], wr=[d["z"][:]])
            else:
                X = Xj(d, kstep - 1)
                au.step("tensor", lambda E, d=d, X=X: E.matmul(d["pb"][:, 0:256],
                        X[:], d["z"][:], start=True, stop=True),
                        rd=[X[:], d["z"][:]], wr=[pb[:]])
                au.step("vector", lambda E, d=d: E.tensor_add(d["z"][:], d["z"][:],
                        d["pb"][:, 0:256]), rd=[pb[:], d["z"][:]], wr=[d["z"][:]])
            if kstep < 6:
                j2 = kstep + 1
                Xp, Xn = Xj(d, j2 - 1), Xj(d, j2)
                au.step("tensor", lambda E, d=d, Xp=Xp: E.transpose(
                        d["pa"][:, 128:256], Xp[:], s.ident[:]),
                        rd=[Xp[:]], wr=[pa[:]])
                au.step("vector", lambda E, d=d: E.tensor_copy(d["Y"][:],
                        d["pa"][:, 128:256]), rd=[pa[:]], wr=[d["Y"][:]])
                au.step("tensor", lambda E, d=d, Xp=Xp: E.matmul(d["pa"][:, 0:128],
                        d["Y"][:], Xp[:], start=True, stop=True),
                        rd=[d["Y"][:], Xp[:]], wr=[pa[:]])
                au.step("vector", lambda E, d=d, Xn=Xn: E.tensor_copy(Xn[:],
                        d["pa"][:, 0:128]), rd=[pa[:]], wr=[Xn[:]])
        def mm_a(E, d=d):
            for cch in range(2):
                ins = E.matmul(d["pa"][:, 0:128], d["knf"][:, ts(cch, 128)],
                               d["qnf"][:, ts(cch, 128)],
                               start=(cch == 0), stop=(cch == 1))
            return ins
        au.step("tensor", mm_a, rd=[d["knf"][:], d["qnf"][:]], wr=[pa[:]])
        au.step("vector", lambda E, d=d: E.tensor_mul(d["at"][:], d["pa"][:, 0:128],
                s.maskUI[:]), rd=[pa[:], s.maskUI[:]], wr=[d["at"][:]])
        if i == 0:
            au.step("tensor", lambda E, d=d: E.matmul(d["pb"][:, 256:512],
                    d["at"][:], d["z"][:], start=True, stop=True),
                    rd=[d["at"][:], d["z"][:]], wr=[pb[:]])
        else:
            def mm_o(E, d=d):
                E.matmul(d["pb"][:, 256:512], d["qnf"][:, 0:128], d["S"][0][:],
                         start=True, stop=False)
                E.matmul(d["pb"][:, 256:512], d["qnf"][:, 128:256], d["S"][1][:],
                         start=False, stop=False)
                return E.matmul(d["pb"][:, 256:512], d["at"][:], d["z"][:],
                                start=False, stop=True)
            au.step("tensor", mm_o, rd=[d["qnf"][:], d["S"][0][:], d["S"][1][:],
                    d["at"][:], d["z"][:]], wr=[pb[:]])
        au.step("scalar", lambda E, d=d: E.activation(d["ost"][:], d["pb"][:, 256:512],
                AF.Identity), rd=[pb[:]], wr=[d["ost"][:]])
        def mm_su(E, d=d):
            E.matmul(d["pa"][:, 0:256], d["kn"][:, 0:128], d["z"][:],
                     start=True, stop=True)
            return E.matmul(d["pa"][:, 256:512], d["kn"][:, 128:256], d["z"][:],
                            start=True, stop=True)
        au.step("tensor", mm_su, rd=[d["kn"][:], d["z"][:]], wr=[pa[:]])
        if i == 0:
            au.step("vector", lambda E, d=d: E.tensor_copy(d["S"][0][:],
                    d["pa"][:, 0:256]), rd=[pa[:]], wr=[d["S"][0][:]])
            au.step("vector", lambda E, d=d: E.tensor_copy(d["S"][1][:],
                    d["pa"][:, 256:512]), rd=[pa[:]], wr=[d["S"][1][:]])
        else:
            au.step("vector", lambda E, d=d: E.tensor_add(d["S"][0][:], d["S"][0][:],
                    d["pa"][:, 0:256]), rd=[pa[:], d["S"][0][:]], wr=[d["S"][0][:]])
            au.step("vector", lambda E, d=d: E.tensor_add(d["S"][1][:], d["S"][1][:],
                    d["pa"][:, 256:512]), rd=[pa[:], d["S"][1][:]], wr=[d["S"][1][:]])
        au.step("gpsimd", lambda E, d=d, seq=seq, i=i, head=head: E.dma_start(
                out=s.DOTM[rows(seq, i), head * 256:(head + 1) * 256],
                in_=d["ost"][:]), rd=[d["ost"][:]], dma=True)
        return steps

    for g in range(NG):
        for i in range(NCHK):
            gens = [chunk_steps(g, i, k) for k in range(NSTRM)]
            pos = [0] * NSTRM
            rem = sum(len(g_) for g_ in gens)
            while rem:
                for k in range(NSTRM):
                    if pos[k] < len(gens[k]):
                        eng, fn, rd, wr, dma = gens[k][pos[k]]
                        au.step(eng, fn, rd=rd, wr=wr, dma=dma)
                        pos[k] += 1
                        rem -= 1
    au.emit()
    _ctx.close()


MK.phase_c = _phase_c


def _phase_prep(self):
    nc, s = self.nc, self
    au = Auto(self, "prep")
    _ctx = ExitStack()
    st_ = _ctx.enter_context
    tmp = st_(nc.sbuf_tensor("p_tmp", [128, 8], F32))
    tmp1 = st_(nc.sbuf_tensor("p_tmp1", [1, 32], F32))
    pp = st_(nc.psum_tensor("p_pp", [128, 16], F32))

    def ld(fn, wr_ap):
        au.step("sync", fn, wr=[wr_ap], dma=True)
    for c in range(NCHUNK):
        ld(lambda E, c=c: E.dma_start(out=s.wbres[:, c * 8:c * 8 + 4], in_=s.Wb[ts(c, 128), :]), s.wbres[:, c * 8:c * 8 + 4])
        ld(lambda E, c=c: E.dma_start(out=s.wbres[:, c * 8 + 4:c * 8 + 8], in_=s.Wres[ts(c, 128), :]), s.wbres[:, c * 8 + 4:c * 8 + 8])
        ld(lambda E, c=c: E.dma_start(out=s.firl[:, ts(c, 64)], in_=s.fir_long[ts(c, 128), :]), s.firl[:, ts(c, 64)])
        ld(lambda E, c=c: E.dma_start(out=s.firs[:, ts(c, 5)], in_=s.fir_short[ts(c, 128), :]), s.firs[:, ts(c, 5)])
        ld(lambda E, c=c: E.dma_start(out=s.wg2[:, ts(c, 4)], in_=s.Wg2[ts(c, 128), :]), s.wg2[:, ts(c, 4)])
        ld(lambda E, c=c: E.dma_start(out=s.convw[:, c * 12:c * 12 + 4], in_=s.wq_conv[ts(c, 128), :]), s.convw[:, c * 12:c * 12 + 4])
        ld(lambda E, c=c: E.dma_start(out=s.convw[:, c * 12 + 4:c * 12 + 8], in_=s.wk_conv[ts(c, 128), :]), s.convw[:, c * 12 + 4:c * 12 + 8])
        ld(lambda E, c=c: E.dma_start(out=s.convw[:, c * 12 + 8:c * 12 + 12], in_=s.wv_conv[ts(c, 128), :]), s.convw[:, c * 12 + 8:c * 12 + 12])
        ld(lambda E, c=c: E.dma_start(out=s.bg1c[:, c:c + 1], in_=s.bg1[ts(c, 128)].unsqueeze(1)), s.bg1c[:, c:c + 1])
        off = (c * 128) % 256
        ld(lambda E, c=c, off=off: E.dma_start(out=s.nwch[:, c:c + 1], in_=s.norm_w[off:off + 128].unsqueeze(1)), s.nwch[:, c:c + 1])
    for c in range(4):
        ld(lambda E, c=c: E.dma_start(out=s.wog2[:, c:c + 1], in_=s.Wog2[ts(c, 128), :]), s.wog2[:, c:c + 1])
        ld(lambda E, c=c: E.dma_start(out=s.bog1c[:, c:c + 1], in_=s.bog1[ts(c, 128)].unsqueeze(1)), s.bog1c[:, c:c + 1])
    ld(lambda E: E.dma_start(out=s.bres_row[0:1, 4:8], in_=s.bres.unsqueeze(0)), s.bres_row[0:1, 4:8])
    ld(lambda E: E.dma_start(out=s.bg2f[0:4, :], in_=s.bg2.unsqueeze(1)), s.bg2f[:])
    ld(lambda E: E.dma_start(out=s.bog2v[0:1, :], in_=s.bog2.unsqueeze(0)), s.bog2v[:])
    ld(lambda E: E.dma_start(out=tmp1[0:1, 0:1], in_=s.logit_temp.unsqueeze(0)), tmp1[0:1, 0:1])
    ld(lambda E: E.dma_start(out=tmp1[0:1, 8:12], in_=s.conv_res_logit.unsqueeze(0)), tmp1[0:1, 8:12])

    G = lambda fn, rd_, wr_: au.step("gpsimd", fn, rd=rd_, wr=wr_)
    G(lambda E: E.memset(s.ident[:], 0.0), [], [s.ident[:]])
    G(lambda E: E.memset(s.maskMn[:], 0.0), [], [s.maskMn[:]])
    G(lambda E: E.memset(s.maskUI[:], 0.0), [], [s.maskUI[:]])
    G(lambda E: E.memset(s.ones_col[:], 1.0), [], [s.ones_col[:]])
    G(lambda E: E.memset(s.zero64[:], 0.0), [], [s.zero64[:]])
    G(lambda E: E.memset(s.bres_row[0:1, 0:4], 0.0), [], [s.bres_row[0:1, 0:4]])
    G(lambda E: E.memset(s.constc[:, 0:1], 1e-5), [], [s.constc[:, 0:1]])
    G(lambda E: E.memset(s.constc[:, 1:2], 1e-6), [], [s.constc[:, 1:2]])
    G(lambda E: E.memset(s.constc[:, 2:3], 1536.0), [], [s.constc[:, 2:3]])
    G(lambda E: E.memset(s.constc[:, 3:4], -1.0), [], [s.constc[:, 3:4]])
    G(lambda E: E.memset(s.constc[:, 4:5], 1.0), [], [s.constc[:, 4:5]])
    G(lambda E: E.memset(s.ones128[:], 1.0), [], [s.ones128[:]])
    G(lambda E: E.affine_select(out=s.ident[:], in_=s.ident[:], compare_op=ALU.not_equal,
                                fill=1.0, base=0, pattern=[[-1, 128]], channel_multiplier=1),
      [s.ident[:]], [s.ident[:]])
    G(lambda E: E.affine_select(out=s.maskMn[:], in_=s.maskMn[:], compare_op=ALU.is_ge,
                                fill=-1.0, base=0, pattern=[[-1, 128]], channel_multiplier=1),
      [s.maskMn[:]], [s.maskMn[:]])
    G(lambda E: E.affine_select(out=s.maskUI[:], in_=s.maskUI[:], compare_op=ALU.is_ge,
                                fill=1.0, base=-1, pattern=[[-1, 128]], channel_multiplier=1),
      [s.maskUI[:]], [s.maskUI[:]])
    au.step("vector", lambda E: E.tensor_copy(s.identb[:], s.ident[:]),
            rd=[s.ident[:]], wr=[s.identb[:]])

    au.step("scalar", lambda E: E.activation(tmp1[0:1, 1:2], tmp1[0:1, 0:1], AF.Exp),
            rd=[tmp1[0:1, 0:1]], wr=[tmp1[0:1, 1:2]])
    au.step("scalar", lambda E: E.activation(tmp1[0:1, 16:20], tmp1[0:1, 8:12], AF.Sigmoid),
            rd=[tmp1[0:1, 8:12]], wr=[tmp1[0:1, 16:20]])
    au.step("vector", lambda E: E.tensor_scalar(tmp1[0:1, 20:21], tmp1[0:1, 1:2], 1.0, None,
            ALU.add), rd=[tmp1[0:1, 1:2]], wr=[tmp1[0:1, 20:21]])
    au.step("vector", lambda E: E.tensor_scalar(tmp1[0:1, 2:3], tmp1[0:1, 0:1], 0.0, 0.35,
            ALU.max, ALU.add), rd=[tmp1[0:1, 0:1]], wr=[tmp1[0:1, 2:3]])
    for it in range(7):
        au.step("scalar", lambda E: E.activation(tmp1[0:1, 22:23], tmp1[0:1, 2:3], AF.Exp,
                scale=s.constc[0:1, 3:4]), rd=[tmp1[0:1, 2:3]], wr=[tmp1[0:1, 22:23]])
        au.step("vector", lambda E: E.tensor_scalar(tmp1[0:1, 23:24], tmp1[0:1, 22:23],
                tmp1[0:1, 20:21], -1.0, ALU.mult, ALU.add),
                rd=[tmp1[0:1, 22:23], tmp1[0:1, 20:21]], wr=[tmp1[0:1, 23:24]])
        au.step("vector", lambda E: E.tensor_add(tmp1[0:1, 2:3], tmp1[0:1, 2:3],
                tmp1[0:1, 23:24]), rd=[tmp1[0:1, 2:3], tmp1[0:1, 23:24]], wr=[tmp1[0:1, 2:3]])
    au.step("vector", lambda E: E.reciprocal(tmp1[0:1, 3:4], tmp1[0:1, 2:3]),
            rd=[tmp1[0:1, 2:3]], wr=[tmp1[0:1, 3:4]])
    au.step("tensor", lambda E: E.matmul(pp[:, 0:1], s.ones_col[0:1, :], tmp1[0:1, 3:4],
            start=True, stop=True), rd=[s.ones_col[:], tmp1[0:1, 3:4]], wr=[pp[:]])
    au.step("scalar", lambda E: E.activation(tmp[:, 0:1], pp[:, 0:1], AF.Identity),
            rd=[pp[:]], wr=[tmp[:, 0:1]])
    au.step("tensor", lambda E: E.matmul(pp[:, 4:8], s.ones_col[0:1, :], tmp1[0:1, 16:20],
            start=True, stop=True), rd=[s.ones_col[:], tmp1[0:1, 16:20]], wr=[pp[:]])
    au.step("scalar", lambda E: E.activation(s.statq[:, 0:4], pp[:, 4:8], AF.Identity),
            rd=[pp[:]], wr=[s.statq[:]])
    au.step("vector", lambda E: E.tensor_scalar(s.bg2f[0:4, :], s.bg2f[0:4, :], tmp[0:4, 0:1],
            None, ALU.mult), rd=[s.bg2f[:], tmp[:, 0:1]], wr=[s.bg2f[:]])
    for c in range(NCHUNK):
        au.step("vector", lambda E, c=c: E.tensor_scalar(s.wg2[:, ts(c, 4)], s.wg2[:, ts(c, 4)],
                tmp[:, 0:1], None, ALU.mult), rd=[s.wg2[:, ts(c, 4)], tmp[:, 0:1]],
                wr=[s.wg2[:, ts(c, 4)]])
    au.emit()
    _ctx.close()


MK.phase_prep = _phase_prep
from concourse.bass_utils import run_bass_kernel_spmd

_NC_CACHE = {}


def _get_m():
    if "m" not in _NC_CACHE:
        _NC_CACHE["m"] = build_full(nseq=2, L=4096)
    return _NC_CACHE["m"]


def kernel(**inputs):
    m = _get_m()
    B, L = 16, 4096
    ncore = 8
    x = np.ascontiguousarray(inputs["L0_attn_hidden_states"], dtype=np.float32)
    ren = {
        "Wq": "L0_attn_Wq", "Wk": "L0_attn_Wk", "Wv": "L0_attn_Wv", "Wb": "L0_attn_Wb",
        "wq_conv": "L0_attn_wq_conv", "wk_conv": "L0_attn_wk_conv", "wv_conv": "L0_attn_wv_conv",
        "Wg1": "L0_attn_Wg1", "bg1": "L0_attn_bg1", "Wg2": "L0_attn_Wg2", "bg2": "L0_attn_bg2",
        "logit_temp": "L0_attn_logit_temp", "conv_res_logit": "L0_attn_conv_res_logit",
        "Wres": "L0_attn_Wres", "bres": "L0_attn_bres",
        "Wog1": "L0_attn_Wog1", "bog1": "L0_attn_bog1",
        "Wog2": "L0_attn_Wog2", "bog2": "L0_attn_bog2",
        "norm_w": "L0_attn_norm_w", "Wo": "L0_attn_Wo",
        "w_gate": "L0_ffn_w_gate", "w_up": "L0_ffn_w_up", "w_down": "L0_ffn_w_down",
    }
    shared = {k: np.ascontiguousarray(inputs[v], dtype=np.float32) for k, v in ren.items()}
    shared["fir_long"] = np.ascontiguousarray(
        inputs["L0_attn_fir_long"], dtype=np.float32).reshape(1024, 64)
    shared["fir_short"] = np.ascontiguousarray(
        inputs["L0_attn_fir_short"], dtype=np.float32).reshape(1024, 5)
    in_maps = []
    for c in range(ncore):
        im = dict(shared)
        im["x"] = np.ascontiguousarray(x[2 * c: 2 * c + 2].reshape(2 * L, 1024))
        in_maps.append(im)
    res = run_bass_kernel_spmd(m.nc, in_maps, core_ids=list(range(ncore)))
    out = np.empty((B, L, 1024), np.float32)
    for c in range(ncore):
        out[2 * c: 2 * c + 2] = np.asarray(res.results[c]["y"]).reshape(2, L, 1024)
    return out
